# version 01
# baseline (speedup 1.0000x reference)
import numpy as np
import ml_dtypes
from contextlib import ExitStack

import concourse.bass as bass
import concourse.mybir as mybir
from concourse.bass_utils import run_bass_kernel_spmd

F32 = mybir.dt.float32
BF16 = mybir.dt.bfloat16
AF = mybir.ActivationFunctionType
ALU = mybir.AluOpType
AX = mybir.AxisListType

T = 4096
D = 1024
NH = 16
HD = 64
NF = 64
MAC = 512
NM = T // MAC
SUB = 4
NCORE = 8
EPS_LN = 1e-5
EPS_PF = 1e-4
PF_NRM = float(NF) ** -0.25
PF_S1 = float(np.sqrt(PF_NRM / 4.6))
PF_PMS = PF_NRM / PF_S1
ATT_S = float(8.0 ** -0.5)


def _bf(x):
    return np.ascontiguousarray(np.asarray(x)).astype(ml_dtypes.bfloat16)


def np_rope_tables():
    theta = 10000.0 ** (-2.0 * np.arange(HD // 2, dtype=np.float64) / HD)
    theta2 = np.concatenate([theta, theta])
    pos = np.arange(1, T + 1, dtype=np.float64)
    C = theta2[:, None] * pos[None, :]
    cos = np.cos(C) * ATT_S
    sin = np.sin(C) * ATT_S
    sin2 = np.concatenate([-sin[:HD // 2], sin[HD // 2:]], axis=0)
    cos2 = np.concatenate([cos, cos], axis=0).astype(np.float32)
    sin22 = np.concatenate([sin2, sin2], axis=0).astype(np.float32)
    return np.ascontiguousarray(cos2), np.ascontiguousarray(sin22)


def np_trimask():
    k = np.arange(128)[:, None]
    q = np.arange(128)[None, :]
    return _bf((q >= k).astype(np.float32))



class Sem:
    def __init__(self, stack, nc, name):
        self.h = stack.enter_context(nc.semaphore(name))
        self.n = 0
        self.name = name


class Prog:
    def __init__(self):
        self.ops = []

    def op(self, eng, fn, waits=(), inc=None, inc_by=1):
        ws = []
        for s in waits:
            if s.n > 0:
                ws.append((s, s.n))
        self.ops.append((eng, fn, tuple(ws), inc, inc_by))
        if inc is not None:
            inc.n += inc_by

    def dma(self, eng, fn, waits=(), inc=None):
        self.op(eng, fn, waits, inc, inc_by=16)

    def drain(self, eng):
        self.ops.append((eng, lambda e: e.drain(), (), None, 1))

    def replay(self, block):
        def make_body(name):
            def body(eng):
                last = {}
                for (e, fn, waits, inc, inc_by) in self.ops:
                    if e != name:
                        continue
                    for (s, c) in waits:
                        if c > last.get(s.name, -1):
                            eng.wait_ge(s.h, c)
                            last[s.name] = c
                    inst = fn(eng)
                    if inc is not None:
                        inst.then_inc(inc.h, inc_by)
            return body
        block.sync(make_body('sync'))
        block.scalar(make_body('scalar'))
        block.vector(make_body('vector'))
        block.tensor(make_body('tensor'))
        block.gpsimd(make_body('gpsimd'))


def mksems(stack, nc, names):
    return {n: Sem(stack, nc, n) for n in names}



def emit_preln_stats(p, xin, ln, S, xload_sem):
    for i in range(SUB):
        p.op('vector', lambda e, i=i: e.tensor_reduce(
            out=ln['scol'][:, i:i + 1], in_=xin[i][:], axis=AX.X, op=ALU.add),
            waits=[xload_sem])
        p.op('vector', lambda e, i=i: e.tensor_tensor(
            out=ln['scr'][:], in0=xin[i][:], in1=xin[i][:], op=ALU.mult))
        p.drain('vector')
        p.op('vector', lambda e, i=i: e.tensor_reduce(
            out=ln['s2col'][:, i:i + 1], in_=ln['scr'][:], axis=AX.X,
            op=ALU.add))
        p.drain('vector')
    p.op('vector', lambda e: e.tensor_scalar(
        out=ln['mcol'][:], in0=ln['scol'][:], scalar1=1.0 / D, scalar2=None,
        op0=ALU.mult))
    p.op('vector', lambda e: e.tensor_scalar(
        out=ln['e2col'][:], in0=ln['s2col'][:], scalar1=1.0 / D, scalar2=None,
        op0=ALU.mult))
    p.drain('vector')
    p.op('vector', lambda e: e.tensor_tensor(
        out=ln['mmcol'][:], in0=ln['mcol'][:], in1=ln['mcol'][:],
        op=ALU.mult))
    p.drain('vector')
    p.op('vector', lambda e: e.tensor_tensor(
        out=ln['varcol'][:], in0=ln['e2col'][:], in1=ln['mmcol'][:],
        op=ALU.subtract), inc=S['var'])
    p.op('scalar', lambda e: e.activation(
        out=ln['lncol'][:], in_=ln['varcol'][:], func=AF.Ln,
        bias=ln['epspp'][:, 0:1], scale=1.0), waits=[S['var']])
    p.drain('scalar')
    p.op('scalar', lambda e: e.activation(
        out=ln['rstdcol'][:], in_=ln['lncol'][:], func=AF.Exp, scale=-0.5),
        inc=S['rstd'])
    p.op('vector', lambda e: e.scalar_tensor_tensor(
        out=ln['nmrcol'][:], in0=ln['mcol'][:], scalar=-1.0,
        in1=ln['rstdcol'][:], op0=ALU.mult, op1=ALU.mult),
        waits=[S['rstd']], inc=S['nmr'])


def emit_xn_and_transpose(p, xin, xnb, xnf_m, ln, ppT, identb, S):
    ppb = ppT[:].bitcast(BF16)
    for i in range(SUB):
        p.op('scalar', lambda e, i=i: e.activation(
            out=xnb[i % 2][:], in_=xin[i][:], func=AF.Identity,
            scale=ln['rstdcol'][:, i:i + 1], bias=ln['nmrcol'][:, i:i + 1]),
            waits=[S['nmr'], S['tp']], inc=S['xn'])
        for d in range(8):
            p.op('tensor', lambda e, i=i, d=d: e.transpose(
                out=ppb[:, d * 128:(d + 1) * 128],
                in_=xnb[i % 2][:, d * 128:(d + 1) * 128], identity=identb[:]),
                waits=([S['xn'], S['tpe']] if d == 0 else ()),
                inc=(S['tp'] if d == 7 else None))
        for d in range(8):
            p.op('vector', lambda e, i=i, d=d: e.tensor_copy(
                out=xnf_m[d][:, i * 128:(i + 1) * 128],
                in_=ppb[:, d * 128:(d + 1) * 128]),
                waits=([S['tp']] if d == 0 else ()),
                inc=(S['tpe'] if d == 7 else None))


def emit_postln_tail(p, S, yf, row, rc, ppS1, ppS2, ppTr, yb, ysq,
                     ones128b, ident32, xres, scr2, yt,
                     x_src, out_dst, epspp):
    for o in range(8):
        p.op('vector', lambda e, o=o: e.tensor_copy(
            out=yb[o % 2][:], in_=yf[o][:]),
            waits=[S['ye'], S['smm']], inc=S['yb'])
        p.op('scalar', lambda e, o=o: e.activation(
            out=ysq[o % 2][:], in_=yb[o % 2][:], func=AF.Square),
            waits=[S['yb']], inc=S['ysq'])
        p.op('tensor', lambda e, o=o: e.matmul(
            out=ppS1[0:1, :], lhsT=ones128b[:], rhs=yb[o % 2][:],
            start=(o == 0), stop=(o == 7)),
            waits=([S['yb'], S['rowe']] if o == 0 else [S['yb']]))
        p.op('tensor', lambda e, o=o: e.matmul(
            out=ppS2[0:1, :], lhsT=ones128b[:], rhs=ysq[o % 2][:],
            start=(o == 0), stop=(o == 7)),
            waits=[S['ysq']],
            inc=(S['smv'] if o == 7 else S['smm']))
    p.op('vector', lambda e: e.tensor_scalar(
        out=row['m'][:], in0=ppS1[0:1, :], scalar1=1.0 / D, scalar2=None,
        op0=ALU.mult), waits=[S['smv']])
    p.op('vector', lambda e: e.tensor_scalar(
        out=row['e2'][:], in0=ppS2[0:1, :], scalar1=1.0 / D, scalar2=None,
        op0=ALU.mult), inc=S['rowe'])
    p.drain('vector')
    p.op('vector', lambda e: e.tensor_tensor(
        out=row['mm'][:], in0=row['m'][:], in1=row['m'][:], op=ALU.mult))
    p.drain('vector')
    p.op('vector', lambda e: e.tensor_tensor(
        out=row['var'][:], in0=row['e2'][:], in1=row['mm'][:],
        op=ALU.subtract), inc=S['rvar'])
    p.op('scalar', lambda e: e.activation(
        out=row['ln'][:], in_=row['var'][:], func=AF.Ln,
        bias=epspp[0:1, 0:1], scale=1.0), waits=[S['rvar']])
    p.drain('scalar')
    p.op('scalar', lambda e: e.activation(
        out=row['rstd'][:], in_=row['ln'][:], func=AF.Exp, scale=-0.5),
        inc=S['rrstd'])
    p.op('vector', lambda e: e.scalar_tensor_tensor(
        out=row['nmr'][:], in0=row['m'][:], scalar=-1.0, in1=row['rstd'][:],
        op0=ALU.mult, op1=ALU.mult), waits=[S['rrstd']], inc=S['rnmr'])
    for i in range(SUB):
        p.op('tensor', lambda e, i=i: e.transpose(
            out=ppTr[:, i:i + 1], in_=row['rstd'][0:1, i * 128:(i + 1) * 128],
            identity=ident32[0:1, 0:1]),
            waits=([S['rnmr'], S['yte']] if i == 0 else ()))
    for i in range(SUB):
        p.op('tensor', lambda e, i=i: e.transpose(
            out=ppTr[:, 4 + i:5 + i],
            in_=row['nmr'][0:1, i * 128:(i + 1) * 128],
            identity=ident32[0:1, 0:1]),
            inc=(S['rt'] if i == SUB - 1 else None))
    p.op('vector', lambda e: e.tensor_copy(out=rc[:], in_=ppTr[:, 0:8]),
         waits=[S['rt']], inc=S['rte'])
    for o in range(8):
        for i in range(SUB):
            p.op('tensor', lambda e, o=o, i=i: e.transpose(
                out=ppTr[:, i * 128:(i + 1) * 128],
                in_=yf[o][:, i * 128:(i + 1) * 128], identity=ident32[:]),
                waits=([S['rte'], S['yte']] if i == 0 else ()),
                inc=(S['yt'] if i == SUB - 1 else None))
        for i in range(SUB):
            p.op('vector', lambda e, o=o, i=i: e.tensor_copy(
                out=yt[i][:, o * 128:(o + 1) * 128],
                in_=ppTr[:, i * 128:(i + 1) * 128]),
                waits=([S['yt'], S['xo']] if i == 0 else ()),
                inc=(S['yte'] if i == SUB - 1 else None))
    for i in range(SUB):
        p.dma('sync', lambda e, i=i: e.dma_start(
            out=xres[i][:], in_=x_src(i)), waits=[S['ap'], S['xn']],
            inc=S['xr'])
        p.op('vector', lambda e, i=i: e.tensor_scalar(
            out=scr2[0][:], in0=yt[i][:], scalar1=rc[:, i:i + 1],
            scalar2=rc[:, 4 + i:5 + i], op0=ALU.mult, op1=ALU.add),
            waits=[S['yte'], S['rte'], S['xo']])
        p.drain('vector')
        p.op('vector', lambda e, i=i: e.tensor_add(
            out=yt[i][:], in0=scr2[0][:], in1=xres[i][:]),
            waits=[S['xr']], inc=S['ap'])
        p.drain('vector')
        p.dma('sync', lambda e, i=i: e.dma_start(
            out=out_dst(i), in_=yt[i][:]), waits=[S['ap']], inc=S['xo'])


def emit_weight_cast(p, src_ap, dst_ap, wbuf, S, rows, cols):
    n = rows // 128
    for c in range(n):
        p.dma('gpsimd', lambda e, c=c: e.dma_start(
            out=wbuf[c % 2][:, 0:cols], in_=src_ap[c * 128:(c + 1) * 128, :]),
            waits=[S['wst']], inc=S['wld'])
        p.dma('sync', lambda e, c=c: e.dma_start(
            out=dst_ap[c * 128:(c + 1) * 128, :], in_=wbuf[c % 2][:, 0:cols]),
            waits=[S['wld']], inc=S['wst'])



def build_l0attn():
    nc = bass.Bass("TRN2", target_bir_lowering=False)
    x = nc.dram_tensor("x", [T, D], F32, kind="ExternalInput").ap()
    wqkv = nc.dram_tensor("wqkv", [3 * D, D], F32, kind="ExternalInput").ap()
    bqkv = nc.dram_tensor("bqkv", [3 * D], F32, kind="ExternalInput").ap()
    wproj = nc.dram_tensor("wproj", [D, D], F32, kind="ExternalInput").ap()
    bproj = nc.dram_tensor("bproj", [D], F32, kind="ExternalInput").ap()
    pm = nc.dram_tensor("pm", [NF, HD], F32, kind="ExternalInput").ap()
    x1 = nc.dram_tensor("x1", [T, D], F32, kind="ExternalOutput").ap()

    wqkvbf = nc.dram_tensor("wqkvbf", [3 * D, D], BF16, kind="Internal").ap()
    wprojbf = nc.dram_tensor("wprojbf", [D, D], BF16, kind="Internal").ap()
    qps = nc.dram_tensor("qps", [D, T], BF16, kind="Internal").ap()

    identb_h = nc.inline_tensor(_bf(np.eye(128, dtype=np.float32)), "identb")
    ident32_h = nc.inline_tensor(np.eye(128, dtype=np.float32), "ident32")

    st = ExitStack()
    with st:
        sb = lambda name, shape, dt: st.enter_context(
            nc.sbuf_tensor(name, shape, dt))
        ps = lambda name, shape: st.enter_context(
            nc.psum_tensor(name, shape, F32))

        PP = [ps(f"pp{i}", [128, 512]) for i in range(8)]
        ppT = PP[0]
        ppQ = [PP[1], PP[2]]
        ppA = PP[3]
        ppB = PP[4]
        ppKV = [PP[5], PP[6], PP[7]]
        ppO = [PP[0], PP[1]]
        ppBC = PP[2]
        ppY = [PP[3], PP[4]]
        ppS1, ppS2 = PP[5], PP[6]
        ppTr = PP[7]

        identb = sb("identb_s", [128, 128], BF16)
        ident32 = sb("ident32_s", [128, 128], F32)
        wqkvT = [sb(f"wqkvT{d}", [128, 3 * D], BF16) for d in range(8)]
        wprojT = [sb(f"wprojT{v}", [128, D], BF16) for v in range(8)]
        xin = [sb(f"xin{i}", [128, D], F32) for i in range(SUB)]
        xnb = [sb(f"xnb{i}", [128, D], BF16) for i in range(2)]
        xnf = [[sb(f"xnf{m}_{d}", [128, MAC], BF16) for d in range(8)]
               for m in range(2)]
        vsb = [sb(f"vsb{i}", [128, NH * 65], BF16) for i in range(SUB)]
        ktmp = [sb(f"ktmp{i}", [128, D], BF16) for i in range(SUB)]
        kf2 = [sb(f"kf2{i}", [64, MAC], BF16) for i in range(2)]
        qaug = [sb(f"qaug{i}", [65, MAC], BF16) for i in range(2)]
        sqt = [sb(f"sqt{i}", [64, MAC], BF16) for i in range(2)]
        qpr = [sb(f"qpr{o}", [128, MAC], BF16) for o in range(8)]
        pmsT = sb("pmsT", [65, HD], BF16)
        pmTn = sb("pmTn", [NF, HD], BF16)
        pmstage = sb("pmstage", [NF, HD], BF16)
        kvsb = sb("kvsb", [128, NH * 65], BF16)
        bqk = sb("bqk", [128, 16], F32)
        bvb = sb("bvb", [128, D], F32)
        bpj = sb("bpj", [128, 8], F32)
        epspp = sb("epspp", [128, 1], F32)
        cq1 = sb("cq1", [1, 1], F32)
        mkneg = sb("mkneg", [128, 8], F32)
        ebias = sb("ebias", [128, 8], F32)
        ones64 = sb("ones64", [64, 1], BF16)
        ones128b = sb("ones128b", [128, 1], BF16)
        onesrF = sb("onesrF", [1, 64], F32)
        ln = dict(
            scol=sb("scol", [128, SUB], F32), s2col=sb("s2col", [128, SUB], F32),
            mcol=sb("mcol", [128, SUB], F32), e2col=sb("e2col", [128, SUB], F32),
            mmcol=sb("mmcol", [128, SUB], F32),
            varcol=sb("varcol", [128, SUB], F32),
            lncol=sb("lncol", [128, SUB], F32),
            rstdcol=sb("rstdcol", [128, SUB], F32),
            nmrcol=sb("nmrcol", [128, SUB], F32),
            scr=sb("lnscr", [128, D], F32), epspp=epspp)
        rr = [sb(f"rr{i}", [1, MAC], F32) for i in range(2)]
        rb = [sb(f"rb{i}", [64, MAC], F32) for i in range(2)]
        attf = [sb(f"attf{v}", [128, MAC], BF16) for v in range(8)]
        yf = [sb(f"yf{o}", [128, MAC], F32) for o in range(8)]
        yb = [sb(f"ybx{i}", [128, MAC], BF16) for i in range(2)]
        ysq = [sb(f"ysq{i}", [128, MAC], BF16) for i in range(2)]
        yt = [sb(f"yt{i}", [128, D], F32) for i in range(SUB)]
        rc = sb("rc", [128, 8], F32)
        row = dict(m=sb("rowm", [1, MAC], F32), e2=sb("rowe2", [1, MAC], F32),
                   mm=sb("rowmm", [1, MAC], F32),
                   var=sb("rowvar", [1, MAC], F32),
                   rstd=sb("rowrstd", [1, MAC], F32),
                   nmr=sb("rownmr", [1, MAC], F32))
        row['ln'] = row['mm']
        xres = xin
        scr2 = [ln["scr"]]
        wbuf = [ktmp[0], ktmp[1]]

        S = mksems(st, nc, [
            'wld', 'wst', 'wtp', 'cst',
            'xi', 'var', 'rstd', 'nmr', 'xn', 'tp', 'tpe',
            'qkv', 'qke', 'sq2', 'bnd', 'aug', 'qp', 'qpr', 'qps',
            'kp', 'kmax', 'keb', 'kex', 've', 'kv', 'kve',
            'ql', 'oa', 'rc', 'bc', 'rbe', 'at',
            'yp', 'ye', 'yb', 'ysq', 'smm', 'smv', 'rowe', 'rvar', 'rrstd',
            'rnmr', 'rt', 'rte', 'yt', 'yte', 'ap', 'xr', 'xo',
        ])

        p = Prog()

        p.dma('sync', lambda e: e.dma_start(out=identb[:], in_=identb_h.ap()[:, :]),
              inc=S['cst'])
        p.dma('sync', lambda e: e.dma_start(out=ident32[:], in_=ident32_h.ap()[:, :]),
              inc=S['cst'])
        p.op('vector', lambda e: e.memset(epspp[:], EPS_LN))
        p.op('vector', lambda e: e.memset(cq1[:], EPS_PF - 0.5 * PF_PMS * PF_PMS))
        p.op('vector', lambda e: e.memset(ones64[:], 1.0))
        p.op('vector', lambda e: e.memset(ones128b[:], 1.0))
        p.op('vector', lambda e: e.memset(onesrF[:], 1.0))
        for i in range(SUB):
            p.op('vector', lambda e, i=i: e.memset(
                vsb[i][:].rearrange("p (h c) -> p h c", c=65)[:, :, 64:65], 1.0))
        p.op('vector', lambda e: e.nop(), inc=S['cst'])

        for c in range(8):
            p.dma('sync', lambda e, c=c: e.dma_start(
                out=bqk[:, c:c + 1], in_=bqkv[c * 128:(c + 1) * 128][:, None]),
                inc=S['cst'])
        for c in range(8):
            p.dma('sync', lambda e, c=c: e.dma_start(
                out=bqk[:, 8 + c:9 + c],
                in_=bqkv[D + c * 128:D + (c + 1) * 128][:, None]),
                inc=S['cst'])
        for c in range(8):
            p.dma('sync', lambda e, c=c: e.dma_start(
                out=bpj[:, c:c + 1], in_=bproj[c * 128:(c + 1) * 128][:, None]),
                inc=S['cst'])
        p.dma('gpsimd', lambda e: e.dma_start(
            out=bvb[:], in_=bqkv[2 * D:3 * D][None, :].broadcast_to([128, D])),
            inc=S['cst'])

        p.dma('gpsimd', lambda e: e.dma_start(out=pmstage[:], in_=pm[:, :]),
              inc=S['wld'])
        p.op('tensor', lambda e: e.transpose(
            out=ppT[:].bitcast(BF16)[0:64, 0:64], in_=pmstage[:],
            identity=identb[0:64, 0:64]),
            waits=[S['wld'], S['cst']], inc=S['wtp'])
        p.op('vector', lambda e: e.tensor_scalar(
            out=pmsT[0:64, :], in0=ppT[:].bitcast(BF16)[0:64, 0:64],
            scalar1=PF_PMS, scalar2=None, op0=ALU.mult), waits=[S['wtp']])
        p.op('vector', lambda e: e.tensor_scalar(
            out=pmTn[:], in0=ppT[:].bitcast(BF16)[0:64, 0:64],
            scalar1=PF_NRM, scalar2=None, op0=ALU.mult), inc=S['qke'])
        p.op('vector', lambda e: e.memset(pmsT[64:65, :], 1.0), inc=S['wtp'])

        emit_weight_cast(p, wqkv, wqkvbf, wbuf, S, 3 * D, D)
        emit_weight_cast(p, wproj, wprojbf, wbuf, S, D, D)
        for d in range(8):
            p.dma('sync', lambda e, d=d: e.dma_start_transpose(
                out=wqkvT[d][:], in_=wqkvbf[0:3 * D, d * 128:(d + 1) * 128]),
                waits=[S['wst']], inc=S['wld'])
        for v in range(8):
            p.dma('sync', lambda e, v=v: e.dma_start_transpose(
                out=wprojT[v][:], in_=wprojbf[0:D, v * 128:(v + 1) * 128]),
                waits=[S['wst']], inc=S['wld'])
        WREADY = S['wld']

        for m in range(NM):
            mb = m % 2
            for i in range(SUB):
                p.dma('sync', lambda e, m=m, i=i: e.dma_start(
                    out=xin[i][:],
                    in_=x[(m * SUB + i) * 128:(m * SUB + i + 1) * 128, :]),
                    waits=[S['xn']], inc=S['xi'])
            emit_preln_stats(p, xin, ln, S, S['xi'])
            emit_xn_and_transpose(p, xin, xnb, xnf[mb], ln, ppT, identb, S)

            for oc in range(16):
                qb = ppQ[oc % 2]
                for d in range(8):
                    p.op('tensor', lambda e, oc=oc, d=d, qb=qb, mb=mb: e.matmul(
                        out=qb[:], lhsT=wqkvT[d][:, oc * 128:(oc + 1) * 128],
                        rhs=xnf[mb][d][:], start=(d == 0), stop=(d == 7)),
                        waits=([S['tpe'], S['qke'], WREADY] if d == 0 else ()),
                        inc=(S['qkv'] if d == 7 else None))
                if oc < 8:
                    for h2 in range(2):
                        b0 = h2 * 64
                        p.op('vector', lambda e, oc=oc, h2=h2, qb=qb, b0=b0:
                             e.tensor_scalar(
                                 out=qaug[h2][0:64, :],
                                 in0=qb[b0:b0 + 64, :],
                                 scalar1=bqk[b0:b0 + 64, oc:oc + 1],
                                 scalar2=PF_S1, op0=ALU.add, op1=ALU.mult),
                             waits=[S['qkv'], S['qp']], inc=S['qke'])
                        p.drain('vector')
                        p.op('vector', lambda e, h2=h2: e.tensor_tensor(
                            out=sqt[h2][:], in0=qaug[h2][0:64, :],
                            in1=qaug[h2][0:64, :], op=ALU.mult),
                            waits=[S['bnd']], inc=S['sq2'])
                        p.op('tensor', lambda e, h2=h2: e.matmul(
                            out=ppA[0:1, :], lhsT=ones64[:], rhs=sqt[h2][:],
                            start=True, stop=True),
                            waits=[S['sq2'], S['aug']], inc=S['bnd'])
                        p.op('scalar', lambda e, h2=h2: e.activation(
                            out=qaug[h2][64:65, :], in_=ppA[0:1, :],
                            func=AF.Identity, scale=-0.5, bias=cq1[0:1, 0:1]),
                            waits=[S['bnd'], S['qp']], inc=S['aug'])
                        p.op('tensor', lambda e, h2=h2: e.matmul(
                            out=ppB[0:64, :], lhsT=pmsT[:], rhs=qaug[h2][:],
                            start=True, stop=True),
                            waits=[S['aug'], S['qpr']], inc=S['qp'])
                        p.op('scalar', lambda e, oc=oc, h2=h2, b0=b0:
                             e.activation(
                                 out=qpr[oc][b0:b0 + 64, :], in_=ppB[0:64, :],
                                 func=AF.Exp),
                             waits=[S['qp']], inc=S['qpr'])
                    p.dma('scalar', lambda e, m=m, oc=oc: e.dma_start(
                        out=qps[oc * 128:(oc + 1) * 128,
                                m * MAC:(m + 1) * MAC], in_=qpr[oc][:]),
                        waits=[S['qpr']], inc=S['qps'])
                else:
                    ock = oc - 8
                    pack = ppA if ock % 2 == 0 else ppB
                    for h2 in range(2):
                        b0 = h2 * 64
                        p.op('vector', lambda e, oc=oc, h2=h2, qb=qb, b0=b0:
                             e.tensor_scalar(
                                 out=kf2[h2][:], in0=qb[b0:b0 + 64, :],
                                 scalar1=bqk[b0:b0 + 64, 8 + oc - 8:9 + oc - 8],
                                 scalar2=None, op0=ALU.add),
                             waits=[S['qkv'], S['kp']], inc=S['qke'])
                    for h2 in range(2):
                        for i2 in range(SUB):
                            j = h2 * 4 + i2
                            p.op('tensor', lambda e, h2=h2, i2=i2, j=j,
                                 pack=pack: e.matmul(
                                     out=pack[:, j * 64:(j + 1) * 64],
                                     lhsT=kf2[h2][:, i2 * 128:(i2 + 1) * 128],
                                     rhs=pmTn[:], start=True, stop=True),
                                 waits=([S['qke'], S['kex'], S['aug'],
                                         S['qpr']] if j == 0 else ()),
                                 inc=(S['kp'] if j == 7 else None))
                    for j in range(8):
                        p.op('vector', lambda e, j=j, pack=pack: e.tensor_reduce(
                            out=mkneg[:, j:j + 1], in_=pack[:, j * 64:(j + 1) * 64],
                            axis=AX.X, op=ALU.max, negate=True),
                            waits=([S['kp']] if j == 0 else ()))
                    p.drain('vector')
                    p.op('vector', lambda e: e.tensor_scalar(
                        out=ebias[:], in0=mkneg[:], scalar1=EPS_PF,
                        scalar2=None, op0=ALU.add),
                        waits=[S['kex']], inc=S['keb'])
                    for h2 in range(2):
                        for i2 in range(SUB):
                            j = h2 * 4 + i2
                            h = 2 * ock + h2
                            p.op('scalar', lambda e, j=j, h=h, i2=i2, pack=pack:
                                 e.activation(
                                     out=ktmp[i2][:, h * 64:(h + 1) * 64],
                                     in_=pack[:, j * 64:(j + 1) * 64],
                                     func=AF.Exp, bias=ebias[:, j:j + 1]),
                                 waits=[S['keb'], S['kv']],
                                 inc=(S['kex'] if j == 7 else None))

            for i2 in range(SUB):
                for vh in range(2):
                    qb = ppQ[vh % 2]
                    for d in range(8):
                        p.op('tensor', lambda e, i2=i2, vh=vh, d=d, qb=qb,
                             mb=mb: e.matmul(
                                 out=qb[:],
                                 lhsT=xnf[mb][d][:, i2 * 128:(i2 + 1) * 128],
                                 rhs=wqkvT[d][:, 2 * D + vh * 512:
                                              2 * D + (vh + 1) * 512],
                                 start=(d == 0), stop=(d == 7)),
                             waits=([S['qke'], S['ve']] if d == 0 else ()),
                             inc=(S['qkv'] if d == 7 else None))
                    p.op('vector', lambda e, i2=i2, vh=vh, qb=qb: e.tensor_tensor(
                        out=vsb[i2][:].rearrange("p (h c) -> p h c", c=65)
                            [:, vh * 8:(vh + 1) * 8, 0:64],
                        in0=qb[:].rearrange("p (h c) -> p h c", c=64),
                        in1=bvb[:, vh * 512:(vh + 1) * 512].rearrange(
                            "p (h c) -> p h c", c=64),
                        op=ALU.add),
                        waits=[S['qkv'], S['kv']], inc=S['ve'])

            for h in range(NH):
                bank, off = (0, h) if h < 7 else ((1, h - 7) if h < 14
                                                  else (2, h - 14))
                for i2 in range(SUB):
                    p.op('tensor', lambda e, h=h, i2=i2, bank=bank, off=off,
                         m=m: e.matmul(
                             out=ppKV[bank][0:64, off * 65:(off + 1) * 65],
                             lhsT=ktmp[i2][:, h * 64:(h + 1) * 64],
                             rhs=vsb[i2][:, h * 65:(h + 1) * 65],
                             start=(m == 0 and i2 == 0 and h in (0, 7, 14)),
                             stop=(m == NM - 1 and i2 == SUB - 1
                                   and h in (6, 13, 15))),
                         waits=([S['kex'], S['ve']] if (h == 0 and i2 == 0)
                                else ()),
                         inc=(S['kv'] if (h == NH - 1 and i2 == SUB - 1)
                              else None))

        for bank in range(3):
            lo = bank * 7 * 65
            w = 7 * 65 if bank < 2 else 2 * 65
            p.op('vector', lambda e, bank=bank, lo=lo, w=w: e.tensor_copy(
                out=kvsb[0:64, lo:lo + w], in_=ppKV[bank][0:64, 0:w]),
                waits=[S['kv']])
        p.drain('vector')
        p.op('vector', lambda e: e.tensor_copy(
            out=kvsb[64:128, :], in_=kvsb[0:64, :]))
        p.drain('vector')
        p.op('vector', lambda e: e.nop(), inc=S['kve'])

        for m in range(NM):
            for oc in range(8):
                p.dma('sync', lambda e, m=m, oc=oc: e.dma_start(
                    out=qpr[oc][:],
                    in_=qps[oc * 128:(oc + 1) * 128, m * MAC:(m + 1) * MAC]),
                    waits=[S['kve'], S['oa'], S['qps']], inc=S['ql'])
            for h in range(NH):
                b0 = (h % 2) * 64
                po = ppO[h % 2]
                p.op('tensor', lambda e, h=h, b0=b0, po=po: e.matmul(
                    out=po[0:65, :],
                    lhsT=kvsb[b0:b0 + 64, h * 65:(h + 1) * 65],
                    rhs=qpr[h // 2][b0:b0 + 64, :], start=True, stop=True),
                    waits=[S['ql'], S['kve'], S['at']], inc=S['oa'])
                p.op('vector', lambda e, h=h, po=po: e.tensor_scalar(
                    out=rr[0][:], in0=po[64:65, :], scalar1=1e-6,
                    scalar2=None, op0=ALU.add), waits=[S['oa']])
                p.drain('vector')
                p.op('vector', lambda e: e.reciprocal(
                    out=rr[1][:], in_=rr[0][:]), waits=[S['bc']],
                    inc=S['rc'])
                p.op('tensor', lambda e: e.matmul(
                    out=ppBC[0:64, :], lhsT=onesrF[:], rhs=rr[1][:],
                    start=True, stop=True),
                    waits=[S['rc'], S['rbe']], inc=S['bc'])
                p.op('scalar', lambda e, h=h: e.activation(
                    out=rb[h % 2][:], in_=ppBC[0:64, :], func=AF.Identity),
                    waits=[S['bc'], S['at']], inc=S['rbe'])
                p.op('vector', lambda e, h=h, b0=b0, po=po: e.tensor_tensor(
                    out=attf[h // 2][b0:b0 + 64, :], in0=po[0:64, :],
                    in1=rb[h % 2][:], op=ALU.mult),
                    waits=[S['rbe']], inc=S['at'])
            for o in range(8):
                py = ppY[o % 2]
                for v in range(8):
                    p.op('tensor', lambda e, o=o, v=v, py=py: e.matmul(
                        out=py[:], lhsT=wprojT[v][:, o * 128:(o + 1) * 128],
                        rhs=attf[v][:], start=(v == 0), stop=(v == 7)),
                        waits=([S['at'], S['ye']] if v == 0 else ()),
                        inc=(S['yp'] if v == 7 else None))
                p.op('vector', lambda e, o=o, py=py: e.tensor_scalar(
                    out=yf[o][:], in0=py[:], scalar1=bpj[:, o:o + 1],
                    scalar2=None, op0=ALU.add),
                    waits=[S['yp']], inc=S['ye'])
            emit_postln_tail(
                p, S, yf, row, rc, ppS1, ppS2, ppTr, yb, ysq, ones128b,
                ident32, xres, scr2, yt,
                x_src=lambda i, m=m: x[(m * SUB + i) * 128:
                                       (m * SUB + i + 1) * 128, :],
                out_dst=lambda i, m=m: x1[(m * SUB + i) * 128:
                                          (m * SUB + i + 1) * 128, :],
                epspp=epspp)

        p.op('sync', lambda e: e.nop(), waits=[S['xo']])

        with nc.Block() as block:
            p.replay(block)
    return nc



def build_l0ffn():
    nc = bass.Bass("TRN2", target_bir_lowering=False)
    x = nc.dram_tensor("x", [T, D], F32, kind="ExternalInput").ap()
    win = nc.dram_tensor("win", [D, 64], F32, kind="ExternalInput").ap()
    binv = nc.dram_tensor("binv", [64], F32, kind="ExternalInput").ap()
    w1 = nc.dram_tensor("w1", [64, 16], F32, kind="ExternalInput").ap()
    b1 = nc.dram_tensor("b1", [64, 16], F32, kind="ExternalInput").ap()
    w2 = nc.dram_tensor("w2", [64, 16], F32, kind="ExternalInput").ap()
    b2 = nc.dram_tensor("b2", [64], F32, kind="ExternalInput").ap()
    wout = nc.dram_tensor("wout", [64, D], F32, kind="ExternalInput").ap()
    bout = nc.dram_tensor("bout", [D], F32, kind="ExternalInput").ap()
    x2 = nc.dram_tensor("x2", [T, D], F32, kind="ExternalOutput").ap()

    identb_h = nc.inline_tensor(_bf(np.eye(128, dtype=np.float32)), "identb")
    ident32_h = nc.inline_tensor(np.eye(128, dtype=np.float32), "ident32")

    st = ExitStack()
    with st:
        sb = lambda name, shape, dt: st.enter_context(
            nc.sbuf_tensor(name, shape, dt))
        ps = lambda name, shape: st.enter_context(
            nc.psum_tensor(name, shape, F32))

        PP = [ps(f"pp{i}", [128, 512]) for i in range(8)]
        ppT = PP[0]
        ppS = PP[1]
        ppG = PP[2]
        ppY = [PP[3], PP[4]]
        ppS1, ppS2 = PP[5], PP[6]
        ppTr = PP[7]

        identb = sb("identb_s", [128, 128], BF16)
        ident32 = sb("ident32_s", [128, 128], F32)
        xin = [sb(f"xin{i}", [128, D], F32) for i in range(SUB)]
        xnb = [sb(f"xnb{i}", [128, D], BF16) for i in range(2)]
        xnf = [[sb(f"xnf{m}_{d}", [128, MAC], BF16) for d in range(8)]
               for m in range(2)]
        winb = sb("winb", [128, 512], BF16)
        woutb = sb("woutb", [64, D], BF16)
        binb4 = sb("binb4", [128, 256], F32)
        w1b = sb("w1b", [128, 1024], F32)
        b1b = sb("b1b", [128, 1024], F32)
        w2b = sb("w2b", [128, 1024], F32)
        b2b4 = sb("b2b4", [128, 256], F32)
        boc = sb("boc", [128, 8], F32)
        s4 = sb("s4", [128, 256], F32)
        mid1 = [sb(f"mid1{i}", [128, 1024], F32) for i in range(2)]
        mid2 = [sb(f"mid2{i}", [128, 1024], F32) for i in range(2)]
        gf32 = sb("gf32", [128, 256], F32)
        gbf = sb("gbf", [128, 256], BF16)
        gF = sb("gF", [64, MAC], BF16)
        epspp = sb("epspp", [128, 1], F32)
        ones128b = sb("ones128b", [128, 1], BF16)
        ln = dict(
            scol=sb("scol", [128, SUB], F32), s2col=sb("s2col", [128, SUB], F32),
            mcol=sb("mcol", [128, SUB], F32), e2col=sb("e2col", [128, SUB], F32),
            mmcol=sb("mmcol", [128, SUB], F32),
            varcol=sb("varcol", [128, SUB], F32),
            lncol=sb("lncol", [128, SUB], F32),
            rstdcol=sb("rstdcol", [128, SUB], F32),
            nmrcol=sb("nmrcol", [128, SUB], F32),
            scr=sb("lnscr", [128, D], F32), epspp=epspp)
        yf = [sb(f"yf{o}", [128, MAC], F32) for o in range(8)]
        yb = [sb(f"ybx{i}", [128, MAC], BF16) for i in range(2)]
        ysq = [sb(f"ysq{i}", [128, MAC], BF16) for i in range(2)]
        yt = [sb(f"yt{i}", [128, D], F32) for i in range(SUB)]
        rc = sb("rc", [128, 8], F32)
        row = dict(m=sb("rowm", [1, MAC], F32), e2=sb("rowe2", [1, MAC], F32),
                   mm=sb("rowmm", [1, MAC], F32),
                   var=sb("rowvar", [1, MAC], F32),
                   rstd=sb("rowrstd", [1, MAC], F32),
                   nmr=sb("rownmr", [1, MAC], F32))
        row['ln'] = row['mm']
        xres = xin
        scr2 = [ln["scr"]]

        S = mksems(st, nc, [
            'cst', 'wld',
            'xi', 'var', 'rstd', 'nmr', 'xn', 'tp', 'tpe',
            'sm', 'se', 'md1', 'md2', 'md3', 'gr', 'gb', 'gt', 'gte',
            'yp', 'ye', 'yb', 'ysq', 'smm', 'smv', 'rowe', 'rvar', 'rrstd',
            'rnmr', 'rt', 'rte', 'yt', 'yte', 'ap', 'xr', 'xo',
        ])

        p = Prog()

        p.dma('sync', lambda e: e.dma_start(out=identb[:], in_=identb_h.ap()[:, :]),
              inc=S['cst'])
        p.dma('sync', lambda e: e.dma_start(out=ident32[:], in_=ident32_h.ap()[:, :]),
              inc=S['cst'])
        p.op('vector', lambda e: e.memset(epspp[:], EPS_LN))
        p.op('vector', lambda e: e.memset(ones128b[:], 1.0))
        p.op('vector', lambda e: e.nop(), inc=S['cst'])
        for d in range(8):
            p.dma('gpsimd', lambda e, d=d: e.dma_start(
                out=winb[:, d * 64:(d + 1) * 64],
                in_=win[d * 128:(d + 1) * 128, :]), inc=S['wld'])
        p.dma('gpsimd', lambda e: e.dma_start(out=woutb[:], in_=wout[:, :]),
              inc=S['wld'])
        p.dma('gpsimd', lambda e: e.dma_start(
            out=binb4[:], in_=binv[None, None, :].broadcast_to([128, 4, 64])),
            inc=S['wld'])
        p.dma('gpsimd', lambda e: e.dma_start(
            out=w1b[:], in_=w1.flatten()[None, :].broadcast_to([128, 1024])),
            inc=S['wld'])
        p.dma('gpsimd', lambda e: e.dma_start(
            out=b1b[:], in_=b1.flatten()[None, :].broadcast_to([128, 1024])),
            inc=S['wld'])
        p.dma('gpsimd', lambda e: e.dma_start(
            out=w2b[:], in_=w2.flatten()[None, :].broadcast_to([128, 1024])),
            inc=S['wld'])
        p.dma('gpsimd', lambda e: e.dma_start(
            out=b2b4[:], in_=b2[None, None, :].broadcast_to([128, 4, 64])),
            inc=S['wld'])
        for c in range(8):
            p.dma('sync', lambda e, c=c: e.dma_start(
                out=boc[:, c:c + 1], in_=bout[c * 128:(c + 1) * 128][:, None]),
                inc=S['wld'])

        for m in range(NM):
            mb = m % 2
            for i in range(SUB):
                p.dma('sync', lambda e, m=m, i=i: e.dma_start(
                    out=xin[i][:],
                    in_=x[(m * SUB + i) * 128:(m * SUB + i + 1) * 128, :]),
                    waits=[S['xn'], S['ap']], inc=S['xi'])
            emit_preln_stats(p, xin, ln, S, S['xi'])
            emit_xn_and_transpose(p, xin, xnb, xnf[mb], ln, ppT, identb, S)

            for i2 in range(SUB):
                for d in range(8):
                    p.op('tensor', lambda e, i2=i2, d=d, mb=mb: e.matmul(
                        out=ppS[:, i2 * 64:(i2 + 1) * 64],
                        lhsT=xnf[mb][d][:, i2 * 128:(i2 + 1) * 128],
                        rhs=winb[:, d * 64:(d + 1) * 64],
                        start=(i2 == 0 and d == 0), stop=(i2 == 3 and d == 7)),
                        waits=([S['tpe'], S['wld'], S['se']]
                               if (i2 == 0 and d == 0) else ()),
                        inc=(S['sm'] if (i2 == 3 and d == 7) else None))
            p.op('vector', lambda e: e.tensor_add(
                out=s4[:], in0=ppS[:, 0:256], in1=binb4[:]),
                waits=[S['sm'], S['md1']], inc=S['se'])
            for i2 in range(SUB):
                sb_view = s4[:, i2 * 64:(i2 + 1) * 64].rearrange(
                    "p (m o) -> p m o", o=1).broadcast_to([128, 64, 16])
                w1v = w1b[:].rearrange("p (m i) -> p m i", i=16)
                p.op('vector', lambda e, i2=i2, sb_view=sb_view, w1v=w1v:
                     e.tensor_tensor(
                         out=mid1[i2 % 2][:].rearrange("p (m i) -> p m i", i=16),
                         in0=sb_view, in1=w1v, op=ALU.mult),
                     waits=[S['se'], S['md2']], inc=S['md1'])
                p.drain('vector')
                p.op('vector', lambda e, i2=i2: e.tensor_add(
                    out=mid2[i2 % 2][:], in0=mid1[i2 % 2][:], in1=b1b[:]),
                    waits=[S['md3']], inc=S['md1'])
                p.op('scalar', lambda e, i2=i2: e.activation(
                    out=mid1[i2 % 2][:], in_=mid2[i2 % 2][:], func=AF.Relu),
                    waits=[S['md1']], inc=S['md2'])
                p.op('vector', lambda e, i2=i2: e.tensor_mul(
                    out=mid2[i2 % 2][:], in0=mid1[i2 % 2][:], in1=w2b[:]),
                    waits=[S['md2']], inc=S['md3'])
                p.drain('vector')
                p.op('vector', lambda e, i2=i2: e.tensor_reduce(
                    out=gf32[:, i2 * 64:(i2 + 1) * 64],
                    in_=mid2[i2 % 2][:].rearrange("p (m i) -> p m i", i=16),
                    axis=AX.X, op=ALU.add), inc=S['gr'])
                p.drain('vector')
            p.op('vector', lambda e: e.tensor_add(
                out=gbf[:], in0=gf32[:], in1=b2b4[:]),
                waits=[S['gr'], S['gt']], inc=S['gb'])
            for i2 in range(SUB):
                p.op('tensor', lambda e, i2=i2: e.transpose(
                    out=ppG[:].bitcast(BF16)[0:64, i2 * 128:(i2 + 1) * 128],
                    in_=gbf[:, i2 * 64:(i2 + 1) * 64], identity=identb[:]),
                    waits=([S['gb'], S['gte']] if i2 == 0 else ()),
                    inc=(S['gt'] if i2 == 3 else None))
            p.op('vector', lambda e: e.tensor_copy(
                out=gF[:], in_=ppG[:].bitcast(BF16)[0:64, 0:512]),
                waits=[S['gt']], inc=S['gte'])
            for o in range(8):
                py = ppY[o % 2]
                p.op('tensor', lambda e, o=o, py=py: e.matmul(
                    out=py[:], lhsT=woutb[:, o * 128:(o + 1) * 128],
                    rhs=gF[:], start=True, stop=True),
                    waits=[S['gte'], S['ye']], inc=S['yp'])
                p.op('vector', lambda e, o=o, py=py: e.tensor_scalar(
                    out=yf[o][:], in0=py[:], scalar1=boc[:, o:o + 1],
                    scalar2=None, op0=ALU.add),
                    waits=[S['yp']], inc=S['ye'])
            emit_postln_tail(
                p, S, yf, row, rc, ppS1, ppS2, ppTr, yb, ysq, ones128b,
                ident32, xres, scr2, yt,
                x_src=lambda i, m=m: x[(m * SUB + i) * 128:
                                       (m * SUB + i + 1) * 128, :],
                out_dst=lambda i, m=m: x2[(m * SUB + i) * 128:
                                          (m * SUB + i + 1) * 128, :],
                epspp=epspp)

        p.op('sync', lambda e: e.nop(), waits=[S['xo']])

        with nc.Block() as block:
            p.replay(block)
    return nc


_L0FFN_NC = None


def _get_l0ffn():
    global _L0FFN_NC
    if _L0FFN_NC is None:
        _L0FFN_NC = build_l0ffn()
    return _L0FFN_NC


def run_l0ffn(x_cores, W):
    nc = _get_l0ffn()
    in_maps = []
    for c in range(NCORE):
        in_maps.append({
            "x": x_cores[c],
            "win": W["L0_ffn_Win"], "binv": W["L0_ffn_bin_"],
            "w1": W["L0_ffn_W1"], "b1": W["L0_ffn_b1"],
            "w2": W["L0_ffn_W2"], "b2": W["L0_ffn_b2"],
            "wout": W["L0_ffn_Wout"], "bout": W["L0_ffn_bout"],
        })
    res = run_bass_kernel_spmd(nc, in_maps, core_ids=list(range(NCORE)))
    return [res.results[c]["x2"] for c in range(NCORE)]



def build_l1attn():
    nc = bass.Bass("TRN2", target_bir_lowering=False)
    x = nc.dram_tensor("x", [T, D], F32, kind="ExternalInput").ap()
    wq = nc.dram_tensor("wq", [D, D], F32, kind="ExternalInput").ap()
    wk = nc.dram_tensor("wk", [D, D], F32, kind="ExternalInput").ap()
    wv = nc.dram_tensor("wv", [D, D], F32, kind="ExternalInput").ap()
    wo = nc.dram_tensor("wo", [D, D], F32, kind="ExternalInput").ap()
    bq = nc.dram_tensor("bq", [D], F32, kind="ExternalInput").ap()
    bk = nc.dram_tensor("bk", [D], F32, kind="ExternalInput").ap()
    bv = nc.dram_tensor("bv", [D], F32, kind="ExternalInput").ap()
    bo = nc.dram_tensor("bo", [D], F32, kind="ExternalInput").ap()
    rcos = nc.dram_tensor("rcos", [128, T], F32, kind="ExternalInput").ap()
    rsin = nc.dram_tensor("rsin", [128, T], F32, kind="ExternalInput").ap()
    x3 = nc.dram_tensor("x3", [T, D], F32, kind="ExternalOutput").ap()

    wbf = [nc.dram_tensor(f"w{n}bf", [D, D], BF16, kind="Internal").ap()
           for n in ("q", "k", "v", "o")]
    xn2 = nc.dram_tensor("xn2", [D, T], BF16, kind="Internal").ap()
    ks_h = nc.dram_tensor("ksh", [NH, 65, T], BF16, kind="Internal").ap()
    vs_h = nc.dram_tensor("vsh", [T, NH * 65], BF16, kind="Internal").ap()

    identb_h = nc.inline_tensor(_bf(np.eye(128, dtype=np.float32)), "identb")
    ident32_h = nc.inline_tensor(np.eye(128, dtype=np.float32), "ident32")
    trim_h = nc.inline_tensor(np_trimask(), "trimask")

    st = ExitStack()
    with st:
        sb = lambda name, shape, dt: st.enter_context(
            nc.sbuf_tensor(name, shape, dt))
        ps = lambda name, shape: st.enter_context(
            nc.psum_tensor(name, shape, F32))

        PP = [ps(f"pp{i}", [128, 512]) for i in range(8)]
        ppT = PP[7]
        ppQ = [PP[0], PP[1]]
        ppA = PP[2]
        ppS_ = [PP[3], PP[4]]
        ppO = [PP[5], PP[6]]
        ppS1, ppS2 = PP[3], PP[4]
        ppTr = PP[7]

        identb = sb("identb_s", [128, 128], BF16)
        ident32 = sb("ident32_s", [128, 128], F32)
        trim = sb("trim_s", [128, 128], BF16)
        wT = [[sb(f"wT{g}_{d}", [128, D], BF16) for d in range(8)]
              for g in range(2)]
        xin = [sb(f"xin{i}", [128, D], F32) for i in range(SUB)]
        xnb = [sb(f"xnb{i}", [128, D], BF16) for i in range(2)]
        xnf = [[sb(f"xnf{m}_{d}", [128, MAC], BF16) for d in range(8)]
               for m in range(2)]
        kst = [sb(f"kst{h}", [65, MAC], BF16) for h in range(NH)]
        vst = [sb(f"vst{i}", [128, NH * 65], BF16) for i in range(SUB)]
        kro = sb("kro", [128, MAC], F32)
        ksh2 = sb("ksh2", [128, MAC], F32)
        scrA = sb("scrA", [128, MAC], F32)
        scrB = sb("scrB", [128, MAC], F32)
        cosS = sb("cosS", [128, MAC], F32)
        sinS = sb("sinS", [128, MAC], F32)
        kc2 = sb("kc2", [1, 2], F32)
        nkc2 = sb("nkc2", [1, 1], F32)
        bcol = sb("bcol", [128, 32], F32)
        bvb = sb("bvb", [128, D], F32)
        kl = [sb(f"kl{i}", [65, MAC], BF16) for i in range(4)]
        vl = [sb(f"vl{i}", [128, 4, 130], BF16) for i in range(2)]
        pt = [sb(f"pt{i}", [128, MAC], BF16) for i in range(2)]
        ptm = sb("ptm", [128, 128], BF16)
        attfL = [sb(f"attf{v}", [128, MAC], BF16) for v in range(8)]
        rr = [sb(f"rr{i}", [1, MAC], F32) for i in range(2)]
        rb = [sb(f"rb{i}", [64, MAC], F32) for i in range(2)]
        epspp = sb("epspp", [128, 1], F32)
        ones64 = sb("ones64", [64, 1], BF16)
        ones64F = sb("ones64F", [128, 1], F32)
        ones128b = sb("ones128b", [128, 1], BF16)
        onesrF = sb("onesrF", [1, 64], F32)
        ln = dict(
            scol=sb("scol", [128, SUB], F32), s2col=sb("s2col", [128, SUB], F32),
            mcol=sb("mcol", [128, SUB], F32), e2col=sb("e2col", [128, SUB], F32),
            mmcol=sb("mmcol", [128, SUB], F32),
            varcol=sb("varcol", [128, SUB], F32),
            lncol=sb("lncol", [128, SUB], F32),
            rstdcol=sb("rstdcol", [128, SUB], F32),
            nmrcol=sb("nmrcol", [128, SUB], F32),
            scr=sb("lnscr", [128, D], F32), epspp=epspp)
        yf = [sb(f"yf{o}", [128, MAC], F32) for o in range(8)]
        yb = [sb(f"ybx{i}", [128, MAC], BF16) for i in range(2)]
        ysq = [sb(f"ysq{i}", [128, MAC], BF16) for i in range(2)]
        yt = [sb(f"yt{i}", [128, D], F32) for i in range(SUB)]
        rc = sb("rc", [128, 8], F32)
        row = dict(m=sb("rowm", [1, MAC], F32), e2=sb("rowe2", [1, MAC], F32),
                   mm=sb("rowmm", [1, MAC], F32),
                   var=sb("rowvar", [1, MAC], F32),
                   rstd=sb("rowrstd", [1, MAC], F32),
                   nmr=sb("rownmr", [1, MAC], F32))
        row['ln'] = row['mm']
        xres = xin
        scr2 = [ln["scr"]]

        S = mksems(st, nc, [
            'cst', 'wld', 'wst', 'w2ld',
            'xi', 'var', 'rstd', 'nmr', 'xn', 'tp', 'tpe', 'xnst',
            'kmm', 'kev', 'ksh', 'krp', 'kn', 'knm', 'kdma', 'vev', 'vdma',
            'tbl', 'qb',
            'kL', 'vL', 'ss', 'px', 'mk', 'pv', 'od', 'at', 'bc',
            'rbe',
            'yp', 'ye', 'yb', 'ysq', 'smm', 'smv', 'rowe', 'rvar', 'rrstd',
            'rnmr', 'rt', 'rte', 'yt', 'yte', 'ap', 'xr', 'xo',
        ])

        p = Prog()

        p.dma('sync', lambda e: e.dma_start(out=identb[:], in_=identb_h.ap()[:, :]),
              inc=S['cst'])
        p.dma('sync', lambda e: e.dma_start(out=ident32[:], in_=ident32_h.ap()[:, :]),
              inc=S['cst'])
        p.dma('sync', lambda e: e.dma_start(out=trim[:], in_=trim_h.ap()[:, :]),
              inc=S['cst'])
        p.op('vector', lambda e: e.memset(epspp[:], EPS_LN))
        p.op('vector', lambda e: e.memset(ones64[:], 1.0))
        p.op('vector', lambda e: e.memset(ones64F[:], 1.0))
        p.op('vector', lambda e: e.memset(ones128b[:], 1.0))
        p.op('vector', lambda e: e.memset(onesrF[:], 1.0))
        p.op('vector', lambda e: e.memset(kc2[:], 0.0))
        for h in range(NH):
            p.op('vector', lambda e, h=h: e.memset(kst[h][64:65, :], 1.0))
        for i in range(SUB):
            p.op('vector', lambda e, i=i: e.memset(
                vst[i][:].rearrange("p (h c) -> p h c", c=65)[:, :, 64:65], 1.0))
        p.op('vector', lambda e: e.nop(), inc=S['cst'])
        for gi, bsrc in enumerate((bq, bk, bv, bo)):
            for c in range(8):
                p.dma('sync', lambda e, gi=gi, c=c, bsrc=bsrc: e.dma_start(
                    out=bcol[:, gi * 8 + c:gi * 8 + c + 1],
                    in_=bsrc[c * 128:(c + 1) * 128][:, None]), inc=S['cst'])
        p.dma('gpsimd', lambda e: e.dma_start(
            out=bvb[:], in_=bv[None, :].broadcast_to([128, D])), inc=S['cst'])
        wstage = [yb[0], yb[1]]
        wstage = [xnb[0], xnb[1]]
        for gi, wsrc in enumerate((wq, wk, wv, wo)):
            for c in range(8):
                p.dma('gpsimd', lambda e, c=c, wsrc=wsrc: e.dma_start(
                    out=wstage[c % 2][:], in_=wsrc[c * 128:(c + 1) * 128, :]),
                    waits=[S['wst']], inc=S['wld'])
                p.dma('sync', lambda e, gi=gi, c=c: e.dma_start(
                    out=wbf[gi][c * 128:(c + 1) * 128, :],
                    in_=wstage[c % 2][:]),
                    waits=[S['wld']], inc=S['wst'])
        for d in range(8):
            p.dma('sync', lambda e, d=d: e.dma_start_transpose(
                out=wT[0][d][:], in_=wbf[1][0:D, d * 128:(d + 1) * 128]),
                waits=[S['wst']], inc=S['w2ld'])
            p.dma('sync', lambda e, d=d: e.dma_start_transpose(
                out=wT[1][d][:], in_=wbf[2][0:D, d * 128:(d + 1) * 128]),
                waits=[S['wst']], inc=S['w2ld'])

        for m in range(NM):
            mb = m % 2
            for i in range(SUB):
                p.dma('sync', lambda e, m=m, i=i: e.dma_start(
                    out=xin[i][:],
                    in_=x[(m * SUB + i) * 128:(m * SUB + i + 1) * 128, :]),
                    waits=[S['xn'], S['wld']], inc=S['xi'])
            emit_preln_stats(p, xin, ln, S, S['xi'])
            emit_xn_and_transpose(p, xin, xnb, xnf[mb], ln, ppT, identb, S)
            for d in range(8):
                p.dma('scalar', lambda e, m=m, d=d, mb=mb: e.dma_start(
                    out=xn2[d * 128:(d + 1) * 128, m * MAC:(m + 1) * MAC],
                    in_=xnf[mb][d][:]),
                    waits=[S['tpe']], inc=S['xnst'])
            p.dma('sync', lambda e, m=m: e.dma_start(
                out=cosS[:], in_=rcos[:, m * MAC:(m + 1) * MAC]),
                waits=[S['krp']], inc=S['tbl'])
            p.dma('sync', lambda e, m=m: e.dma_start(
                out=sinS[:], in_=rsin[:, m * MAC:(m + 1) * MAC]),
                waits=[S['krp']], inc=S['tbl'])
            for oc in range(8):
                qb_ = ppQ[oc % 2]
                for d in range(8):
                    p.op('tensor', lambda e, oc=oc, d=d, qb_=qb_, mb=mb:
                         e.matmul(
                             out=qb_[:],
                             lhsT=wT[0][d][:, oc * 128:(oc + 1) * 128],
                             rhs=xnf[mb][d][:], start=(d == 0), stop=(d == 7)),
                         waits=([S['tpe'], S['w2ld'], S['kev']] if d == 0
                                else ()),
                         inc=(S['kmm'] if d == 7 else None))
                p.op('vector', lambda e, oc=oc, qb_=qb_: e.tensor_scalar(
                    out=kro[:], in0=qb_[:], scalar1=bcol[:, 8 + oc:9 + oc],
                    scalar2=None, op0=ALU.add),
                    waits=[S['kmm'], S['krp']], inc=S['kev'])
                p.drain('vector')
                p.op('vector', lambda e: e.tensor_mul(
                    out=scrB[:], in0=kro[:], in1=kro[:]),
                    waits=[S['kn']], inc=S['ksh'])
                for h2 in range(2):
                    p.op('tensor', lambda e, h2=h2: e.matmul(
                        out=ppA[0:1, :],
                        lhsT=ones64F[h2 * 64:(h2 + 1) * 64, :],
                        rhs=scrB[h2 * 64:(h2 + 1) * 64, :],
                        start=True, stop=True),
                        waits=[S['ksh'], S['knm']], inc=S['kn'])
                    p.op('vector', lambda e, h2=h2: e.tensor_reduce(
                        out=kc2[0:1, 1:2], in_=ppA[0:1, :], axis=AX.X,
                        op=ALU.max), waits=[S['kn']])
                    p.drain('vector')
                    p.op('vector', lambda e: e.tensor_max(
                        out=kc2[0:1, 0:1], in0=kc2[0:1, 0:1],
                        in1=kc2[0:1, 1:2]), inc=S['knm'])
                    p.drain('vector')
                p.op('vector', lambda e: e.tensor_copy(
                    out=ksh2[0:32, :], in_=kro[32:64, :]))
                p.op('vector', lambda e: e.tensor_copy(
                    out=ksh2[32:64, :], in_=kro[0:32, :]))
                p.op('vector', lambda e: e.tensor_copy(
                    out=ksh2[64:96, :], in_=kro[96:128, :]))
                p.op('vector', lambda e: e.tensor_copy(
                    out=ksh2[96:128, :], in_=kro[64:96, :]))
                p.drain('vector')
                p.op('vector', lambda e: e.tensor_mul(
                    out=scrA[:], in0=kro[:], in1=cosS[:]), waits=[S['tbl']])
                p.op('vector', lambda e: e.tensor_mul(
                    out=scrB[:], in0=ksh2[:], in1=sinS[:]), waits=[S['kn']])
                p.drain('vector')
                for h2 in range(2):
                    p.op('vector', lambda e, oc=oc, h2=h2: e.tensor_add(
                        out=kst[2 * oc + h2][0:64, :],
                        in0=scrA[h2 * 64:(h2 + 1) * 64, :],
                        in1=scrB[h2 * 64:(h2 + 1) * 64, :]),
                        waits=[S['kdma']],
                        inc=(S['krp'] if h2 == 1 else None))
                p.drain('vector')
                for h2 in range(2):
                    p.dma('scalar', lambda e, m=m, oc=oc, h2=h2: e.dma_start(
                        out=ks_h[2 * oc + h2, :, m * MAC:(m + 1) * MAC],
                        in_=kst[2 * oc + h2][:]),
                        waits=[S['krp']], inc=S['kdma'])
            for i2 in range(SUB):
                for vh in range(2):
                    qb_ = ppQ[vh % 2]
                    for d in range(8):
                        p.op('tensor', lambda e, i2=i2, vh=vh, d=d, qb_=qb_,
                             mb=mb: e.matmul(
                                 out=qb_[:],
                                 lhsT=xnf[mb][d][:, i2 * 128:(i2 + 1) * 128],
                                 rhs=wT[1][d][:, vh * 512:(vh + 1) * 512],
                                 start=(d == 0), stop=(d == 7)),
                             waits=([S['kev'], S['vev']] if d == 0 else ()),
                             inc=(S['kmm'] if d == 7 else None))
                    p.op('vector', lambda e, i2=i2, vh=vh, qb_=qb_:
                         e.tensor_tensor(
                             out=vst[i2][:].rearrange("p (h c) -> p h c", c=65)
                                 [:, vh * 8:(vh + 1) * 8, 0:64],
                             in0=qb_[:].rearrange("p (h c) -> p h c", c=64),
                             in1=bvb[:, vh * 512:(vh + 1) * 512].rearrange(
                                 "p (h c) -> p h c", c=64),
                             op=ALU.add),
                         waits=[S['kmm'], S['vdma']], inc=S['vev'])
                p.dma('scalar', lambda e, m=m, i2=i2: e.dma_start(
                    out=vs_h[(m * SUB + i2) * 128:(m * SUB + i2 + 1) * 128, :],
                    in_=vst[i2][:]),
                    waits=[S['vev']], inc=S['vdma'])

        p.op('vector', lambda e: e.tensor_scalar(
            out=nkc2[:], in0=kc2[0:1, 0:1],
            scalar1=-0.5 * ATT_S * ATT_S, scalar2=None, op0=ALU.mult),
            waits=[S['knm']], inc=S['knm'])

        for d in range(8):
            p.dma('sync', lambda e, d=d: e.dma_start_transpose(
                out=wT[0][d][:], in_=wbf[0][0:D, d * 128:(d + 1) * 128]),
                waits=[S['kmm']], inc=S['w2ld'])
            p.dma('sync', lambda e, d=d: e.dma_start_transpose(
                out=wT[1][d][:], in_=wbf[3][0:D, d * 128:(d + 1) * 128]),
                waits=[S['kmm']], inc=S['w2ld'])

        ptrot = [0]
        for qi in range(NM):
            for d in range(8):
                p.dma('sync', lambda e, qi=qi, d=d: e.dma_start(
                    out=xnf[0][d][:],
                    in_=xn2[d * 128:(d + 1) * 128, qi * MAC:(qi + 1) * MAC]),
                    waits=[S['kmm'], S['xnst']], inc=S['xi'])
            p.dma('sync', lambda e, qi=qi: e.dma_start(
                out=cosS[:], in_=rcos[:, qi * MAC:(qi + 1) * MAC]),
                waits=[S['krp']], inc=S['tbl'])
            p.dma('sync', lambda e, qi=qi: e.dma_start(
                out=sinS[:], in_=rsin[:, qi * MAC:(qi + 1) * MAC]),
                waits=[S['krp']], inc=S['tbl'])
            for oc in range(8):
                qb_ = ppQ[oc % 2]
                for d in range(8):
                    p.op('tensor', lambda e, oc=oc, d=d, qb_=qb_: e.matmul(
                        out=qb_[:], lhsT=wT[0][d][:, oc * 128:(oc + 1) * 128],
                        rhs=xnf[0][d][:], start=(d == 0), stop=(d == 7)),
                        waits=([S['xi'], S['w2ld'], S['kev'], S['ye']]
                               if d == 0 else ()),
                        inc=(S['kmm'] if d == 7 else None))
                p.op('vector', lambda e, oc=oc, qb_=qb_: e.tensor_scalar(
                    out=kro[:], in0=qb_[:], scalar1=bcol[:, oc:oc + 1],
                    scalar2=None, op0=ALU.add),
                    waits=[S['kmm'], S['krp']], inc=S['kev'])
                p.drain('vector')
                p.op('vector', lambda e: e.tensor_mul(
                    out=scrB[:], in0=kro[:], in1=kro[:]),
                    waits=[S['kn']], inc=S['ksh'])
                for h2 in range(2):
                    p.op('tensor', lambda e, h2=h2: e.matmul(
                        out=ppA[0:1, :],
                        lhsT=ones64F[h2 * 64:(h2 + 1) * 64, :],
                        rhs=scrB[h2 * 64:(h2 + 1) * 64, :],
                        start=True, stop=True),
                        waits=[S['ksh'], S['knm']], inc=S['kn'])
                    p.op('scalar', lambda e, oc=oc, h2=h2: e.activation(
                        out=kst[2 * oc + h2][64:65, :], in_=ppA[0:1, :],
                        func=AF.Identity, scale=-0.5 * ATT_S * ATT_S,
                        bias=nkc2[0:1, 0:1]),
                        waits=[S['kn'], S['ss'], S['kdma']], inc=S['knm'])
                p.op('vector', lambda e: e.tensor_copy(
                    out=ksh2[0:32, :], in_=kro[32:64, :]))
                p.op('vector', lambda e: e.tensor_copy(
                    out=ksh2[32:64, :], in_=kro[0:32, :]))
                p.op('vector', lambda e: e.tensor_copy(
                    out=ksh2[64:96, :], in_=kro[96:128, :]))
                p.op('vector', lambda e: e.tensor_copy(
                    out=ksh2[96:128, :], in_=kro[64:96, :]))
                p.drain('vector')
                p.op('vector', lambda e: e.tensor_mul(
                    out=scrA[:], in0=kro[:], in1=cosS[:]), waits=[S['tbl']])
                p.op('vector', lambda e: e.tensor_mul(
                    out=scrB[:], in0=ksh2[:], in1=sinS[:]), waits=[S['kn']])
                p.drain('vector')
                for h2 in range(2):
                    p.op('vector', lambda e, oc=oc, h2=h2: e.tensor_add(
                        out=kst[2 * oc + h2][0:64, :],
                        in0=scrA[h2 * 64:(h2 + 1) * 64, :],
                        in1=scrB[h2 * 64:(h2 + 1) * 64, :]),
                        waits=[S['ss'], S['kdma']],
                        inc=(S['krp'] if h2 == 1 else None))
                p.drain('vector')
            p.op('vector', lambda e: e.nop(), inc=S['qb'])

            for pr in range(8):
                first_pv = [True, True]
                for ki in range(qi + 1):
                    kb = ki % 2
                    diag = (ki == qi)
                    for hh in range(2):
                        h = 2 * pr + hh
                        p.dma('sync', lambda e, h=h, ki=ki, kb=kb, hh=hh:
                              e.dma_start(
                                  out=kl[hh * 2 + kb][:],
                                  in_=ks_h[h, :, ki * MAC:(ki + 1) * MAC]),
                              waits=[S['ss'], S['qb']], inc=S['kL'])
                    p.dma('sync', lambda e, pr=pr, ki=ki, kb=kb: e.dma_start(
                        out=vl[kb][:],
                        in_=vs_h[ki * MAC:(ki + 1) * MAC,
                                 2 * pr * 65:2 * pr * 65 + 130].rearrange(
                                     "(s p) c -> p s c", p=128)),
                        waits=[S['pv']], inc=S['vL'])
                    for hh in range(2):
                        h = 2 * pr + hh
                        nks = 4
                        pend = []
                        for ks in range(nks):
                            if diag:
                                nq = MAC - ks * 128
                                qoff = ks * 128
                            else:
                                nq = MAC
                                qoff = 0
                            rot = ptrot[0] % 2
                            ptrot[0] += 1
                            sbank = ppS_[ks % 2]
                            p.op('tensor', lambda e, h=h, ks=ks, kb=kb,
                                 hh=hh, nq=nq, qoff=qoff, sbank=sbank:
                                 e.matmul(
                                     out=sbank[0:128, 0:nq],
                                     lhsT=kl[hh * 2 + kb][:, ks * 128:
                                                          (ks + 1) * 128],
                                     rhs=kst[h][:, qoff:MAC],
                                     start=True, stop=True),
                                 waits=[S['kL'], S['qb'], S['px']],
                                 inc=S['ss'])
                            p.op('scalar', lambda e, rot=rot, nq=nq,
                                 sbank=sbank: e.activation(
                                     out=pt[rot][0:128, 0:nq],
                                     in_=sbank[0:128, 0:nq], func=AF.Exp),
                                 waits=[S['ss'], S['pv'], S['mk']],
                                 inc=S['px'])
                            if diag:
                                p.op('vector', lambda e, rot=rot:
                                     e.tensor_mul(
                                         out=ptm[:], in0=pt[rot][:, 0:128],
                                         in1=trim[:]),
                                     waits=[S['px'], S['pv']], inc=S['mk'])
                            qs_list = range(ks, 4) if diag else range(4)
                            for qs in qs_list:
                                dblk = diag and (qs == ks)
                                rhs_ap = (ptm[:] if dblk else
                                          pt[rot][:, (qs * 128 - qoff):
                                                  (qs * 128 - qoff) + 128])
                                w = [S['mk'] if dblk else S['px'], S['vL']]
                                if first_pv[hh]:
                                    w.append(S['at'])
                                p.op('tensor', lambda e, hh=hh, ks=ks, kb=kb,
                                     qs=qs, rhs_ap=rhs_ap, h=h,
                                     fp=first_pv[hh]: e.matmul(
                                         out=ppO[hh][0:65,
                                                     qs * 128:(qs + 1) * 128],
                                         lhsT=vl[kb][:, ks,
                                                     hh * 65:(hh + 1) * 65],
                                         rhs=rhs_ap,
                                         start=fp,
                                         stop=(ki == qi and ks == 3 and
                                               qs == 3)),
                                     waits=w, inc=S['pv'])
                                first_pv[hh] = False
                p.op('tensor', lambda e: e.nop(), waits=[S['pv']],
                     inc=S['od'])
                for hh in range(2):
                    p.op('vector', lambda e, hh=hh: e.reciprocal(
                        out=rr[hh][:], in_=ppO[hh][64:65, :]),
                        waits=[S['od'], S['bc']], inc=S['rbe'])
                    p.drain('vector')
                    p.op('tensor', lambda e, hh=hh: e.matmul(
                        out=ppA[0:64, :], lhsT=onesrF[:], rhs=rr[hh][:],
                        start=True, stop=True),
                        waits=[S['rbe'], S['knm'], S['at']], inc=S['bc'])
                    p.op('scalar', lambda e, hh=hh: e.activation(
                        out=rb[hh][:], in_=ppA[0:64, :], func=AF.Identity),
                        waits=[S['bc']], inc=S['rbe'])
                    p.op('vector', lambda e, pr=pr, hh=hh: e.tensor_tensor(
                        out=attfL[pr][hh * 64:(hh + 1) * 64, :],
                        in0=ppO[hh][0:64, :], in1=rb[hh][:], op=ALU.mult),
                        waits=[S['rbe'], S['yp']], inc=S['at'])
            for o in range(8):
                py = ppQ[o % 2]
                for v in range(8):
                    p.op('tensor', lambda e, o=o, v=v, py=py: e.matmul(
                        out=py[:], lhsT=wT[1][v][:, o * 128:(o + 1) * 128],
                        rhs=attfL[v][:], start=(v == 0), stop=(v == 7)),
                        waits=([S['at'], S['ye'], S['kev']] if v == 0 else ()),
                        inc=(S['yp'] if v == 7 else None))
                p.op('vector', lambda e, o=o, py=py: e.tensor_scalar(
                    out=yf[o][:], in0=py[:], scalar1=bcol[:, 24 + o:25 + o],
                    scalar2=None, op0=ALU.add),
                    waits=[S['yp']], inc=S['ye'])
            emit_postln_tail(
                p, S, yf, row, rc, ppS1, ppS2, ppTr, yb, ysq, ones128b,
                ident32, xres, scr2, yt,
                x_src=lambda i, qi=qi: x[(qi * SUB + i) * 128:
                                         (qi * SUB + i + 1) * 128, :],
                out_dst=lambda i, qi=qi: x3[(qi * SUB + i) * 128:
                                            (qi * SUB + i + 1) * 128, :],
                epspp=epspp)

        p.op('sync', lambda e: e.nop(), waits=[S['xo']])

        with nc.Block() as block:
            p.replay(block)
    return nc


_L1ATTN_NC = None


def _get_l1attn():
    global _L1ATTN_NC
    if _L1ATTN_NC is None:
        _L1ATTN_NC = build_l1attn()
    return _L1ATTN_NC


def run_l1attn(x_cores, W):
    nc = _get_l1attn()
    rcos, rsin = np_rope_tables()
    in_maps = []
    for c in range(NCORE):
        in_maps.append({
            "x": x_cores[c],
            "wq": W["L1_attn_Wq"], "bq": W["L1_attn_bq"],
            "wk": W["L1_attn_Wk"], "bk": W["L1_attn_bk"],
            "wv": W["L1_attn_Wv"], "bv": W["L1_attn_bv"],
            "wo": W["L1_attn_Wo"], "bo": W["L1_attn_bo"],
            "rcos": rcos, "rsin": rsin,
        })
    res = run_bass_kernel_spmd(nc, in_maps, core_ids=list(range(NCORE)))
    return [res.results[c]["x3"] for c in range(NCORE)]



NEXP = 9
QTOK = 1024
NQT = T // QTOK
IC = 11


def build_moe():
    nc = bass.Bass("TRN2", target_bir_lowering=False)
    x = nc.dram_tensor("x", [T, D], F32, kind="ExternalInput").ap()
    gw = nc.dram_tensor("gw", [8, D], F32, kind="ExternalInput").ap()
    wg = nc.dram_tensor("wg", [8, D, 1408], F32, kind="ExternalInput").ap()
    wu = nc.dram_tensor("wu", [8, D, 1408], F32, kind="ExternalInput").ap()
    wd = nc.dram_tensor("wd", [8, 1408, D], F32, kind="ExternalInput").ap()
    swg = nc.dram_tensor("swg", [D, 1408], F32, kind="ExternalInput").ap()
    swu = nc.dram_tensor("swu", [D, 1408], F32, kind="ExternalInput").ap()
    swd = nc.dram_tensor("swd", [1408, D], F32, kind="ExternalInput").ap()
    sgw = nc.dram_tensor("sgw", [1, D], F32, kind="ExternalInput").ap()
    x4 = nc.dram_tensor("x4", [T, D], F32, kind="ExternalOutput").ap()

    wgbf = nc.dram_tensor("wgbf", [NEXP, D, 1408], BF16, kind="Internal").ap()
    wubf = nc.dram_tensor("wubf", [NEXP, D, 1408], BF16, kind="Internal").ap()
    wdbf = nc.dram_tensor("wdbf", [NEXP, 1408, D], BF16, kind="Internal").ap()

    identb_h = nc.inline_tensor(_bf(np.eye(128, dtype=np.float32)), "identb")
    ident32_h = nc.inline_tensor(np.eye(128, dtype=np.float32), "ident32")

    st = ExitStack()
    with st:
        sb = lambda name, shape, dt: st.enter_context(
            nc.sbuf_tensor(name, shape, dt))
        ps = lambda name, shape: st.enter_context(
            nc.psum_tensor(name, shape, F32))

        PP = [ps(f"pp{i}", [128, 512]) for i in range(8)]
        ppT = PP[7]
        ppTr = PP[7]
        ppGU = [[PP[0], PP[1]], [PP[2], PP[3]]]
        ppD = [PP[4], PP[5]]
        ppX = PP[6]
        ppS1, ppS2 = PP[0], PP[1]

        identb = sb("identb_s", [128, 128], BF16)
        ident32 = sb("ident32_s", [128, 128], F32)
        xin = [sb(f"xin{i}", [128, D], F32) for i in range(SUB)]
        xnb = [sb(f"xnb{i}", [128, D], BF16) for i in range(2)]
        xnfQ = [sb(f"xnfq{d}", [128, QTOK], BF16) for d in range(8)]
        macc = [sb(f"macc{o}", [128, QTOK], F32) for o in range(8)]
        actF = [sb(f"actf{i}", [128, QTOK], BF16) for i in range(IC)]
        dwb = [sb(f"dwb{i}", [128, D], BF16) for i in range(IC)]
        wgb = [sb(f"wgb{i}", [128, D], BF16) for i in range(2)]
        wub = [sb(f"wub{i}", [128, D], BF16) for i in range(2)]
        cmbB = sb("cmbB", [128, QTOK], F32)
        cmball = sb("cmball", [128, 64], F32)
        cmbrow = sb("cmbrow", [1, QTOK], F32)
        sgrow = sb("sgrow", [1, QTOK], F32)
        gwT = sb("gwT", [128, 64], BF16)
        sgwT = sb("sgwT", [128, 8], BF16)
        gstage = sb("gstage", [8, D], BF16)
        sgt = sb("sgt", [128, 512], F32)
        su = sb("su", [128, 512], F32)
        lg0 = sb("lg0", [128, 8], F32)
        el = sb("el", [128, 8], F32)
        mskA = sb("mskA", [128, 8], F32)
        lg1 = sb("lg1", [128, 8], F32)
        nm1 = sb("nm1", [128, 1], F32)
        m2s = sb("m2s", [128, 1], F32)
        dsum = sb("dsum", [128, 1], F32)
        recs = sb("recs", [128, 1], F32)
        epspp = sb("epspp", [128, 1], F32)
        ones128b = sb("ones128b", [128, 1], BF16)
        onesr128F = sb("onesr128F", [1, 128], F32)
        ln = dict(
            scol=sb("scol", [128, SUB], F32), s2col=sb("s2col", [128, SUB], F32),
            mcol=sb("mcol", [128, SUB], F32), e2col=sb("e2col", [128, SUB], F32),
            mmcol=sb("mmcol", [128, SUB], F32),
            varcol=sb("varcol", [128, SUB], F32),
            lncol=sb("lncol", [128, SUB], F32),
            rstdcol=sb("rstdcol", [128, SUB], F32),
            nmrcol=sb("nmrcol", [128, SUB], F32),
            scr=sb("lnscr", [128, D], F32), epspp=epspp)
        yb = [sb(f"ybx{i}", [128, MAC], BF16) for i in range(2)]
        ysq = [sb(f"ysq{i}", [128, MAC], BF16) for i in range(2)]
        yt = [sb(f"yt{i}", [128, D], F32) for i in range(SUB)]
        rc = sb("rc", [128, 8], F32)
        row = dict(m=sb("rowm", [1, MAC], F32), e2=sb("rowe2", [1, MAC], F32),
                   mm=sb("rowmm", [1, MAC], F32),
                   var=sb("rowvar", [1, MAC], F32),
                   rstd=sb("rowrstd", [1, MAC], F32),
                   nmr=sb("rownmr", [1, MAC], F32))
        row['ln'] = row['mm']
        xres = xin
        scr2 = [ln["scr"]]

        S = mksems(st, nc, [
            'cst', 'gld', 'gtp', 'gte',
            'xi', 'var', 'rstd', 'nmr', 'xn', 'tp', 'tpe',
            'glg', 'gch', 'sgm', 'sgs',
            'ctp', 'cte', 'cbc', 'cbe',
            'wldg', 'wstg', 'wldd', 'gmm', 'gev', 'act', 'dmm', 'dev',
            'yp', 'ye', 'yb', 'ysq', 'smm', 'smv', 'rowe', 'rvar', 'rrstd',
            'rnmr', 'rt', 'rte', 'yt', 'yte', 'ap', 'xr', 'xo',
        ])

        p = Prog()

        p.dma('sync', lambda e: e.dma_start(out=identb[:], in_=identb_h.ap()[:, :]),
              inc=S['cst'])
        p.dma('sync', lambda e: e.dma_start(out=ident32[:], in_=ident32_h.ap()[:, :]),
              inc=S['cst'])
        p.op('vector', lambda e: e.memset(epspp[:], EPS_LN))
        p.op('vector', lambda e: e.memset(ones128b[:], 1.0))
        p.op('vector', lambda e: e.memset(onesr128F[:], 1.0))
        p.op('vector', lambda e: e.nop(), inc=S['cst'])
        p.dma('gpsimd', lambda e: e.dma_start(out=gstage[0:8, :], in_=gw[:, :]),
              inc=S['gld'])
        for d in range(8):
            p.op('tensor', lambda e, d=d: e.transpose(
                out=ppX[:].bitcast(BF16)[0:128, d * 8:(d + 1) * 8],
                in_=gstage[0:8, d * 128:(d + 1) * 128],
                identity=identb[0:8, 0:8]),
                waits=([S['gld'], S['cst']] if d == 0 else ()),
                inc=(S['gtp'] if d == 7 else None))
        p.op('vector', lambda e: e.tensor_copy(
            out=gwT[:], in_=ppX[:].bitcast(BF16)[0:128, 0:64]),
            waits=[S['gtp']], inc=S['gte'])
        p.dma('gpsimd', lambda e: e.dma_start(out=gstage[0:1, :], in_=sgw[:, :]),
              waits=[S['gte']], inc=S['gld'])
        for d in range(8):
            p.op('tensor', lambda e, d=d: e.transpose(
                out=ppX[:].bitcast(BF16)[0:128, 2 * d:2 * d + 1],
                in_=gstage[0:1, d * 128:(d + 1) * 128],
                identity=identb[0:1, 0:1]),
                waits=([S['gld'], S['gte']] if d == 0 else ()),
                inc=(S['gtp'] if d == 7 else None))
        p.op('vector', lambda e: e.tensor_copy(
            out=sgwT[:],
            in_=ppX[:].bitcast(BF16)[0:128, 0:16].rearrange(
                "p (a b) -> p a b", b=2)[:, :, 0:1].rearrange(
                    "p a b -> p (a b)")),
            waits=[S['gtp']], inc=S['gte'])

        for q in range(NQT):
            for ml in range(2):
                m = q * 2 + ml
                for i in range(SUB):
                    p.dma('sync', lambda e, m=m, i=i: e.dma_start(
                        out=xin[i][:],
                        in_=x[(m * SUB + i) * 128:(m * SUB + i + 1) * 128, :]),
                        waits=[S['xn'], S['ap'], S['gmm']], inc=S['xi'])
                emit_preln_stats(p, xin, ln, S, S['xi'])
                ppb = ppT[:].bitcast(BF16)
                for i in range(SUB):
                    p.op('scalar', lambda e, i=i: e.activation(
                        out=xnb[i % 2][:], in_=xin[i][:], func=AF.Identity,
                        scale=ln['rstdcol'][:, i:i + 1],
                        bias=ln['nmrcol'][:, i:i + 1]),
                        waits=[S['nmr'], S['tp']], inc=S['xn'])
                    for d in range(8):
                        p.op('tensor', lambda e, i=i, d=d, ppb=ppb: e.transpose(
                            out=ppb[:, d * 128:(d + 1) * 128],
                            in_=xnb[i % 2][:, d * 128:(d + 1) * 128],
                            identity=identb[:]),
                            waits=([S['xn'], S['tpe']] if d == 0 else ()),
                            inc=(S['tp'] if d == 7 else None))
                    for d in range(8):
                        p.op('vector', lambda e, i=i, d=d, ml=ml, ppb=ppb:
                             e.tensor_copy(
                                 out=xnfQ[d][:, ml * 512 + i * 128:
                                             ml * 512 + (i + 1) * 128],
                                 in_=ppb[:, d * 128:(d + 1) * 128]),
                             waits=([S['tp'], S['gmm']] if d == 0 else ()),
                             inc=(S['tpe'] if d == 7 else None))
            for s in range(8):
                for d in range(8):
                    p.op('tensor', lambda e, s=s, d=d: e.matmul(
                        out=ppX[0:128, s * 8:(s + 1) * 8],
                        lhsT=xnfQ[d][:, s * 128:(s + 1) * 128],
                        rhs=gwT[:, d * 8:(d + 1) * 8],
                        start=(d == 0), stop=(d == 7)),
                        waits=([S['tpe'], S['gte'], S['cte'], S['gch']]
                               if d == 0 else ()),
                        inc=(S['glg'] if d == 7 else None))
                p.op('vector', lambda e, s=s: e.tensor_reduce(
                    out=nm1[:], in_=ppX[0:128, s * 8:(s + 1) * 8],
                    axis=AX.X, op=ALU.max, negate=True), waits=[S['glg']])
                p.drain('vector')
                p.op('vector', lambda e, s=s: e.tensor_scalar(
                    out=lg0[:], in0=ppX[0:128, s * 8:(s + 1) * 8],
                    scalar1=nm1[:, 0:1], scalar2=None, op0=ALU.add),
                    inc=S['gch'])
                p.drain('vector')
                p.op('scalar', lambda e: e.activation(
                    out=el[:], in_=lg0[:], func=AF.Exp),
                    waits=[S['gch']], inc=S['sgm'])
                p.op('vector', lambda e: e.tensor_scalar(
                    out=mskA[:], in0=lg0[:], scalar1=-1e-30, scalar2=None,
                    op0=ALU.is_ge))
                p.drain('vector')
                p.op('vector', lambda e: e.scalar_tensor_tensor(
                    out=lg1[:], in0=mskA[:], scalar=-1e30, in1=lg0[:],
                    op0=ALU.mult, op1=ALU.add))
                p.drain('vector')
                p.op('vector', lambda e: e.tensor_reduce(
                    out=m2s[:], in_=lg1[:], axis=AX.X, op=ALU.max))
                p.drain('vector')
                p.op('vector', lambda e: e.tensor_scalar(
                    out=lg1[:], in0=lg0[:], scalar1=m2s[:, 0:1], scalar2=None,
                    op0=ALU.is_ge))
                p.drain('vector')
                p.op('vector', lambda e: e.tensor_tensor(
                    out=el[:], in0=el[:], in1=lg1[:], op=ALU.mult),
                    waits=[S['sgm']])
                p.drain('vector')
                p.op('vector', lambda e: e.tensor_reduce(
                    out=dsum[:], in_=el[:], axis=AX.X, op=ALU.add))
                p.drain('vector')
                p.op('vector', lambda e: e.reciprocal(
                    out=recs[:], in_=dsum[:]))
                p.drain('vector')
                p.op('vector', lambda e, s=s: e.tensor_scalar(
                    out=cmball[:, s * 8:(s + 1) * 8], in0=el[:],
                    scalar1=recs[:, 0:1], scalar2=None, op0=ALU.mult),
                    inc=S['gch'])
                p.drain('vector')
            for tq in range(2):
                for d in range(8):
                    p.op('tensor', lambda e, tq=tq, d=d: e.matmul(
                        out=ppX[0:1, 0:512],
                        lhsT=sgwT[:, d:d + 1],
                        rhs=xnfQ[d][:, tq * 512:(tq + 1) * 512],
                        start=(d == 0), stop=(d == 7)),
                        waits=([S['gch'], S['sgs']] if d == 0 else ()),
                        inc=(S['glg'] if d == 7 else None))
                p.op('scalar', lambda e, tq=tq: e.activation(
                    out=sgrow[0:1, tq * 512:(tq + 1) * 512],
                    in_=ppX[0:1, 0:512], func=AF.Sigmoid),
                    waits=[S['glg']], inc=S['sgs'])

            for ex in range(NEXP):
                if ex < 8:
                    for s in range(8):
                        p.op('tensor', lambda e, ex=ex, s=s: e.transpose(
                            out=ppX[:].bitcast(F32)[0:1,
                                                    (s % 4) * 128:
                                                    (s % 4) * 128 + 128],
                            in_=cmball[:, s * 8 + ex:s * 8 + ex + 1],
                            identity=ident32[:]),
                            waits=([S['gch'], S['sgs'], S['cte']]
                                   if s % 4 == 0 else ()),
                            inc=(S['ctp'] if s % 4 == 3 else None))
                        if s % 4 == 3:
                            half = s // 4
                            p.op('vector', lambda e, half=half: e.tensor_copy(
                                out=cmbrow[0:1, half * 512:(half + 1) * 512],
                                in_=ppX[:].bitcast(F32)[0:1, 0:512]),
                                waits=[S['ctp']], inc=S['cte'])
                else:
                    p.op('vector', lambda e: e.tensor_copy(
                        out=cmbrow[:], in_=sgrow[:]),
                        waits=[S['sgs'], S['cbc']], inc=S['cte'])
                for half in range(2):
                    p.op('tensor', lambda e, half=half: e.matmul(
                        out=ppX[0:128, 0:512], lhsT=onesr128F[:],
                        rhs=cmbrow[0:1, half * 512:(half + 1) * 512],
                        start=True, stop=True),
                        waits=[S['cte'], S['cbe'], S['act']], inc=S['cbc'])
                    p.op('vector', lambda e, half=half: e.tensor_copy(
                        out=cmbB[:, half * 512:(half + 1) * 512],
                        in_=ppX[0:128, 0:512]),
                        waits=[S['cbc']], inc=S['cbe'])
                for ic in range(IC):
                    if q == 0:
                        p.dma('gpsimd', lambda e, ex=ex, ic=ic: e.dma_start(
                            out=wgb[ic % 2][:].rearrange(
                                "p (hc i) -> p hc i", i=128),
                            in_=(wg[ex, :, ic * 128:(ic + 1) * 128]
                                 if ex < 8 else
                                 swg[:, ic * 128:(ic + 1) * 128]).rearrange(
                                     "(hc p) i -> p hc i", p=128)),
                            waits=[S['wstg'], S['gmm']], inc=S['wldg'])
                        p.dma('sync', lambda e, ex=ex, ic=ic: e.dma_start(
                            out=wgbf[ex, :, ic * 128:(ic + 1) * 128].rearrange(
                                "(hc p) i -> p hc i", p=128),
                            in_=wgb[ic % 2][:].rearrange(
                                "p (hc i) -> p hc i", i=128)),
                            waits=[S['wldg']], inc=S['wstg'])
                        p.dma('gpsimd', lambda e, ex=ex, ic=ic: e.dma_start(
                            out=wub[ic % 2][:].rearrange(
                                "p (hc i) -> p hc i", i=128),
                            in_=(wu[ex, :, ic * 128:(ic + 1) * 128]
                                 if ex < 8 else
                                 swu[:, ic * 128:(ic + 1) * 128]).rearrange(
                                     "(hc p) i -> p hc i", p=128)),
                            waits=[S['wstg'], S['gmm']], inc=S['wldg'])
                        p.dma('sync', lambda e, ex=ex, ic=ic: e.dma_start(
                            out=wubf[ex, :, ic * 128:(ic + 1) * 128].rearrange(
                                "(hc p) i -> p hc i", p=128),
                            in_=wub[ic % 2][:].rearrange(
                                "p (hc i) -> p hc i", i=128)),
                            waits=[S['wldg']], inc=S['wstg'])
                        p.dma('gpsimd', lambda e, ex=ex, ic=ic: e.dma_start(
                            out=dwb[ic][:],
                            in_=(wd[ex, ic * 128:(ic + 1) * 128, :]
                                 if ex < 8 else
                                 swd[ic * 128:(ic + 1) * 128, :])),
                            waits=[S['dmm']], inc=S['wldd'])
                        p.dma('sync', lambda e, ex=ex, ic=ic: e.dma_start(
                            out=wdbf[ex, ic * 128:(ic + 1) * 128, :],
                            in_=dwb[ic][:]),
                            waits=[S['wldd']], inc=S['wstg'])
                    else:
                        p.dma('sync', lambda e, ex=ex, ic=ic: e.dma_start(
                            out=wgb[ic % 2][:].rearrange(
                                "p (hc i) -> p hc i", i=128),
                            in_=wgbf[ex, :, ic * 128:(ic + 1) * 128].rearrange(
                                "(hc p) i -> p hc i", p=128)),
                            waits=[S['wstg'], S['gmm']], inc=S['wldg'])
                        p.op('sync', lambda e: e.nop(), inc=S['wstg'])
                        p.dma('sync', lambda e, ex=ex, ic=ic: e.dma_start(
                            out=wub[ic % 2][:].rearrange(
                                "p (hc i) -> p hc i", i=128),
                            in_=wubf[ex, :, ic * 128:(ic + 1) * 128].rearrange(
                                "(hc p) i -> p hc i", p=128)),
                            waits=[S['wstg'], S['gmm']], inc=S['wldg'])
                        p.op('sync', lambda e: e.nop(), inc=S['wstg'])
                        p.dma('sync', lambda e, ex=ex, ic=ic: e.dma_start(
                            out=dwb[ic][:],
                            in_=wdbf[ex, ic * 128:(ic + 1) * 128, :]),
                            waits=[S['dmm']], inc=S['wldd'])
                        p.op('sync', lambda e: e.nop(), inc=S['wstg'])
                    for tq in range(2):
                        pg = ppGU[0][ic % 2]
                        pu = ppGU[1][ic % 2]
                        for hc in range(8):
                            p.op('tensor', lambda e, ic=ic, tq=tq, hc=hc,
                                 pg=pg: e.matmul(
                                     out=pg[:],
                                     lhsT=wgb[ic % 2][:, hc * 128:(hc + 1) * 128],
                                     rhs=xnfQ[hc][:, tq * 512:(tq + 1) * 512],
                                     start=(hc == 0), stop=(hc == 7)),
                                 waits=([S['wldg'], S['gev'], S['cbe']]
                                        if hc == 0 else ()),
                                 inc=(S['gmm'] if hc == 7 else None))
                        for hc in range(8):
                            p.op('tensor', lambda e, ic=ic, tq=tq, hc=hc,
                                 pu=pu: e.matmul(
                                     out=pu[:],
                                     lhsT=wub[ic % 2][:, hc * 128:(hc + 1) * 128],
                                     rhs=xnfQ[hc][:, tq * 512:(tq + 1) * 512],
                                     start=(hc == 0), stop=(hc == 7)),
                                 waits=(),
                                 inc=(S['gmm'] if hc == 7 else None))
                        p.op('scalar', lambda e, pg=pg: e.activation(
                            out=sgt[:], in_=pg[:], func=AF.Silu),
                            waits=[S['gmm'], S['act']], inc=S['gev'])
                        p.op('vector', lambda e, pu=pu: e.tensor_tensor(
                            out=su[:], in0=pu[:], in1=sgt[:], op=ALU.mult),
                            waits=[S['gev']], inc=S['gev'])
                        p.drain('vector')
                        p.op('vector', lambda e, ic=ic, tq=tq: e.tensor_tensor(
                            out=actF[ic][:, tq * 512:(tq + 1) * 512],
                            in0=su[:], in1=cmbB[:, tq * 512:(tq + 1) * 512],
                            op=ALU.mult),
                            waits=[S['dmm']], inc=S['act'])
                        p.drain('vector')
                for tq in range(2):
                    for ho in range(8):
                        pd = ppD[ho % 2]
                        for ic in range(IC):
                            p.op('tensor', lambda e, tq=tq, ho=ho, ic=ic,
                                 pd=pd: e.matmul(
                                     out=pd[:],
                                     lhsT=dwb[ic][:, ho * 128:(ho + 1) * 128],
                                     rhs=actF[ic][:, tq * 512:(tq + 1) * 512],
                                     start=(ic == 0), stop=(ic == IC - 1)),
                                 waits=([S['act'], S['wldd'], S['dev']]
                                        if ic == 0 else ()),
                                 inc=(S['dmm'] if ic == IC - 1 else None))
                        if ex == 0:
                            p.op('vector', lambda e, tq=tq, ho=ho, pd=pd:
                                 e.tensor_copy(
                                     out=macc[ho][:, tq * 512:(tq + 1) * 512],
                                     in_=pd[:]),
                                 waits=[S['dmm'], S['yt'], S['yb']],
                                 inc=S['dev'])
                        else:
                            p.op('vector', lambda e, tq=tq, ho=ho, pd=pd:
                                 e.tensor_tensor(
                                     out=macc[ho][:, tq * 512:(tq + 1) * 512],
                                     in0=pd[:],
                                     in1=macc[ho][:, tq * 512:(tq + 1) * 512],
                                     op=ALU.add),
                                 waits=[S['dmm']], inc=S['dev'])
                        p.drain('vector')
            p.op('vector', lambda e: e.nop(), waits=[S['dev']], inc=S['ye'])

            for ml in range(2):
                m = q * 2 + ml
                yfv = [macc[o][:, ml * 512:(ml + 1) * 512] for o in range(8)]
                emit_postln_tail(
                    p, S, yfv, row, rc, ppS1, ppS2, ppTr, yb, ysq, ones128b,
                    ident32, xres, scr2, yt,
                    x_src=lambda i, m=m: x[(m * SUB + i) * 128:
                                           (m * SUB + i + 1) * 128, :],
                    out_dst=lambda i, m=m: x4[(m * SUB + i) * 128:
                                              (m * SUB + i + 1) * 128, :],
                    epspp=epspp)

        p.op('sync', lambda e: e.nop(), waits=[S['xo']])

        with nc.Block() as block:
            p.replay(block)
    return nc


_MOE_NC = None


def _get_moe():
    global _MOE_NC
    if _MOE_NC is None:
        _MOE_NC = build_moe()
    return _MOE_NC


def run_moe(x_cores, W):
    nc = _get_moe()
    in_maps = []
    for c in range(NCORE):
        in_maps.append({
            "x": x_cores[c],
            "gw": W["L1_ffn_gate_w"], "wg": W["L1_ffn_w_gate"],
            "wu": W["L1_ffn_w_up"], "wd": W["L1_ffn_w_down"],
            "swg": W["L1_ffn_sw_gate"], "swu": W["L1_ffn_sw_up"],
            "swd": W["L1_ffn_sw_down"], "sgw": W["L1_ffn_shared_gate_w"],
        })
    res = run_bass_kernel_spmd(nc, in_maps, core_ids=list(range(NCORE)))
    return [res.results[c]["x4"] for c in range(NCORE)]



_L0ATTN_NC = None


def _get_l0attn():
    global _L0ATTN_NC
    if _L0ATTN_NC is None:
        _L0ATTN_NC = build_l0attn()
    return _L0ATTN_NC


def run_l0attn(x_cores, W):
    nc = _get_l0attn()
    in_maps = []
    for c in range(NCORE):
        in_maps.append({
            "x": x_cores[c],
            "wqkv": W["L0_attn_W_qkv"], "bqkv": W["L0_attn_b_qkv"],
            "wproj": W["L0_attn_W_proj"], "bproj": W["L0_attn_b_proj"],
            "pm": W["L0_attn_proj_mat"],
        })
    res = run_bass_kernel_spmd(nc, in_maps, core_ids=list(range(NCORE)))
    return [res.results[c]["x1"] for c in range(NCORE)]


def kernel(**inputs):
    W = {k: np.ascontiguousarray(np.asarray(v, dtype=np.float32))
         for k, v in inputs.items()}
    x0 = W.pop("L0_attn_x")
    x_cores = [np.ascontiguousarray(x0[c]) for c in range(NCORE)]
    x1 = run_l0attn(x_cores, W)
    x2 = run_l0ffn(x1, W)
    x3 = run_l1attn(x2, W)
    x4 = run_moe(x3, W)
    return np.stack(x4, axis=0).astype(np.float32)


# version 04
# speedup vs baseline: 1.3640x; 1.3640x over previous
import numpy as np
import ml_dtypes
from contextlib import ExitStack

import concourse.bass as bass
import concourse.mybir as mybir
from concourse.bass_utils import run_bass_kernel_spmd

F32 = mybir.dt.float32
BF16 = mybir.dt.bfloat16
AF = mybir.ActivationFunctionType
ALU = mybir.AluOpType
AX = mybir.AxisListType

T = 4096
D = 1024
NH = 16
HD = 64
NF = 64
MAC = 512
NM = T // MAC
SUB = 4
NCORE = 8
EPS_LN = 1e-5
EPS_PF = 1e-4
PF_NRM = float(NF) ** -0.25
PF_S1 = float(np.sqrt(PF_NRM / 4.6))
PF_PMS = PF_NRM / PF_S1
ATT_S = float(8.0 ** -0.5)
NEXP = 9
QTOK = 1024
NQT = T // QTOK
IC = 11


def _bf(x):
    return np.ascontiguousarray(np.asarray(x)).astype(ml_dtypes.bfloat16)


def np_rope_tables():
    theta = 10000.0 ** (-2.0 * np.arange(HD // 2, dtype=np.float64) / HD)
    theta2 = np.concatenate([theta, theta])
    pos = np.arange(1, T + 1, dtype=np.float64)
    C = theta2[:, None] * pos[None, :]
    cos = np.cos(C) * ATT_S
    sin = np.sin(C) * ATT_S
    sin2 = np.concatenate([-sin[:HD // 2], sin[HD // 2:]], axis=0)
    cos2 = np.concatenate([cos, cos], axis=0).astype(np.float32)
    sin22 = np.concatenate([sin2, sin2], axis=0).astype(np.float32)
    return np.ascontiguousarray(cos2), np.ascontiguousarray(sin22)


def np_trimask():
    k = np.arange(128)[:, None]
    q = np.arange(128)[None, :]
    return _bf((q >= k).astype(np.float32))



class Sem:
    def __init__(self, stack, nc, name):
        self.h = stack.enter_context(nc.semaphore(name))
        self.n = 0
        self.name = name


class Prog:
    def __init__(self):
        self.ops = []

    def op(self, eng, fn, waits=(), inc=None, inc_by=1):
        ws = []
        for s in waits:
            if isinstance(s, tuple):
                s, c = s
                if c > 0:
                    assert c <= s.n, f"future wait {s.name}: {c} > {s.n}"
                    ws.append((s, c))
            elif s.n > 0:
                ws.append((s, s.n))
        self.ops.append((eng, fn, tuple(ws), inc, inc_by))
        if inc is not None:
            inc.n += inc_by

    def dma(self, eng, fn, waits=(), inc=None):
        self.op(eng, fn, waits, inc, inc_by=16)

    def drain(self, eng):
        self.ops.append((eng, lambda e: e.drain(), (), None, 1))

    def replay(self, block):
        def make_body(name):
            def body(eng):
                last = {}
                for (e, fn, waits, inc, inc_by) in self.ops:
                    if e != name:
                        continue
                    for (s, c) in waits:
                        if c > last.get(s.name, -1):
                            eng.wait_ge(s.h, c)
                            last[s.name] = c
                    inst = fn(eng)
                    if inc is not None:
                        inst.then_inc(inc.h, inc_by)
            return body
        block.sync(make_body('sync'))
        block.scalar(make_body('scalar'))
        block.vector(make_body('vector'))
        block.tensor(make_body('tensor'))
        block.gpsimd(make_body('gpsimd'))


def mksems(stack, nc, names):
    return {n: Sem(stack, nc, n) for n in names}



def emit_preln_stats(p, xin, ln, S, xload_sem):
    for i in range(SUB):
        p.op('vector', lambda e, i=i: e.tensor_reduce(
            out=ln['scol'][:, i:i + 1], in_=xin[i][:], axis=AX.X, op=ALU.add),
            waits=[xload_sem])
        p.op('vector', lambda e, i=i: e.tensor_tensor(
            out=ln['scr'][:], in0=xin[i][:], in1=xin[i][:], op=ALU.mult))
        p.drain('vector')
        p.op('vector', lambda e, i=i: e.tensor_reduce(
            out=ln['s2col'][:, i:i + 1], in_=ln['scr'][:], axis=AX.X,
            op=ALU.add))
        p.drain('vector')
    p.op('vector', lambda e: e.tensor_scalar(
        out=ln['mcol'][:], in0=ln['scol'][:], scalar1=1.0 / D, scalar2=None,
        op0=ALU.mult))
    p.op('vector', lambda e: e.tensor_scalar(
        out=ln['e2col'][:], in0=ln['s2col'][:], scalar1=1.0 / D, scalar2=None,
        op0=ALU.mult))
    p.drain('vector')
    p.op('vector', lambda e: e.tensor_tensor(
        out=ln['mmcol'][:], in0=ln['mcol'][:], in1=ln['mcol'][:],
        op=ALU.mult))
    p.drain('vector')
    p.op('vector', lambda e: e.tensor_tensor(
        out=ln['varcol'][:], in0=ln['e2col'][:], in1=ln['mmcol'][:],
        op=ALU.subtract), inc=S['var'])
    p.op('scalar', lambda e: e.activation(
        out=ln['lncol'][:], in_=ln['varcol'][:], func=AF.Ln,
        bias=ln['epspp'][:, 0:1], scale=1.0), waits=[S['var']])
    p.drain('scalar')
    p.op('scalar', lambda e: e.activation(
        out=ln['rstdcol'][:], in_=ln['lncol'][:], func=AF.Exp, scale=-0.5),
        inc=S['rstd'])
    p.op('vector', lambda e: e.scalar_tensor_tensor(
        out=ln['nmrcol'][:], in0=ln['mcol'][:], scalar=-1.0,
        in1=ln['rstdcol'][:], op0=ALU.mult, op1=ALU.mult),
        waits=[S['rstd']], inc=S['nmr'])


def emit_xn_and_transpose(p, xin, xnb, xnf_m, ln, ppT, identb, S):
    ppb = ppT[:].bitcast(BF16)
    for i in range(SUB):
        p.op('scalar', lambda e, i=i: e.activation(
            out=xnb[i % 2][:], in_=xin[i][:], func=AF.Identity,
            scale=ln['rstdcol'][:, i:i + 1], bias=ln['nmrcol'][:, i:i + 1]),
            waits=[S['nmr'], S['tp']], inc=S['xn'])
        for d in range(8):
            p.op('tensor', lambda e, i=i, d=d: e.transpose(
                out=ppb[:, d * 128:(d + 1) * 128],
                in_=xnb[i % 2][:, d * 128:(d + 1) * 128], identity=identb[:]),
                waits=([S['xn'], S['tpe']] if d == 0 else ()),
                inc=(S['tp'] if d == 7 else None))
        for d in range(8):
            p.op('vector', lambda e, i=i, d=d: e.tensor_copy(
                out=xnf_m[d][:, i * 128:(i + 1) * 128],
                in_=ppb[:, d * 128:(d + 1) * 128]),
                waits=([S['tp']] if d == 0 else ()),
                inc=(S['tpe'] if d == 7 else None))


def emit_postln_tail(p, S, yf, row, rc, ppS1, ppS2, ppTr, yb, ysq,
                     ones128b, ident32, xres, scr2, yt,
                     x_src, out_dst, epspp):
    for o in range(8):
        p.op('vector', lambda e, o=o: e.tensor_copy(
            out=yb[o % 2][:], in_=yf[o][:]),
            waits=[S['ye'], S['smm']], inc=S['yb'])
        p.op('scalar', lambda e, o=o: e.activation(
            out=ysq[o % 2][:], in_=yb[o % 2][:], func=AF.Square),
            waits=[S['yb']], inc=S['ysq'])
        p.op('tensor', lambda e, o=o: e.matmul(
            out=ppS1[0:1, :], lhsT=ones128b[:], rhs=yb[o % 2][:],
            start=(o == 0), stop=(o == 7)),
            waits=([S['yb'], S['rowe']] if o == 0 else [S['yb']]))
        p.op('tensor', lambda e, o=o: e.matmul(
            out=ppS2[0:1, :], lhsT=ones128b[:], rhs=ysq[o % 2][:],
            start=(o == 0), stop=(o == 7)),
            waits=[S['ysq']],
            inc=(S['smv'] if o == 7 else S['smm']))
    p.op('vector', lambda e: e.tensor_scalar(
        out=row['m'][:], in0=ppS1[0:1, :], scalar1=1.0 / D, scalar2=None,
        op0=ALU.mult), waits=[S['smv']])
    p.op('vector', lambda e: e.tensor_scalar(
        out=row['e2'][:], in0=ppS2[0:1, :], scalar1=1.0 / D, scalar2=None,
        op0=ALU.mult), inc=S['rowe'])
    p.drain('vector')
    p.op('vector', lambda e: e.tensor_tensor(
        out=row['mm'][:], in0=row['m'][:], in1=row['m'][:], op=ALU.mult))
    p.drain('vector')
    p.op('vector', lambda e: e.tensor_tensor(
        out=row['var'][:], in0=row['e2'][:], in1=row['mm'][:],
        op=ALU.subtract), inc=S['rvar'])
    p.op('scalar', lambda e: e.activation(
        out=row['ln'][:], in_=row['var'][:], func=AF.Ln,
        bias=epspp[0:1, 0:1], scale=1.0), waits=[S['rvar']])
    p.drain('scalar')
    p.op('scalar', lambda e: e.activation(
        out=row['rstd'][:], in_=row['ln'][:], func=AF.Exp, scale=-0.5),
        inc=S['rrstd'])
    p.op('vector', lambda e: e.scalar_tensor_tensor(
        out=row['nmr'][:], in0=row['m'][:], scalar=-1.0, in1=row['rstd'][:],
        op0=ALU.mult, op1=ALU.mult), waits=[S['rrstd']], inc=S['rnmr'])
    for i in range(SUB):
        p.op('tensor', lambda e, i=i: e.transpose(
            out=ppTr[:, i:i + 1], in_=row['rstd'][0:1, i * 128:(i + 1) * 128],
            identity=ident32[0:1, 0:1]),
            waits=([S['rnmr'], S['yte']] if i == 0 else ()))
    for i in range(SUB):
        p.op('tensor', lambda e, i=i: e.transpose(
            out=ppTr[:, 4 + i:5 + i],
            in_=row['nmr'][0:1, i * 128:(i + 1) * 128],
            identity=ident32[0:1, 0:1]),
            inc=(S['rt'] if i == SUB - 1 else None))
    p.op('vector', lambda e: e.tensor_copy(out=rc[:], in_=ppTr[:, 0:8]),
         waits=[S['rt']], inc=S['rte'])
    for o in range(8):
        for i in range(SUB):
            p.op('tensor', lambda e, o=o, i=i: e.transpose(
                out=ppTr[:, i * 128:(i + 1) * 128],
                in_=yf[o][:, i * 128:(i + 1) * 128], identity=ident32[:]),
                waits=([S['rte'], S['yte']] if i == 0 else ()),
                inc=(S['yt'] if i == SUB - 1 else None))
        for i in range(SUB):
            p.op('vector', lambda e, o=o, i=i: e.tensor_copy(
                out=yt[i][:, o * 128:(o + 1) * 128],
                in_=ppTr[:, i * 128:(i + 1) * 128]),
                waits=([S['yt'], S['xo']] if i == 0 else ()),
                inc=(S['yte'] if i == SUB - 1 else None))
    for i in range(SUB):
        p.dma('sync', lambda e, i=i: e.dma_start(
            out=xres[i][:], in_=x_src(i)), waits=[S['ap'], S['xn']],
            inc=S['xr'])
        p.op('vector', lambda e, i=i: e.tensor_scalar(
            out=scr2[0][:], in0=yt[i][:], scalar1=rc[:, i:i + 1],
            scalar2=rc[:, 4 + i:5 + i], op0=ALU.mult, op1=ALU.add),
            waits=[S['yte'], S['rte'], S['xo']])
        p.drain('vector')
        p.op('vector', lambda e, i=i: e.tensor_add(
            out=yt[i][:], in0=scr2[0][:], in1=xres[i][:]),
            waits=[S['xr']], inc=S['ap'])
        p.drain('vector')
        p.dma('sync', lambda e, i=i: e.dma_start(
            out=out_dst(i), in_=yt[i][:]), waits=[S['ap']], inc=S['xo'])


def emit_weight_cast(p, src_ap, dst_ap, wbuf, S, rows, cols):
    n = rows // 128
    for c in range(n):
        p.dma('gpsimd', lambda e, c=c: e.dma_start(
            out=wbuf[c % 2][:, 0:cols], in_=src_ap[c * 128:(c + 1) * 128, :]),
            waits=[S['wst']], inc=S['wld'])
        p.dma('sync', lambda e, c=c: e.dma_start(
            out=dst_ap[c * 128:(c + 1) * 128, :], in_=wbuf[c % 2][:, 0:cols]),
            waits=[S['wld']], inc=S['wst'])


SEMS_ALL = ['wld', 'wst', 'wtp', 'cst', 'xi', 'var', 'rstd', 'nmr', 'xn', 'tp', 'tpe', 'qkv', 'qke', 'sq2', 'bnd', 'aug', 'qp', 'qpr', 'qps', 'kp', 'kmax', 'keb', 'kex', 've', 'kv', 'kve', 'ql', 'oa', 'rc', 'bc', 'rbe', 'at', 'yp', 'ye', 'yb', 'ysq', 'smm', 'smv', 'rowe', 'rvar', 'rrstd', 'rnmr', 'rt', 'rte', 'yt', 'yte', 'ap', 'xr', 'xo', 'sm', 'se', 'md1', 'md2', 'md3', 'gr', 'gb', 'gt', 'gte', 'w2ld', 'xnst', 'kmm', 'kev', 'ksh', 'krp', 'kn', 'knm', 'kdma', 'vev', 'vdma', 'tbl', 'qb', 'kL', 'vL', 'ss', 'px', 'mk', 'pv', 'od', 'gld', 'gtp', 'glg', 'gch', 'sgm', 'sgs', 'ctp', 'cte', 'cbc', 'cbe', 'wldg', 'wstg', 'wldd', 'gmm', 'gev', 'act', 'dmm', 'dev', 'ph_sync', 'ph_scalar', 'ph_vector', 'ph_tensor', 'ph_gpsimd']

ENGS = ('sync', 'scalar', 'vector', 'tensor', 'gpsimd')


def emit_barrier(p, S):
    for eng in ENGS:
        p.op(eng, lambda e: e.nop(), inc=S['ph_' + eng])
    allw = [S['ph_' + g] for g in ENGS]
    for eng in ENGS:
        p.op(eng, lambda e: e.nop(), waits=allw)


def emit_l0attn(nc, st, p, S, PP, io, pref='a_'):
    x = io['x']
    wqkv = io['wqkv']
    bqkv = io['bqkv']
    wproj = io['wproj']
    bproj = io['bproj']
    pm = io['pm']
    x1 = io['x1']

    wqkvbf = nc.dram_tensor(pref + "wqkvbf", [3 * D, D], BF16, kind="Internal").ap()
    wprojbf = nc.dram_tensor(pref + "wprojbf", [D, D], BF16, kind="Internal").ap()
    qps = nc.dram_tensor(pref + "qps", [D, T], BF16, kind="Internal").ap()

    identb_h = nc.inline_tensor(_bf(np.eye(128, dtype=np.float32)), pref + "identb")
    ident32_h = nc.inline_tensor(np.eye(128, dtype=np.float32), pref + "ident32")

    sb = lambda name, shape, dt: nc.alloc_sbuf_tensor(
        pref + name, shape, dt)

    ppT = PP[0]
    ppQ = [PP[1], PP[2]]
    ppA = PP[3]
    ppB = PP[4]
    ppKV = [PP[5], PP[6], PP[7]]
    ppO = [PP[0], PP[1]]
    ppBC = PP[2]
    ppY = [PP[3], PP[4]]
    ppS1, ppS2 = PP[5], PP[6]
    ppTr = PP[7]

    identb = sb("identb_s", [128, 128], BF16)
    ident32 = sb("ident32_s", [128, 128], F32)
    wqkvT = [sb(f"wqkvT{d}", [128, 3 * D], BF16) for d in range(8)]
    wprojT = [wqkvT[v][:, 0:D] for v in range(8)]
    xin = [sb(f"xin{i}", [128, D], F32) for i in range(SUB)]
    xnb = [sb(f"xnb{i}", [128, D], BF16) for i in range(2)]
    xnf = [[sb(f"xnf{m}_{d}", [128, MAC], BF16) for d in range(8)]
           for m in range(2)]
    vsb = [sb(f"vsb{i}", [128, NH * 65], BF16) for i in range(SUB)]
    ktmp = [sb(f"ktmp{i}", [128, D], BF16) for i in range(SUB)]
    kf2 = [sb(f"kf2{i}", [64, MAC], BF16) for i in range(4)]
    qaug = [sb(f"qaug{i}", [65, MAC], BF16) for i in range(4)]
    sqt = [sb(f"sqt{i}", [64, MAC], BF16) for i in range(4)]
    qpr = [sb(f"qpr{o}", [128, MAC], BF16) for o in range(8)]
    pmsT = sb("pmsT", [65, HD], BF16)
    pmTn = sb("pmTn", [NF, HD], BF16)
    pmstage = sb("pmstage", [NF, HD], BF16)
    kvsb = sb("kvsb", [128, NH * 65], BF16)
    bqk = sb("bqk", [128, 16], F32)
    bvb = sb("bvb", [128, D], F32)
    bpj = sb("bpj", [128, 8], F32)
    epspp = sb("epspp", [128, 1], F32)
    cq1 = sb("cq1", [1, 1], F32)
    mkneg = sb("mkneg", [128, 8], F32)
    ebias = sb("ebias", [128, 8], F32)
    ones64 = sb("ones64", [64, 1], BF16)
    ones128b = sb("ones128b", [128, 1], BF16)
    onesrF = sb("onesrF", [1, 64], F32)
    ln = dict(
        scol=sb("scol", [128, SUB], F32), s2col=sb("s2col", [128, SUB], F32),
        mcol=sb("mcol", [128, SUB], F32), e2col=sb("e2col", [128, SUB], F32),
        mmcol=sb("mmcol", [128, SUB], F32),
        varcol=sb("varcol", [128, SUB], F32),
        lncol=sb("lncol", [128, SUB], F32),
        rstdcol=sb("rstdcol", [128, SUB], F32),
        nmrcol=sb("nmrcol", [128, SUB], F32),
        scr=sb("lnscr", [128, D], F32), epspp=epspp)
    rr = [sb(f"rr{i}", [1, MAC], F32) for i in range(2)]
    rb = [sb(f"rb{i}", [64, MAC], F32) for i in range(2)]
    attf = [sb(f"attf{v}", [128, MAC], BF16) for v in range(8)]
    yf = [sb(f"yf{o}", [128, MAC], F32) for o in range(8)]
    yb = [sb(f"ybx{i}", [128, MAC], BF16) for i in range(2)]
    ysq = [sb(f"ysq{i}", [128, MAC], BF16) for i in range(2)]
    yt = [sb(f"yt{i}", [128, D], F32) for i in range(SUB)]
    rc = sb("rc", [128, 8], F32)
    row = dict(m=sb("rowm", [1, MAC], F32), e2=sb("rowe2", [1, MAC], F32),
               mm=sb("rowmm", [1, MAC], F32),
               var=sb("rowvar", [1, MAC], F32),
               rstd=sb("rowrstd", [1, MAC], F32),
               nmr=sb("rownmr", [1, MAC], F32))
    row['ln'] = row['mm']
    xres = xin
    scr2 = [ln["scr"]]
    wbuf = [ktmp[0], ktmp[1]]



    p.dma('sync', lambda e: e.dma_start(out=identb[:], in_=identb_h.ap()[:, :]),
          inc=S['cst'])
    p.dma('sync', lambda e: e.dma_start(out=ident32[:], in_=ident32_h.ap()[:, :]),
          inc=S['cst'])
    p.op('vector', lambda e: e.memset(epspp[:], EPS_LN))
    p.op('vector', lambda e: e.memset(cq1[:], EPS_PF - 0.5 * PF_PMS * PF_PMS))
    p.op('vector', lambda e: e.memset(ones64[:], 1.0))
    p.op('vector', lambda e: e.memset(ones128b[:], 1.0))
    p.op('vector', lambda e: e.memset(onesrF[:], 1.0))
    for i in range(SUB):
        p.op('vector', lambda e, i=i: e.memset(
            vsb[i][:].rearrange("p (h c) -> p h c", c=65)[:, :, 64:65], 1.0))
    p.op('vector', lambda e: e.nop(), inc=S['cst'])

    for c in range(8):
        p.dma('sync', lambda e, c=c: e.dma_start(
            out=bqk[:, c:c + 1], in_=bqkv[c * 128:(c + 1) * 128][:, None]),
            inc=S['cst'])
    for c in range(8):
        p.dma('sync', lambda e, c=c: e.dma_start(
            out=bqk[:, 8 + c:9 + c],
            in_=bqkv[D + c * 128:D + (c + 1) * 128][:, None]),
            inc=S['cst'])
    for c in range(8):
        p.dma('sync', lambda e, c=c: e.dma_start(
            out=bpj[:, c:c + 1], in_=bproj[c * 128:(c + 1) * 128][:, None]),
            inc=S['cst'])
    p.dma('gpsimd', lambda e: e.dma_start(
        out=bvb[:], in_=bqkv[2 * D:3 * D][None, :].broadcast_to([128, D])),
        inc=S['cst'])

    p.dma('gpsimd', lambda e: e.dma_start(out=pmstage[:], in_=pm[:, :]),
          inc=S['wld'])
    p.op('tensor', lambda e: e.transpose(
        out=ppT[:].bitcast(BF16)[0:64, 0:64], in_=pmstage[:],
        identity=identb[0:64, 0:64]),
        waits=[S['wld'], S['cst']], inc=S['wtp'])
    p.op('vector', lambda e: e.tensor_scalar(
        out=pmsT[0:64, :], in0=ppT[:].bitcast(BF16)[0:64, 0:64],
        scalar1=PF_PMS, scalar2=None, op0=ALU.mult), waits=[S['wtp']])
    p.op('vector', lambda e: e.tensor_scalar(
        out=pmTn[:], in0=ppT[:].bitcast(BF16)[0:64, 0:64],
        scalar1=PF_NRM, scalar2=None, op0=ALU.mult), inc=S['qke'])
    p.op('vector', lambda e: e.memset(pmsT[64:65, :], 1.0), inc=S['wtp'])

    emit_weight_cast(p, wqkv, wqkvbf, wbuf, S, 3 * D, D)
    emit_weight_cast(p, wproj, wprojbf, wbuf, S, D, D)
    for d in range(8):
        p.dma('sync', lambda e, d=d: e.dma_start_transpose(
            out=wqkvT[d][:], in_=wqkvbf[0:3 * D, d * 128:(d + 1) * 128]),
            waits=[S['wst']], inc=S['wld'])
    WREADY = S['wld']

    BQA = {}
    PENDA = [None]
    XNA = S['xn'].n
    for m in range(NM):
        mb = m % 2
        for i in range(SUB):
            p.dma('sync', lambda e, m=m, i=i: e.dma_start(
                out=xin[i][:],
                in_=x[(m * SUB + i) * 128:(m * SUB + i + 1) * 128, :]),
                waits=[(S['xn'], XNA + max(0, (m - 1) * 4 + i + 1))],
                inc=S['xi'])
        emit_preln_stats(p, xin, ln, S, S['xi'])
        emit_xn_and_transpose(p, xin, xnb, xnf[mb], ln, ppT, identb, S)

        for oc in range(16):
            qb = ppQ[oc % 2]
            for d in range(8):
                p.op('tensor', lambda e, oc=oc, d=d, qb=qb, mb=mb: e.matmul(
                    out=qb[:], lhsT=wqkvT[d][:, oc * 128:(oc + 1) * 128],
                    rhs=xnf[mb][d][:], start=(d == 0), stop=(d == 7)),
                    waits=(([S['tpe'], WREADY] +
                            ([BQA[oc % 2]] if BQA.get(oc % 2) else []))
                           if d == 0 else ()),
                    inc=(S['qkv'] if d == 7 else None))
            if PENDA[0] is not None:
                PENDA[0]()
                PENDA[0] = None
            if oc < 8:
                for h2 in range(2):
                    b0 = h2 * 64
                    bb = (oc % 2) * 2 + h2
                    p.op('vector', lambda e, oc=oc, h2=h2, qb=qb, b0=b0,
                         bb=bb: e.tensor_scalar(
                             out=qaug[bb][0:64, :],
                             in0=qb[b0:b0 + 64, :],
                             scalar1=bqk[b0:b0 + 64, oc:oc + 1],
                             scalar2=PF_S1, op0=ALU.add, op1=ALU.mult),
                         waits=[S['qkv'], S['qp']], inc=S['qke'])
                    p.drain('vector')
                    p.op('vector', lambda e, bb=bb: e.tensor_tensor(
                        out=sqt[bb][:], in0=qaug[bb][0:64, :],
                        in1=qaug[bb][0:64, :], op=ALU.mult),
                        waits=[S['bnd']], inc=S['sq2'])
                BQA[oc % 2] = (S['qke'], S['qke'].n)

                def qpath(p=p, S=S, oc=oc, m=m):
                    for h2 in range(2):
                        b0 = h2 * 64
                        bb = (oc % 2) * 2 + h2
                        p.op('tensor', lambda e, bb=bb: e.matmul(
                            out=ppA[0:1, :], lhsT=ones64[:],
                            rhs=sqt[bb][:],
                            start=True, stop=True),
                            waits=[S['sq2'], S['aug']], inc=S['bnd'])
                        p.op('scalar', lambda e, bb=bb: e.activation(
                            out=qaug[bb][64:65, :], in_=ppA[0:1, :],
                            func=AF.Identity, scale=-0.5,
                            bias=cq1[0:1, 0:1]),
                            waits=[S['bnd'], S['qp']], inc=S['aug'])
                        p.op('tensor', lambda e, bb=bb: e.matmul(
                            out=ppB[0:64, :], lhsT=pmsT[:], rhs=qaug[bb][:],
                            start=True, stop=True),
                            waits=[S['aug'], S['qpr']], inc=S['qp'])
                        p.op('scalar', lambda e, oc=oc, h2=h2, b0=b0:
                             e.activation(
                                 out=qpr[oc][b0:b0 + 64, :],
                                 in_=ppB[0:64, :],
                                 func=AF.Exp),
                             waits=[S['qp']], inc=S['qpr'])
                    p.dma('scalar', lambda e, m=m, oc=oc: e.dma_start(
                        out=qps[oc * 128:(oc + 1) * 128,
                                m * MAC:(m + 1) * MAC], in_=qpr[oc][:]),
                        waits=[S['qpr']], inc=S['qps'])
                PENDA[0] = qpath
            else:
                ock = oc - 8
                pack = ppA if ock % 2 == 0 else ppB
                for h2 in range(2):
                    b0 = h2 * 64
                    bb = (oc % 2) * 2 + h2
                    p.op('vector', lambda e, oc=oc, h2=h2, qb=qb, b0=b0,
                         bb=bb: e.tensor_scalar(
                             out=kf2[bb][:], in0=qb[b0:b0 + 64, :],
                             scalar1=bqk[b0:b0 + 64, 8 + oc - 8:9 + oc - 8],
                             scalar2=None, op0=ALU.add),
                         waits=[S['qkv'], S['kp']], inc=S['qke'])
                BQA[oc % 2] = (S['qke'], S['qke'].n)

                def kpath(p=p, S=S, oc=oc, ock=ock, pack=pack):
                    for h2 in range(2):
                        for i2 in range(SUB):
                            j = h2 * 4 + i2
                            bb = (oc % 2) * 2 + h2
                            p.op('tensor', lambda e, bb=bb, i2=i2, j=j,
                                 pack=pack: e.matmul(
                                     out=pack[:, j * 64:(j + 1) * 64],
                                     lhsT=kf2[bb][:, i2 * 128:(i2 + 1) * 128],
                                     rhs=pmTn[:], start=True, stop=True),
                                 waits=([S['qke'], S['kex'], S['aug'],
                                         S['qpr']] if j == 0 else ()),
                                 inc=(S['kp'] if j == 7 else None))
                    for j in range(8):
                        p.op('vector', lambda e, j=j, pack=pack:
                             e.tensor_reduce(
                                 out=mkneg[:, j:j + 1],
                                 in_=pack[:, j * 64:(j + 1) * 64],
                                 axis=AX.X, op=ALU.max, negate=True),
                             waits=([S['kp']] if j == 0 else ()))
                    p.drain('vector')
                    p.op('vector', lambda e: e.tensor_scalar(
                        out=ebias[:], in0=mkneg[:], scalar1=EPS_PF,
                        scalar2=None, op0=ALU.add),
                        waits=[S['kex']], inc=S['keb'])
                    for h2 in range(2):
                        for i2 in range(SUB):
                            j = h2 * 4 + i2
                            h = 2 * ock + h2
                            p.op('scalar', lambda e, j=j, h=h, i2=i2,
                                 pack=pack: e.activation(
                                     out=ktmp[i2][:, h * 64:(h + 1) * 64],
                                     in_=pack[:, j * 64:(j + 1) * 64],
                                     func=AF.Exp, bias=ebias[:, j:j + 1]),
                                 waits=[S['keb'], S['kv']],
                                 inc=(S['kex'] if j == 7 else None))
                PENDA[0] = kpath

        if PENDA[0] is not None:
            PENDA[0]()
            PENDA[0] = None
        for i2 in range(SUB):
            for vh in range(2):
                qb = ppQ[vh % 2]
                for d in range(8):
                    p.op('tensor', lambda e, i2=i2, vh=vh, d=d, qb=qb,
                         mb=mb: e.matmul(
                             out=qb[:],
                             lhsT=xnf[mb][d][:, i2 * 128:(i2 + 1) * 128],
                             rhs=wqkvT[d][:, 2 * D + vh * 512:
                                          2 * D + (vh + 1) * 512],
                             start=(d == 0), stop=(d == 7)),
                         waits=(([BQA[vh % 2]] if BQA.get(vh % 2) else [])
                                if d == 0 else ()),
                         inc=(S['qkv'] if d == 7 else None))
                p.op('vector', lambda e, i2=i2, vh=vh, qb=qb: e.tensor_tensor(
                    out=vsb[i2][:].rearrange("p (h c) -> p h c", c=65)
                        [:, vh * 8:(vh + 1) * 8, 0:64],
                    in0=qb[:].rearrange("p (h c) -> p h c", c=64),
                    in1=bvb[:, vh * 512:(vh + 1) * 512].rearrange(
                        "p (h c) -> p h c", c=64),
                    op=ALU.add),
                    waits=[S['qkv'], S['kv']], inc=S['ve'])
                BQA[vh % 2] = (S['ve'], S['ve'].n)

        for h in range(NH):
            bank, off = (0, h) if h < 7 else ((1, h - 7) if h < 14
                                              else (2, h - 14))
            for i2 in range(SUB):
                p.op('tensor', lambda e, h=h, i2=i2, bank=bank, off=off,
                     m=m: e.matmul(
                         out=ppKV[bank][0:64, off * 65:(off + 1) * 65],
                         lhsT=ktmp[i2][:, h * 64:(h + 1) * 64],
                         rhs=vsb[i2][:, h * 65:(h + 1) * 65],
                         start=(m == 0 and i2 == 0 and h in (0, 7, 14)),
                         stop=(m == NM - 1 and i2 == SUB - 1
                               and h in (6, 13, 15))),
                     waits=([S['kex'], S['ve']] if (h == 0 and i2 == 0)
                            else ()),
                     inc=(S['kv'] if (h == NH - 1 and i2 == SUB - 1)
                          else None))

    for v in range(8):
        p.dma('sync', lambda e, v=v: e.dma_start_transpose(
            out=wprojT[v], in_=wprojbf[0:D, v * 128:(v + 1) * 128]),
            waits=[S['wst'], S['qkv']], inc=S['wld'])
    for bank in range(3):
        lo = bank * 7 * 65
        w = 7 * 65 if bank < 2 else 2 * 65
        p.op('vector', lambda e, bank=bank, lo=lo, w=w: e.tensor_copy(
            out=kvsb[0:64, lo:lo + w], in_=ppKV[bank][0:64, 0:w]),
            waits=[S['kv']])
    p.drain('vector')
    p.op('vector', lambda e: e.tensor_copy(
        out=kvsb[64:128, :], in_=kvsb[0:64, :]))
    p.drain('vector')
    p.op('vector', lambda e: e.nop(), inc=S['kve'])

    ATH = []
    PENDB = [None]
    BQB = {}
    for m in range(NM):
        for oc in range(8):
            p.dma('sync', lambda e, m=m, oc=oc: e.dma_start(
                out=qpr[oc][:],
                in_=qps[oc * 128:(oc + 1) * 128, m * MAC:(m + 1) * MAC]),
                waits=[S['kve'], S['oa'], S['qps']], inc=S['ql'])
        for h in range(NH):
            b0 = (h % 2) * 64
            po = ppO[h % 2]
            hu = m * NH + h
            p.op('tensor', lambda e, h=h, b0=b0, po=po: e.matmul(
                out=po[0:65, :],
                lhsT=kvsb[b0:b0 + 64, h * 65:(h + 1) * 65],
                rhs=qpr[h // 2][b0:b0 + 64, :], start=True, stop=True),
                waits=([S['ql'], S['kve']] +
                       ([(S['at'], ATH[hu - 2])] if hu >= 2 else [])),
                inc=S['oa'])
            if PENDB[0] is not None:
                PENDB[0]()
                PENDB[0] = None

            def divpath(p=p, S=S, h=h, b0=b0, po=po):
                p.op('vector', lambda e, h=h, po=po: e.tensor_scalar(
                    out=rr[0][:], in0=po[64:65, :], scalar1=1e-6,
                    scalar2=None, op0=ALU.add), waits=[S['oa'], S['rc']])
                p.drain('vector')
                p.op('vector', lambda e: e.reciprocal(
                    out=rr[1][:], in_=rr[0][:]), waits=[S['bc']],
                    inc=S['rc'])
                p.op('tensor', lambda e: e.matmul(
                    out=ppBC[0:64, :], lhsT=onesrF[:], rhs=rr[1][:],
                    start=True, stop=True),
                    waits=[S['rc'], S['rbe']], inc=S['bc'])
                p.op('scalar', lambda e, h=h: e.activation(
                    out=rb[h % 2][:], in_=ppBC[0:64, :], func=AF.Identity),
                    waits=[S['bc'], S['at']], inc=S['rbe'])
                p.op('vector', lambda e, h=h, b0=b0, po=po: e.tensor_tensor(
                    out=attf[h // 2][b0:b0 + 64, :], in0=po[0:64, :],
                    in1=rb[h % 2][:], op=ALU.mult),
                    waits=[S['rbe']], inc=S['at'])
                ATH.append(S['at'].n)
            PENDB[0] = divpath
        if PENDB[0] is not None:
            PENDB[0]()
            PENDB[0] = None
        for o in range(8):
            py = ppY[o % 2]
            for v in range(8):
                p.op('tensor', lambda e, o=o, v=v, py=py: e.matmul(
                    out=py[:], lhsT=wprojT[v][:, o * 128:(o + 1) * 128],
                    rhs=attf[v][:], start=(v == 0), stop=(v == 7)),
                    waits=(([S['at'], S['wld']] +
                            ([BQB[o % 2]] if BQB.get(o % 2) else []))
                           if v == 0 else ()),
                    inc=(S['yp'] if v == 7 else None))
            p.op('vector', lambda e, o=o, py=py: e.tensor_scalar(
                out=yf[o][:], in0=py[:], scalar1=bpj[:, o:o + 1],
                scalar2=None, op0=ALU.add),
                waits=[S['yp']], inc=S['ye'])
            BQB[o % 2] = (S['ye'], S['ye'].n)
        emit_postln_tail(
            p, S, yf, row, rc, ppS1, ppS2, ppTr, yb, ysq, ones128b,
            ident32, xres, scr2, yt,
            x_src=lambda i, m=m: x[(m * SUB + i) * 128:
                                   (m * SUB + i + 1) * 128, :],
            out_dst=lambda i, m=m: x1[(m * SUB + i) * 128:
                                      (m * SUB + i + 1) * 128, :],
            epspp=epspp)

    p.op('sync', lambda e: e.nop(), waits=[S['xo']])



def emit_l0ffn(nc, st, p, S, PP, io, pref='b_'):
    x = io['x']
    win = io['win']
    binv = io['binv']
    w1 = io['w1']
    b1 = io['b1']
    w2 = io['w2']
    b2 = io['b2']
    wout = io['wout']
    bout = io['bout']
    x2 = io['x2']

    identb_h = nc.inline_tensor(_bf(np.eye(128, dtype=np.float32)), pref + "identb")
    ident32_h = nc.inline_tensor(np.eye(128, dtype=np.float32), pref + "ident32")

    sb = lambda name, shape, dt: nc.alloc_sbuf_tensor(
        pref + name, shape, dt)

    ppT = PP[0]
    ppS = PP[1]
    ppG = PP[2]
    ppY = [PP[3], PP[4]]
    ppS1, ppS2 = PP[5], PP[6]
    ppTr = PP[7]

    identb = sb("identb_s", [128, 128], BF16)
    ident32 = sb("ident32_s", [128, 128], F32)
    xin = [sb(f"xin{i}", [128, D], F32) for i in range(SUB)]
    xnb = [sb(f"xnb{i}", [128, D], BF16) for i in range(2)]
    xnf = [[sb(f"xnf{m}_{d}", [128, MAC], BF16) for d in range(8)]
           for m in range(2)]
    winb = sb("winb", [128, 512], BF16)
    woutb = sb("woutb", [64, D], BF16)
    binb4 = sb("binb4", [128, 256], F32)
    w1b = sb("w1b", [128, 1024], F32)
    b1b = sb("b1b", [128, 1024], F32)
    w2b = sb("w2b", [128, 1024], F32)
    b2b4 = sb("b2b4", [128, 256], F32)
    boc = sb("boc", [128, 8], F32)
    s4 = sb("s4", [128, 256], F32)
    mid1 = [sb(f"mid1{i}", [128, 1024], F32) for i in range(2)]
    mid2 = [sb(f"mid2{i}", [128, 1024], F32) for i in range(2)]
    gf32 = sb("gf32", [128, 256], F32)
    gbf = sb("gbf", [128, 256], BF16)
    gF = sb("gF", [64, MAC], BF16)
    epspp = sb("epspp", [128, 1], F32)
    ones128b = sb("ones128b", [128, 1], BF16)
    ln = dict(
        scol=sb("scol", [128, SUB], F32), s2col=sb("s2col", [128, SUB], F32),
        mcol=sb("mcol", [128, SUB], F32), e2col=sb("e2col", [128, SUB], F32),
        mmcol=sb("mmcol", [128, SUB], F32),
        varcol=sb("varcol", [128, SUB], F32),
        lncol=sb("lncol", [128, SUB], F32),
        rstdcol=sb("rstdcol", [128, SUB], F32),
        nmrcol=sb("nmrcol", [128, SUB], F32),
        scr=sb("lnscr", [128, D], F32), epspp=epspp)
    yf = [sb(f"yf{o}", [128, MAC], F32) for o in range(8)]
    yb = [sb(f"ybx{i}", [128, MAC], BF16) for i in range(2)]
    ysq = [sb(f"ysq{i}", [128, MAC], BF16) for i in range(2)]
    yt = [sb(f"yt{i}", [128, D], F32) for i in range(SUB)]
    rc = sb("rc", [128, 8], F32)
    row = dict(m=sb("rowm", [1, MAC], F32), e2=sb("rowe2", [1, MAC], F32),
               mm=sb("rowmm", [1, MAC], F32),
               var=sb("rowvar", [1, MAC], F32),
               rstd=sb("rowrstd", [1, MAC], F32),
               nmr=sb("rownmr", [1, MAC], F32))
    row['ln'] = row['mm']
    xres = xin
    scr2 = [ln["scr"]]



    p.dma('sync', lambda e: e.dma_start(out=identb[:], in_=identb_h.ap()[:, :]),
          inc=S['cst'])
    p.dma('sync', lambda e: e.dma_start(out=ident32[:], in_=ident32_h.ap()[:, :]),
          inc=S['cst'])
    p.op('vector', lambda e: e.memset(epspp[:], EPS_LN))
    p.op('vector', lambda e: e.memset(ones128b[:], 1.0))
    p.op('vector', lambda e: e.nop(), inc=S['cst'])
    for d in range(8):
        p.dma('gpsimd', lambda e, d=d: e.dma_start(
            out=winb[:, d * 64:(d + 1) * 64],
            in_=win[d * 128:(d + 1) * 128, :]), inc=S['wld'])
    p.dma('gpsimd', lambda e: e.dma_start(out=woutb[:], in_=wout[:, :]),
          inc=S['wld'])
    p.dma('gpsimd', lambda e: e.dma_start(
        out=binb4[:], in_=binv[None, None, :].broadcast_to([128, 4, 64])),
        inc=S['wld'])
    p.dma('gpsimd', lambda e: e.dma_start(
        out=w1b[:], in_=w1.flatten()[None, :].broadcast_to([128, 1024])),
        inc=S['wld'])
    p.dma('gpsimd', lambda e: e.dma_start(
        out=b1b[:], in_=b1.flatten()[None, :].broadcast_to([128, 1024])),
        inc=S['wld'])
    p.dma('gpsimd', lambda e: e.dma_start(
        out=w2b[:], in_=w2.flatten()[None, :].broadcast_to([128, 1024])),
        inc=S['wld'])
    p.dma('gpsimd', lambda e: e.dma_start(
        out=b2b4[:], in_=b2[None, None, :].broadcast_to([128, 4, 64])),
        inc=S['wld'])
    for c in range(8):
        p.dma('sync', lambda e, c=c: e.dma_start(
            out=boc[:, c:c + 1], in_=bout[c * 128:(c + 1) * 128][:, None]),
            inc=S['wld'])

    for m in range(NM):
        mb = m % 2
        for i in range(SUB):
            p.dma('sync', lambda e, m=m, i=i: e.dma_start(
                out=xin[i][:],
                in_=x[(m * SUB + i) * 128:(m * SUB + i + 1) * 128, :]),
                waits=[S['xn'], S['ap']], inc=S['xi'])
        emit_preln_stats(p, xin, ln, S, S['xi'])
        emit_xn_and_transpose(p, xin, xnb, xnf[mb], ln, ppT, identb, S)

        for i2 in range(SUB):
            for d in range(8):
                p.op('tensor', lambda e, i2=i2, d=d, mb=mb: e.matmul(
                    out=ppS[:, i2 * 64:(i2 + 1) * 64],
                    lhsT=xnf[mb][d][:, i2 * 128:(i2 + 1) * 128],
                    rhs=winb[:, d * 64:(d + 1) * 64],
                    start=(i2 == 0 and d == 0), stop=(i2 == 3 and d == 7)),
                    waits=([S['tpe'], S['wld'], S['se']]
                           if (i2 == 0 and d == 0) else ()),
                    inc=(S['sm'] if (i2 == 3 and d == 7) else None))
        p.op('vector', lambda e: e.tensor_add(
            out=s4[:], in0=ppS[:, 0:256], in1=binb4[:]),
            waits=[S['sm'], S['md1']], inc=S['se'])
        for i2 in range(SUB):
            sb_view = s4[:, i2 * 64:(i2 + 1) * 64].rearrange(
                "p (m o) -> p m o", o=1).broadcast_to([128, 64, 16])
            w1v = w1b[:].rearrange("p (m i) -> p m i", i=16)
            p.op('vector', lambda e, i2=i2, sb_view=sb_view, w1v=w1v:
                 e.tensor_tensor(
                     out=mid1[i2 % 2][:].rearrange("p (m i) -> p m i", i=16),
                     in0=sb_view, in1=w1v, op=ALU.mult),
                 waits=[S['se'], S['md2']], inc=S['md1'])
            p.drain('vector')
            p.op('vector', lambda e, i2=i2: e.tensor_add(
                out=mid2[i2 % 2][:], in0=mid1[i2 % 2][:], in1=b1b[:]),
                waits=[S['md3']], inc=S['md1'])
            p.op('scalar', lambda e, i2=i2: e.activation(
                out=mid1[i2 % 2][:], in_=mid2[i2 % 2][:], func=AF.Relu),
                waits=[S['md1']], inc=S['md2'])
            p.op('vector', lambda e, i2=i2: e.tensor_mul(
                out=mid2[i2 % 2][:], in0=mid1[i2 % 2][:], in1=w2b[:]),
                waits=[S['md2']], inc=S['md3'])
            p.drain('vector')
            p.op('vector', lambda e, i2=i2: e.tensor_reduce(
                out=gf32[:, i2 * 64:(i2 + 1) * 64],
                in_=mid2[i2 % 2][:].rearrange("p (m i) -> p m i", i=16),
                axis=AX.X, op=ALU.add), inc=S['gr'])
            p.drain('vector')
        p.op('vector', lambda e: e.tensor_add(
            out=gbf[:], in0=gf32[:], in1=b2b4[:]),
            waits=[S['gr'], S['gt']], inc=S['gb'])
        for i2 in range(SUB):
            p.op('tensor', lambda e, i2=i2: e.transpose(
                out=ppG[:].bitcast(BF16)[0:64, i2 * 128:(i2 + 1) * 128],
                in_=gbf[:, i2 * 64:(i2 + 1) * 64], identity=identb[:]),
                waits=([S['gb'], S['gte']] if i2 == 0 else ()),
                inc=(S['gt'] if i2 == 3 else None))
        p.op('vector', lambda e: e.tensor_copy(
            out=gF[:], in_=ppG[:].bitcast(BF16)[0:64, 0:512]),
            waits=[S['gt']], inc=S['gte'])
        for o in range(8):
            py = ppY[o % 2]
            p.op('tensor', lambda e, o=o, py=py: e.matmul(
                out=py[:], lhsT=woutb[:, o * 128:(o + 1) * 128],
                rhs=gF[:], start=True, stop=True),
                waits=[S['gte'], S['ye']], inc=S['yp'])
            p.op('vector', lambda e, o=o, py=py: e.tensor_scalar(
                out=yf[o][:], in0=py[:], scalar1=boc[:, o:o + 1],
                scalar2=None, op0=ALU.add),
                waits=[S['yp']], inc=S['ye'])
        emit_postln_tail(
            p, S, yf, row, rc, ppS1, ppS2, ppTr, yb, ysq, ones128b,
            ident32, xres, scr2, yt,
            x_src=lambda i, m=m: x[(m * SUB + i) * 128:
                                   (m * SUB + i + 1) * 128, :],
            out_dst=lambda i, m=m: x2[(m * SUB + i) * 128:
                                      (m * SUB + i + 1) * 128, :],
            epspp=epspp)

    p.op('sync', lambda e: e.nop(), waits=[S['xo']])



def emit_l1attn(nc, st, p, S, PP, io, pref='c_'):
    x = io['x']
    wq = io['wq']
    wk = io['wk']
    wv = io['wv']
    wo = io['wo']
    bq = io['bq']
    bk = io['bk']
    bv = io['bv']
    bo = io['bo']
    rcos = io['rcos']
    rsin = io['rsin']
    x3 = io['x3']

    wbf = [nc.dram_tensor(pref + f"w{n}bf", [D, D], BF16, kind="Internal").ap()
        for n in ("q", "k", "v", "o")]
    xn2 = nc.dram_tensor(pref + "xn2", [D, T], BF16, kind="Internal").ap()
    ks_h = nc.dram_tensor(pref + "ksh", [NH, 65, T], BF16, kind="Internal").ap()
    vs_h = nc.dram_tensor(pref + "vsh", [T, NH * 65], BF16, kind="Internal").ap()

    identb_h = nc.inline_tensor(_bf(np.eye(128, dtype=np.float32)), pref + "identb")
    ident32_h = nc.inline_tensor(np.eye(128, dtype=np.float32), pref + "ident32")
    trim_h = nc.inline_tensor(np_trimask(), pref + "trimask")

    sb = lambda name, shape, dt: nc.alloc_sbuf_tensor(
        pref + name, shape, dt)

    ppT = PP[7]
    ppQ = [PP[0], PP[1]]
    ppA = PP[2]
    ppS_ = [PP[3], PP[4]]
    ppO = [PP[5], PP[6]]
    ppS1, ppS2 = PP[3], PP[4]
    ppTr = PP[7]

    identb = sb("identb_s", [128, 128], BF16)
    ident32 = sb("ident32_s", [128, 128], F32)
    trim = sb("trim_s", [128, 128], BF16)
    wT = [[sb(f"wT{g}_{d}", [128, D], BF16) for d in range(8)]
          for g in range(2)]
    xin = [sb(f"xin{i}", [128, D], F32) for i in range(SUB)]
    xnb = [sb(f"xnb{i}", [128, D], BF16) for i in range(2)]
    xnf = [[sb(f"xnf{m}_{d}", [128, MAC], BF16) for d in range(8)]
           for m in range(2)]
    kst = [sb(f"kst{h}", [65, MAC], BF16) for h in range(NH)]
    vst = [sb(f"vst{i}", [128, NH * 65], BF16) for i in range(SUB)]
    kro = sb("kro", [128, MAC], F32)
    ksh2 = sb("ksh2", [128, MAC], F32)
    scrA = sb("scrA", [128, MAC], F32)
    scrB = sb("scrB", [128, MAC], F32)
    scrC = sb("scrC", [128, MAC], F32)
    cosS = sb("cosS", [128, MAC], F32)
    sinS = sb("sinS", [128, MAC], F32)
    kc2 = sb("kc2", [1, 2], F32)
    nkc2 = sb("nkc2", [1, 1], F32)
    bcol = sb("bcol", [128, 32], F32)
    bvb = sb("bvb", [128, D], F32)
    kl = [sb(f"kl{i}", [65, MAC], BF16) for i in range(4)]
    vl = [sb(f"vl{i}", [128, 4, 130], BF16) for i in range(2)]
    pt = [sb(f"pt{i}", [128, MAC], BF16) for i in range(2)]
    ptm = sb("ptm", [128, 128], BF16)
    attfL = [sb(f"attf{v}", [128, MAC], BF16) for v in range(8)]
    rr = [sb(f"rr{i}", [1, MAC], F32) for i in range(2)]
    rb = [sb(f"rb{i}", [64, MAC], F32) for i in range(2)]
    epspp = sb("epspp", [128, 1], F32)
    ones64 = sb("ones64", [64, 1], BF16)
    ones64F = sb("ones64F", [128, 1], F32)
    ones128b = sb("ones128b", [128, 1], BF16)
    onesrF = sb("onesrF", [1, 64], F32)
    ln = dict(
        scol=sb("scol", [128, SUB], F32), s2col=sb("s2col", [128, SUB], F32),
        mcol=sb("mcol", [128, SUB], F32), e2col=sb("e2col", [128, SUB], F32),
        mmcol=sb("mmcol", [128, SUB], F32),
        varcol=sb("varcol", [128, SUB], F32),
        lncol=sb("lncol", [128, SUB], F32),
        rstdcol=sb("rstdcol", [128, SUB], F32),
        nmrcol=sb("nmrcol", [128, SUB], F32),
        scr=sb("lnscr", [128, D], F32), epspp=epspp)
    yf = [sb(f"yf{o}", [128, MAC], F32) for o in range(8)]
    yb = [sb(f"ybx{i}", [128, MAC], BF16) for i in range(2)]
    ysq = [sb(f"ysq{i}", [128, MAC], BF16) for i in range(2)]
    yt = [sb(f"yt{i}", [128, D], F32) for i in range(SUB)]
    rc = sb("rc", [128, 8], F32)
    row = dict(m=sb("rowm", [1, MAC], F32), e2=sb("rowe2", [1, MAC], F32),
               mm=sb("rowmm", [1, MAC], F32),
               var=sb("rowvar", [1, MAC], F32),
               rstd=sb("rowrstd", [1, MAC], F32),
               nmr=sb("rownmr", [1, MAC], F32))
    row['ln'] = row['mm']
    xres = xin
    scr2 = [ln["scr"]]



    p.dma('sync', lambda e: e.dma_start(out=identb[:], in_=identb_h.ap()[:, :]),
          inc=S['cst'])
    p.dma('sync', lambda e: e.dma_start(out=ident32[:], in_=ident32_h.ap()[:, :]),
          inc=S['cst'])
    p.dma('sync', lambda e: e.dma_start(out=trim[:], in_=trim_h.ap()[:, :]),
          inc=S['cst'])
    p.op('vector', lambda e: e.memset(epspp[:], EPS_LN))
    p.op('vector', lambda e: e.memset(ones64[:], 1.0))
    p.op('vector', lambda e: e.memset(ones64F[:], 1.0))
    p.op('vector', lambda e: e.memset(ones128b[:], 1.0))
    p.op('vector', lambda e: e.memset(onesrF[:], 1.0))
    p.op('vector', lambda e: e.memset(kc2[:], 0.0))
    for h in range(NH):
        p.op('vector', lambda e, h=h: e.memset(kst[h][64:65, :], 1.0))
    for i in range(SUB):
        p.op('vector', lambda e, i=i: e.memset(
            vst[i][:].rearrange("p (h c) -> p h c", c=65)[:, :, 64:65], 1.0))
    p.op('vector', lambda e: e.nop(), inc=S['cst'])
    for gi, bsrc in enumerate((bq, bk, bv, bo)):
        for c in range(8):
            p.dma('sync', lambda e, gi=gi, c=c, bsrc=bsrc: e.dma_start(
                out=bcol[:, gi * 8 + c:gi * 8 + c + 1],
                in_=bsrc[c * 128:(c + 1) * 128][:, None]), inc=S['cst'])
    p.dma('gpsimd', lambda e: e.dma_start(
        out=bvb[:], in_=bv[None, :].broadcast_to([128, D])), inc=S['cst'])
    wstage = [yb[0], yb[1]]
    wstage = [xnb[0], xnb[1]]
    for gi, wsrc in enumerate((wq, wk, wv, wo)):
        for c in range(8):
            p.dma('gpsimd', lambda e, c=c, wsrc=wsrc: e.dma_start(
                out=wstage[c % 2][:], in_=wsrc[c * 128:(c + 1) * 128, :]),
                waits=[S['wst']], inc=S['wld'])
            p.dma('sync', lambda e, gi=gi, c=c: e.dma_start(
                out=wbf[gi][c * 128:(c + 1) * 128, :],
                in_=wstage[c % 2][:]),
                waits=[S['wld']], inc=S['wst'])
    for d in range(8):
        p.dma('sync', lambda e, d=d: e.dma_start_transpose(
            out=wT[0][d][:], in_=wbf[1][0:D, d * 128:(d + 1) * 128]),
            waits=[S['wst']], inc=S['w2ld'])
        p.dma('sync', lambda e, d=d: e.dma_start_transpose(
            out=wT[1][d][:], in_=wbf[2][0:D, d * 128:(d + 1) * 128]),
            waits=[S['wst']], inc=S['w2ld'])

    BQ = {}
    PEND = [None]
    KNH = [0]
    XN0 = S['xn'].n
    for m in range(NM):
        mb = m % 2
        for i in range(SUB):
            p.dma('sync', lambda e, m=m, i=i: e.dma_start(
                out=xin[i][:],
                in_=x[(m * SUB + i) * 128:(m * SUB + i + 1) * 128, :]),
                waits=[(S['xn'], XN0 + max(0, (m - 1) * 4 + i + 1)),
                       S['wld']], inc=S['xi'])
        emit_preln_stats(p, xin, ln, S, S['xi'])
        emit_xn_and_transpose(p, xin, xnb, xnf[mb], ln, ppT, identb, S)
        for d in range(8):
            p.dma('scalar', lambda e, m=m, d=d, mb=mb: e.dma_start(
                out=xn2[d * 128:(d + 1) * 128, m * MAC:(m + 1) * MAC],
                in_=xnf[mb][d][:]),
                waits=[S['tpe']], inc=S['xnst'])
        p.dma('sync', lambda e, m=m: e.dma_start(
            out=cosS[:], in_=rcos[:, m * MAC:(m + 1) * MAC]),
            waits=[S['krp']], inc=S['tbl'])
        p.dma('sync', lambda e, m=m: e.dma_start(
            out=sinS[:], in_=rsin[:, m * MAC:(m + 1) * MAC]),
            waits=[S['krp']], inc=S['tbl'])
        for oc in range(8):
            qb_ = ppQ[oc % 2]
            for d in range(8):
                p.op('tensor', lambda e, oc=oc, d=d, qb_=qb_, mb=mb:
                     e.matmul(
                         out=qb_[:],
                         lhsT=wT[0][d][:, oc * 128:(oc + 1) * 128],
                         rhs=xnf[mb][d][:], start=(d == 0), stop=(d == 7)),
                     waits=(([S['tpe'], S['w2ld']] +
                             ([BQ[oc % 2]] if BQ.get(oc % 2) else []))
                            if d == 0 else ()),
                     inc=(S['kmm'] if d == 7 else None))
            if PEND[0] is not None:
                PEND[0]()
                PEND[0] = None
            p.op('vector', lambda e, oc=oc, qb_=qb_: e.tensor_scalar(
                out=kro[:], in0=qb_[:], scalar1=bcol[:, 8 + oc:9 + oc],
                scalar2=None, op0=ALU.add),
                waits=[S['kmm'], S['krp']], inc=S['kev'])
            BQ[oc % 2] = (S['kev'], S['kev'].n)
            p.drain('vector')
            p.op('vector', lambda e: e.tensor_mul(
                out=scrC[:], in0=kro[:], in1=kro[:]),
                waits=[(S['kn'], KNH[0])], inc=S['ksh'])

            def knorm_path(p=p, S=S, scrC=scrC, ppA=ppA, kc2=kc2,
                           ones64F=ones64F, KNH=KNH):
                for h2 in range(2):
                    p.op('tensor', lambda e, h2=h2: e.matmul(
                        out=ppA[0:1, :],
                        lhsT=ones64F[h2 * 64:(h2 + 1) * 64, :],
                        rhs=scrC[h2 * 64:(h2 + 1) * 64, :],
                        start=True, stop=True),
                        waits=[S['ksh'], S['knm']], inc=S['kn'])
                    p.op('vector', lambda e, h2=h2: e.tensor_reduce(
                        out=kc2[0:1, 1:2], in_=ppA[0:1, :], axis=AX.X,
                        op=ALU.max), waits=[S['kn']])
                    p.drain('vector')
                    p.op('vector', lambda e: e.tensor_max(
                        out=kc2[0:1, 0:1], in0=kc2[0:1, 0:1],
                        in1=kc2[0:1, 1:2]), inc=S['knm'])
                    p.drain('vector')
                KNH[0] = S['kn'].n
            PEND[0] = knorm_path
            p.op('vector', lambda e: e.tensor_copy(
                out=ksh2[0:32, :], in_=kro[32:64, :]))
            p.op('vector', lambda e: e.tensor_copy(
                out=ksh2[32:64, :], in_=kro[0:32, :]))
            p.op('vector', lambda e: e.tensor_copy(
                out=ksh2[64:96, :], in_=kro[96:128, :]))
            p.op('vector', lambda e: e.tensor_copy(
                out=ksh2[96:128, :], in_=kro[64:96, :]))
            p.drain('vector')
            p.op('vector', lambda e: e.tensor_mul(
                out=scrA[:], in0=kro[:], in1=cosS[:]), waits=[S['tbl']])
            p.op('vector', lambda e: e.tensor_mul(
                out=scrB[:], in0=ksh2[:], in1=sinS[:]))
            p.drain('vector')
            for h2 in range(2):
                p.op('vector', lambda e, oc=oc, h2=h2: e.tensor_add(
                    out=kst[2 * oc + h2][0:64, :],
                    in0=scrA[h2 * 64:(h2 + 1) * 64, :],
                    in1=scrB[h2 * 64:(h2 + 1) * 64, :]),
                    waits=[S['kdma']],
                    inc=(S['krp'] if h2 == 1 else None))
            p.drain('vector')
            for h2 in range(2):
                p.dma('scalar', lambda e, m=m, oc=oc, h2=h2: e.dma_start(
                    out=ks_h[2 * oc + h2, :, m * MAC:(m + 1) * MAC],
                    in_=kst[2 * oc + h2][:]),
                    waits=[S['krp']], inc=S['kdma'])
        for i2 in range(SUB):
            for vh in range(2):
                qb_ = ppQ[vh % 2]
                for d in range(8):
                    p.op('tensor', lambda e, i2=i2, vh=vh, d=d, qb_=qb_,
                         mb=mb: e.matmul(
                             out=qb_[:],
                             lhsT=xnf[mb][d][:, i2 * 128:(i2 + 1) * 128],
                             rhs=wT[1][d][:, vh * 512:(vh + 1) * 512],
                             start=(d == 0), stop=(d == 7)),
                         waits=(([BQ[vh % 2]] if BQ.get(vh % 2) else [])
                                if d == 0 else ()),
                         inc=(S['kmm'] if d == 7 else None))
                p.op('vector', lambda e, i2=i2, vh=vh, qb_=qb_:
                     e.tensor_tensor(
                         out=vst[i2][:].rearrange("p (h c) -> p h c", c=65)
                             [:, vh * 8:(vh + 1) * 8, 0:64],
                         in0=qb_[:].rearrange("p (h c) -> p h c", c=64),
                         in1=bvb[:, vh * 512:(vh + 1) * 512].rearrange(
                             "p (h c) -> p h c", c=64),
                         op=ALU.add),
                     waits=[S['kmm'], S['vdma']], inc=S['vev'])
                BQ[vh % 2] = (S['vev'], S['vev'].n)
            p.dma('scalar', lambda e, m=m, i2=i2: e.dma_start(
                out=vs_h[(m * SUB + i2) * 128:(m * SUB + i2 + 1) * 128, :],
                in_=vst[i2][:]),
                waits=[S['vev']], inc=S['vdma'])

    if PEND[0] is not None:
        PEND[0]()
        PEND[0] = None
    p.op('vector', lambda e: e.tensor_scalar(
        out=nkc2[:], in0=kc2[0:1, 0:1],
        scalar1=-0.5 * ATT_S * ATT_S, scalar2=None, op0=ALU.mult),
        waits=[S['knm']], inc=S['knm'])

    for d in range(8):
        p.dma('sync', lambda e, d=d: e.dma_start_transpose(
            out=wT[0][d][:], in_=wbf[0][0:D, d * 128:(d + 1) * 128]),
            waits=[S['kmm']], inc=S['w2ld'])
        p.dma('sync', lambda e, d=d: e.dma_start_transpose(
            out=wT[1][d][:], in_=wbf[3][0:D, d * 128:(d + 1) * 128]),
            waits=[S['kmm']], inc=S['w2ld'])

    ptrot = [0]
    FL = dict(kl={}, vl={}, sb={}, pt={}, ptm=0, knm=0)
    for qi in range(NM):
        for d in range(8):
            p.dma('sync', lambda e, qi=qi, d=d: e.dma_start(
                out=xnf[0][d][:],
                in_=xn2[d * 128:(d + 1) * 128, qi * MAC:(qi + 1) * MAC]),
                waits=[S['kmm'], S['xnst']], inc=S['xi'])
        p.dma('sync', lambda e, qi=qi: e.dma_start(
            out=cosS[:], in_=rcos[:, qi * MAC:(qi + 1) * MAC]),
            waits=[S['krp']], inc=S['tbl'])
        p.dma('sync', lambda e, qi=qi: e.dma_start(
            out=sinS[:], in_=rsin[:, qi * MAC:(qi + 1) * MAC]),
            waits=[S['krp']], inc=S['tbl'])
        for oc in range(8):
            qb_ = ppQ[oc % 2]
            for d in range(8):
                p.op('tensor', lambda e, oc=oc, d=d, qb_=qb_: e.matmul(
                    out=qb_[:], lhsT=wT[0][d][:, oc * 128:(oc + 1) * 128],
                    rhs=xnf[0][d][:], start=(d == 0), stop=(d == 7)),
                    waits=(([S['xi'], S['w2ld']] +
                            ([BQ[oc % 2]] if BQ.get(oc % 2) else []))
                           if d == 0 else ()),
                    inc=(S['kmm'] if d == 7 else None))
            if PEND[0] is not None:
                PEND[0]()
                PEND[0] = None
            p.op('vector', lambda e, oc=oc, qb_=qb_: e.tensor_scalar(
                out=kro[:], in0=qb_[:], scalar1=bcol[:, oc:oc + 1],
                scalar2=None, op0=ALU.add),
                waits=[S['kmm'], S['krp']], inc=S['kev'])
            BQ[oc % 2] = (S['kev'], S['kev'].n)
            p.drain('vector')
            p.op('vector', lambda e: e.tensor_mul(
                out=scrC[:], in0=kro[:], in1=kro[:]),
                waits=[(S['kn'], KNH[0])], inc=S['ksh'])

            def qbound_path(p=p, S=S, scrC=scrC, ppA=ppA, kst=kst,
                            nkc2=nkc2, ones64F=ones64F, KNH=KNH, oc=oc):
                for h2 in range(2):
                    p.op('tensor', lambda e, h2=h2: e.matmul(
                        out=ppA[0:1, :],
                        lhsT=ones64F[h2 * 64:(h2 + 1) * 64, :],
                        rhs=scrC[h2 * 64:(h2 + 1) * 64, :],
                        start=True, stop=True),
                        waits=[S['ksh'], S['knm']], inc=S['kn'])
                    p.op('scalar', lambda e, oc=oc, h2=h2: e.activation(
                        out=kst[2 * oc + h2][64:65, :], in_=ppA[0:1, :],
                        func=AF.Identity, scale=-0.5 * ATT_S * ATT_S,
                        bias=nkc2[0:1, 0:1]),
                        waits=[S['kn'], S['ss'], S['kdma']], inc=S['knm'])
                KNH[0] = S['kn'].n
            PEND[0] = qbound_path
            p.op('vector', lambda e: e.tensor_copy(
                out=ksh2[0:32, :], in_=kro[32:64, :]))
            p.op('vector', lambda e: e.tensor_copy(
                out=ksh2[32:64, :], in_=kro[0:32, :]))
            p.op('vector', lambda e: e.tensor_copy(
                out=ksh2[64:96, :], in_=kro[96:128, :]))
            p.op('vector', lambda e: e.tensor_copy(
                out=ksh2[96:128, :], in_=kro[64:96, :]))
            p.drain('vector')
            p.op('vector', lambda e: e.tensor_mul(
                out=scrA[:], in0=kro[:], in1=cosS[:]), waits=[S['tbl']])
            p.op('vector', lambda e: e.tensor_mul(
                out=scrB[:], in0=ksh2[:], in1=sinS[:]))
            p.drain('vector')
            for h2 in range(2):
                p.op('vector', lambda e, oc=oc, h2=h2: e.tensor_add(
                    out=kst[2 * oc + h2][0:64, :],
                    in0=scrA[h2 * 64:(h2 + 1) * 64, :],
                    in1=scrB[h2 * 64:(h2 + 1) * 64, :]),
                    waits=[S['ss'], S['kdma']],
                    inc=(S['krp'] if h2 == 1 else None))
            p.drain('vector')
        if PEND[0] is not None:
            PEND[0]()
            PEND[0] = None
        p.op('vector', lambda e: e.nop(), inc=S['qb'])
        FL['knm'] = S['knm'].n

        for pr in range(8):
            first_pv = [True, True]
            for ki in range(qi + 1):
                kb = ki % 2
                diag = (ki == qi)
                for hh in range(2):
                    h = 2 * pr + hh
                    klw = [(S['ss'], FL['kl'].get(hh * 2 + kb, 0)),
                           S['qb']]
                    p.dma('sync', lambda e, h=h, ki=ki, kb=kb, hh=hh:
                          e.dma_start(
                              out=kl[hh * 2 + kb][:],
                              in_=ks_h[h, :, ki * MAC:(ki + 1) * MAC]),
                          waits=klw, inc=S['kL'])
                p.dma('sync', lambda e, pr=pr, ki=ki, kb=kb: e.dma_start(
                    out=vl[kb][:],
                    in_=vs_h[ki * MAC:(ki + 1) * MAC,
                             2 * pr * 65:2 * pr * 65 + 130].rearrange(
                                 "(s p) c -> p s c", p=128)),
                    waits=[(S['pv'], FL['vl'].get(kb, 0))], inc=S['vL'])
                for hh in range(2):
                    h = 2 * pr + hh
                    nks = 4
                    for ks in range(nks):
                        if diag:
                            nq = MAC - ks * 128
                            qoff = ks * 128
                        else:
                            nq = MAC
                            qoff = 0
                        rot = ptrot[0] % 2
                        ptrot[0] += 1
                        sbank = ppS_[ks % 2]
                        p.op('tensor', lambda e, h=h, ks=ks, kb=kb,
                             hh=hh, nq=nq, qoff=qoff, sbank=sbank:
                             e.matmul(
                                 out=sbank[0:128, 0:nq],
                                 lhsT=kl[hh * 2 + kb][:, ks * 128:
                                                      (ks + 1) * 128],
                                 rhs=kst[h][:, qoff:MAC],
                                 start=True, stop=True),
                             waits=[S['kL'], S['qb'],
                                    (S['px'], FL['sb'].get(ks % 2, 0)),
                                    (S['knm'], FL['knm'])],
                             inc=S['ss'])
                        FL['kl'][hh * 2 + kb] = S['ss'].n
                        p.op('scalar', lambda e, rot=rot, nq=nq,
                             sbank=sbank: e.activation(
                                 out=pt[rot][0:128, 0:nq],
                                 in_=sbank[0:128, 0:nq], func=AF.Exp),
                             waits=[S['ss'],
                                    (S['pv'], FL['pt'].get(rot, 0))],
                             inc=S['px'])
                        FL['sb'][ks % 2] = S['px'].n
                        if diag:
                            p.op('vector', lambda e, rot=rot:
                                 e.tensor_mul(
                                     out=ptm[:], in0=pt[rot][:, 0:128],
                                     in1=trim[:]),
                                 waits=[S['px'],
                                        (S['pv'], FL['ptm'])],
                                 inc=S['mk'])
                        qs_list = range(ks, 4) if diag else range(4)
                        for qs in qs_list:
                            dblk = diag and (qs == ks)
                            rhs_ap = (ptm[:] if dblk else
                                      pt[rot][:, (qs * 128 - qoff):
                                              (qs * 128 - qoff) + 128])
                            w = [S['mk'] if dblk else S['px'], S['vL']]
                            if first_pv[hh]:
                                w.append(S['at'])
                            p.op('tensor', lambda e, hh=hh, ks=ks, kb=kb,
                                 qs=qs, rhs_ap=rhs_ap, h=h,
                                 fp=first_pv[hh]: e.matmul(
                                     out=ppO[hh][0:65,
                                                 qs * 128:(qs + 1) * 128],
                                     lhsT=vl[kb][:, ks,
                                                 hh * 65:(hh + 1) * 65],
                                     rhs=rhs_ap,
                                     start=fp,
                                     stop=(ki == qi and ks == 3 and
                                           qs == 3)),
                                 waits=w, inc=S['pv'])
                            first_pv[hh] = False
                            if dblk:
                                FL['ptm'] = S['pv'].n
                        FL['pt'][rot] = S['pv'].n
                FL['vl'][kb] = S['pv'].n
            p.op('tensor', lambda e: e.nop(), waits=[S['pv']],
                 inc=S['od'])
            for hh in range(2):
                p.op('vector', lambda e, hh=hh: e.reciprocal(
                    out=rr[hh][:], in_=ppO[hh][64:65, :]),
                    waits=[S['od'], S['bc']], inc=S['rbe'])
                p.drain('vector')
            for hh in range(2):
                p.op('tensor', lambda e, hh=hh: e.matmul(
                    out=(ppA[0:64, :] if hh == 0 else ppTr[0:64, :]),
                    lhsT=onesrF[:], rhs=rr[hh][:],
                    start=True, stop=True),
                    waits=[S['rbe'], S['knm'], S['at'], S['rte'],
                           S['yte']],
                    inc=S['bc'])
            for hh in range(2):
                p.op('scalar', lambda e, hh=hh: e.activation(
                    out=rb[hh][:],
                    in_=(ppA[0:64, :] if hh == 0 else ppTr[0:64, :]),
                    func=AF.Identity),
                    waits=[S['bc']], inc=S['rbe'])
            for hh in range(2):
                p.op('vector', lambda e, pr=pr, hh=hh: e.tensor_tensor(
                    out=attfL[pr][hh * 64:(hh + 1) * 64, :],
                    in0=ppO[hh][0:64, :], in1=rb[hh][:], op=ALU.mult),
                    waits=[S['rbe'], S['yp']], inc=S['at'])
        for o in range(8):
            py = ppQ[o % 2]
            for v in range(8):
                p.op('tensor', lambda e, o=o, v=v, py=py: e.matmul(
                    out=py[:], lhsT=wT[1][v][:, o * 128:(o + 1) * 128],
                    rhs=attfL[v][:], start=(v == 0), stop=(v == 7)),
                    waits=(([S['at']] +
                            ([BQ[o % 2]] if BQ.get(o % 2) else []))
                           if v == 0 else ()),
                    inc=(S['yp'] if v == 7 else None))
            p.op('vector', lambda e, o=o, py=py: e.tensor_scalar(
                out=yf[o][:], in0=py[:], scalar1=bcol[:, 24 + o:25 + o],
                scalar2=None, op0=ALU.add),
                waits=[S['yp']], inc=S['ye'])
            BQ[o % 2] = (S['ye'], S['ye'].n)
        emit_postln_tail(
            p, S, yf, row, rc, ppS1, ppS2, ppTr, yb, ysq, ones128b,
            ident32, xres, scr2, yt,
            x_src=lambda i, qi=qi: x[(qi * SUB + i) * 128:
                                     (qi * SUB + i + 1) * 128, :],
            out_dst=lambda i, qi=qi: x3[(qi * SUB + i) * 128:
                                        (qi * SUB + i + 1) * 128, :],
            epspp=epspp)

    p.op('sync', lambda e: e.nop(), waits=[S['xo']])



def emit_moe(nc, st, p, S, PP, io, pref='d_'):
    x = io['x']
    gw = io['gw']
    wg = io['wg']
    wu = io['wu']
    wd = io['wd']
    swg = io['swg']
    swu = io['swu']
    swd = io['swd']
    sgw = io['sgw']
    x4 = io['x4']

    wgbf = nc.dram_tensor(pref + "wgbf", [NEXP, D, 1408], BF16, kind="Internal").ap()
    wubf = nc.dram_tensor(pref + "wubf", [NEXP, D, 1408], BF16, kind="Internal").ap()
    wdbf = nc.dram_tensor(pref + "wdbf", [NEXP, 1408, D], BF16, kind="Internal").ap()

    identb_h = nc.inline_tensor(_bf(np.eye(128, dtype=np.float32)), pref + "identb")
    ident32_h = nc.inline_tensor(np.eye(128, dtype=np.float32), pref + "ident32")

    sb = lambda name, shape, dt: nc.alloc_sbuf_tensor(
        pref + name, shape, dt)

    ppT = PP[7]
    ppTr = PP[7]
    ppGU = [[PP[0], PP[1]], [PP[2], PP[3]]]
    ppD = [PP[4], PP[5]]
    ppX = PP[6]
    ppS1, ppS2 = PP[0], PP[1]

    identb = sb("identb_s", [128, 128], BF16)
    ident32 = sb("ident32_s", [128, 128], F32)
    xin = [sb(f"xin{i}", [128, D], F32) for i in range(SUB)]
    xnb = [sb(f"xnb{i}", [128, D], BF16) for i in range(2)]
    xnfQ = [sb(f"xnfq{d}", [128, QTOK], BF16) for d in range(8)]
    macc = [sb(f"macc{o}", [128, QTOK], F32) for o in range(8)]
    actF = [sb(f"actf{i}", [128, QTOK], BF16) for i in range(IC)]
    dwb = [sb(f"dwb{i}", [128, D], BF16) for i in range(IC)]
    wgb = [sb(f"wgb{i}", [128, D], BF16) for i in range(2)]
    wub = [sb(f"wub{i}", [128, D], BF16) for i in range(2)]
    cmbB = sb("cmbB", [128, QTOK], F32)
    cmball = sb("cmball", [128, 64], F32)
    cmbrow = sb("cmbrow", [1, QTOK], F32)
    sgrow = sb("sgrow", [1, QTOK], F32)
    gwT = sb("gwT", [128, 64], BF16)
    sgwT = sb("sgwT", [128, 8], BF16)
    gstage = sb("gstage", [8, D], BF16)
    sgt = [sb(f"sgt{i}", [128, 512], F32) for i in range(2)]
    su = [sb(f"su{i}", [128, 512], F32) for i in range(2)]
    lg0 = sb("lg0", [128, 8], F32)
    el = sb("el", [128, 8], F32)
    mskA = sb("mskA", [128, 8], F32)
    lg1 = sb("lg1", [128, 8], F32)
    nm1 = sb("nm1", [128, 1], F32)
    m2s = sb("m2s", [128, 1], F32)
    dsum = sb("dsum", [128, 1], F32)
    recs = sb("recs", [128, 1], F32)
    epspp = sb("epspp", [128, 1], F32)
    ones128b = sb("ones128b", [128, 1], BF16)
    onesr128F = sb("onesr128F", [1, 128], F32)
    ln = dict(
        scol=sb("scol", [128, SUB], F32), s2col=sb("s2col", [128, SUB], F32),
        mcol=sb("mcol", [128, SUB], F32), e2col=sb("e2col", [128, SUB], F32),
        mmcol=sb("mmcol", [128, SUB], F32),
        varcol=sb("varcol", [128, SUB], F32),
        lncol=sb("lncol", [128, SUB], F32),
        rstdcol=sb("rstdcol", [128, SUB], F32),
        nmrcol=sb("nmrcol", [128, SUB], F32),
        scr=sb("lnscr", [128, D], F32), epspp=epspp)
    yb = [sb(f"ybx{i}", [128, MAC], BF16) for i in range(2)]
    ysq = [sb(f"ysq{i}", [128, MAC], BF16) for i in range(2)]
    yt = [sb(f"yt{i}", [128, D], F32) for i in range(SUB)]
    rc = sb("rc", [128, 8], F32)
    row = dict(m=sb("rowm", [1, MAC], F32), e2=sb("rowe2", [1, MAC], F32),
               mm=sb("rowmm", [1, MAC], F32),
               var=sb("rowvar", [1, MAC], F32),
               rstd=sb("rowrstd", [1, MAC], F32),
               nmr=sb("rownmr", [1, MAC], F32))
    row['ln'] = row['mm']
    xres = xin
    scr2 = [ln["scr"]]



    p.dma('sync', lambda e: e.dma_start(out=identb[:], in_=identb_h.ap()[:, :]),
          inc=S['cst'])
    p.dma('sync', lambda e: e.dma_start(out=ident32[:], in_=ident32_h.ap()[:, :]),
          inc=S['cst'])
    p.op('vector', lambda e: e.memset(epspp[:], EPS_LN))
    p.op('vector', lambda e: e.memset(ones128b[:], 1.0))
    p.op('vector', lambda e: e.memset(onesr128F[:], 1.0))
    p.op('vector', lambda e: e.nop(), inc=S['cst'])
    p.dma('gpsimd', lambda e: e.dma_start(out=gstage[0:8, :], in_=gw[:, :]),
          inc=S['gld'])
    for d in range(8):
        p.op('tensor', lambda e, d=d: e.transpose(
            out=ppX[:].bitcast(BF16)[0:128, d * 8:(d + 1) * 8],
            in_=gstage[0:8, d * 128:(d + 1) * 128],
            identity=identb[0:8, 0:8]),
            waits=([S['gld'], S['cst']] if d == 0 else ()),
            inc=(S['gtp'] if d == 7 else None))
    p.op('vector', lambda e: e.tensor_copy(
        out=gwT[:], in_=ppX[:].bitcast(BF16)[0:128, 0:64]),
        waits=[S['gtp']], inc=S['gte'])
    p.dma('gpsimd', lambda e: e.dma_start(out=gstage[0:1, :], in_=sgw[:, :]),
          waits=[S['gte']], inc=S['gld'])
    for d in range(8):
        p.op('tensor', lambda e, d=d: e.transpose(
            out=ppX[:].bitcast(BF16)[0:128, 2 * d:2 * d + 1],
            in_=gstage[0:1, d * 128:(d + 1) * 128],
            identity=identb[0:1, 0:1]),
            waits=([S['gld'], S['gte']] if d == 0 else ()),
            inc=(S['gtp'] if d == 7 else None))
    p.op('vector', lambda e: e.tensor_copy(
        out=sgwT[:],
        in_=ppX[:].bitcast(BF16)[0:128, 0:16].rearrange(
            "p (a b) -> p a b", b=2)[:, :, 0:1].rearrange(
                "p a b -> p (a b)")),
        waits=[S['gtp']], inc=S['gte'])

    HX = dict(uw=0, u2=0, v2=0, gmmh=[], gmmgh=[], wstgh=[], gevh=[],
              acth=[], devh=[])

    for q in range(NQT):
        for ml in range(2):
            m = q * 2 + ml
            for i in range(SUB):
                p.dma('sync', lambda e, m=m, i=i: e.dma_start(
                    out=xin[i][:],
                    in_=x[(m * SUB + i) * 128:(m * SUB + i + 1) * 128, :]),
                    waits=[S['xn'], S['ap'], S['gmm']], inc=S['xi'])
            emit_preln_stats(p, xin, ln, S, S['xi'])
            ppb = ppT[:].bitcast(BF16)
            for i in range(SUB):
                p.op('scalar', lambda e, i=i: e.activation(
                    out=xnb[i % 2][:], in_=xin[i][:], func=AF.Identity,
                    scale=ln['rstdcol'][:, i:i + 1],
                    bias=ln['nmrcol'][:, i:i + 1]),
                    waits=[S['nmr'], S['tp']], inc=S['xn'])
                for d in range(8):
                    p.op('tensor', lambda e, i=i, d=d, ppb=ppb: e.transpose(
                        out=ppb[:, d * 128:(d + 1) * 128],
                        in_=xnb[i % 2][:, d * 128:(d + 1) * 128],
                        identity=identb[:]),
                        waits=([S['xn'], S['tpe']] if d == 0 else ()),
                        inc=(S['tp'] if d == 7 else None))
                for d in range(8):
                    p.op('vector', lambda e, i=i, d=d, ml=ml, ppb=ppb:
                         e.tensor_copy(
                             out=xnfQ[d][:, ml * 512 + i * 128:
                                         ml * 512 + (i + 1) * 128],
                             in_=ppb[:, d * 128:(d + 1) * 128]),
                         waits=([S['tp'], S['gmm']] if d == 0 else ()),
                         inc=(S['tpe'] if d == 7 else None))
        for s in range(8):
            for d in range(8):
                p.op('tensor', lambda e, s=s, d=d: e.matmul(
                    out=ppX[0:128, s * 8:(s + 1) * 8],
                    lhsT=xnfQ[d][:, s * 128:(s + 1) * 128],
                    rhs=gwT[:, d * 8:(d + 1) * 8],
                    start=(d == 0), stop=(d == 7)),
                    waits=([S['tpe'], S['gte'], S['cte'], S['gch']]
                           if d == 0 else ()),
                    inc=(S['glg'] if d == 7 else None))
            p.op('vector', lambda e, s=s: e.tensor_reduce(
                out=nm1[:], in_=ppX[0:128, s * 8:(s + 1) * 8],
                axis=AX.X, op=ALU.max, negate=True), waits=[S['glg']])
            p.drain('vector')
            p.op('vector', lambda e, s=s: e.tensor_scalar(
                out=lg0[:], in0=ppX[0:128, s * 8:(s + 1) * 8],
                scalar1=nm1[:, 0:1], scalar2=None, op0=ALU.add),
                inc=S['gch'])
            p.drain('vector')
            p.op('scalar', lambda e: e.activation(
                out=el[:], in_=lg0[:], func=AF.Exp),
                waits=[S['gch']], inc=S['sgm'])
            p.op('vector', lambda e: e.tensor_scalar(
                out=mskA[:], in0=lg0[:], scalar1=-1e-30, scalar2=None,
                op0=ALU.is_ge))
            p.drain('vector')
            p.op('vector', lambda e: e.scalar_tensor_tensor(
                out=lg1[:], in0=mskA[:], scalar=-1e30, in1=lg0[:],
                op0=ALU.mult, op1=ALU.add))
            p.drain('vector')
            p.op('vector', lambda e: e.tensor_reduce(
                out=m2s[:], in_=lg1[:], axis=AX.X, op=ALU.max))
            p.drain('vector')
            p.op('vector', lambda e: e.tensor_scalar(
                out=lg1[:], in0=lg0[:], scalar1=m2s[:, 0:1], scalar2=None,
                op0=ALU.is_ge))
            p.drain('vector')
            p.op('vector', lambda e: e.tensor_tensor(
                out=el[:], in0=el[:], in1=lg1[:], op=ALU.mult),
                waits=[S['sgm']])
            p.drain('vector')
            p.op('vector', lambda e: e.tensor_reduce(
                out=dsum[:], in_=el[:], axis=AX.X, op=ALU.add))
            p.drain('vector')
            p.op('vector', lambda e: e.reciprocal(
                out=recs[:], in_=dsum[:]))
            p.drain('vector')
            p.op('vector', lambda e, s=s: e.tensor_scalar(
                out=cmball[:, s * 8:(s + 1) * 8], in0=el[:],
                scalar1=recs[:, 0:1], scalar2=None, op0=ALU.mult),
                inc=S['gch'])
            p.drain('vector')
        for tq in range(2):
            for d in range(8):
                p.op('tensor', lambda e, tq=tq, d=d: e.matmul(
                    out=ppX[0:1, 0:512],
                    lhsT=sgwT[:, d:d + 1],
                    rhs=xnfQ[d][:, tq * 512:(tq + 1) * 512],
                    start=(d == 0), stop=(d == 7)),
                    waits=([S['gch'], S['sgs']] if d == 0 else ()),
                    inc=(S['glg'] if d == 7 else None))
            p.op('scalar', lambda e, tq=tq: e.activation(
                out=sgrow[0:1, tq * 512:(tq + 1) * 512],
                in_=ppX[0:1, 0:512], func=AF.Sigmoid),
                waits=[S['glg']], inc=S['sgs'])

        for ex in range(NEXP):
            if ex < 8:
                for s in range(8):
                    p.op('tensor', lambda e, ex=ex, s=s: e.transpose(
                        out=ppX[:].bitcast(F32)[0:1,
                                                (s % 4) * 128:
                                                (s % 4) * 128 + 128],
                        in_=cmball[:, s * 8 + ex:s * 8 + ex + 1],
                        identity=ident32[:]),
                        waits=([S['gch'], S['sgs'], S['cte']]
                               if s % 4 == 0 else ()),
                        inc=(S['ctp'] if s % 4 == 3 else None))
                    if s % 4 == 3:
                        half = s // 4
                        p.op('vector', lambda e, half=half: e.tensor_copy(
                            out=cmbrow[0:1, half * 512:(half + 1) * 512],
                            in_=ppX[:].bitcast(F32)[0:1, 0:512]),
                            waits=[S['ctp']], inc=S['cte'])
            else:
                p.op('vector', lambda e: e.tensor_copy(
                    out=cmbrow[:], in_=sgrow[:]),
                    waits=[S['sgs'], S['cbc']], inc=S['cte'])
            for half in range(2):
                p.op('tensor', lambda e, half=half: e.matmul(
                    out=ppX[0:128, 0:512], lhsT=onesr128F[:],
                    rhs=cmbrow[0:1, half * 512:(half + 1) * 512],
                    start=True, stop=True),
                    waits=[S['cte'], S['cbe'], S['act']], inc=S['cbc'])
                p.op('vector', lambda e, half=half: e.tensor_copy(
                    out=cmbB[:, half * 512:(half + 1) * 512],
                    in_=ppX[0:128, 0:512]),
                    waits=[S['cbc']], inc=S['cbe'])
            for ic in range(IC):
                uw = HX['uw']
                wwait = ([(S['gmm'], HX['gmmh'][uw - 2]),
                          (S['wstg'], HX['wstgh'][uw - 2])] if uw >= 2 else [])
                if q == 0:
                    p.dma('gpsimd', lambda e, ex=ex, ic=ic, uw=uw: e.dma_start(
                        out=wgb[uw % 2][:].rearrange(
                            "p (hc i) -> p hc i", i=128),
                        in_=(wg[ex, :, ic * 128:(ic + 1) * 128]
                             if ex < 8 else
                             swg[:, ic * 128:(ic + 1) * 128]).rearrange(
                                 "(hc p) i -> p hc i", p=128)),
                        waits=wwait, inc=S['wldg'])
                    p.dma('sync', lambda e, ex=ex, ic=ic, uw=uw: e.dma_start(
                        out=wgbf[ex, :, ic * 128:(ic + 1) * 128].rearrange(
                            "(hc p) i -> p hc i", p=128),
                        in_=wgb[uw % 2][:].rearrange(
                            "p (hc i) -> p hc i", i=128)),
                        waits=[S['wldg']], inc=S['wstg'])
                    p.dma('gpsimd', lambda e, ex=ex, ic=ic, uw=uw: e.dma_start(
                        out=wub[uw % 2][:].rearrange(
                            "p (hc i) -> p hc i", i=128),
                        in_=(wu[ex, :, ic * 128:(ic + 1) * 128]
                             if ex < 8 else
                             swu[:, ic * 128:(ic + 1) * 128]).rearrange(
                                 "(hc p) i -> p hc i", p=128)),
                        waits=wwait, inc=S['wldg'])
                    p.dma('sync', lambda e, ex=ex, ic=ic, uw=uw: e.dma_start(
                        out=wubf[ex, :, ic * 128:(ic + 1) * 128].rearrange(
                            "(hc p) i -> p hc i", p=128),
                        in_=wub[uw % 2][:].rearrange(
                            "p (hc i) -> p hc i", i=128)),
                        waits=[S['wldg']], inc=S['wstg'])
                    p.dma('gpsimd', lambda e, ex=ex, ic=ic: e.dma_start(
                        out=dwb[ic][:],
                        in_=(wd[ex, ic * 128:(ic + 1) * 128, :]
                             if ex < 8 else
                             swd[ic * 128:(ic + 1) * 128, :])),
                        waits=[S['dmm']], inc=S['wldd'])
                    p.dma('sync', lambda e, ex=ex, ic=ic: e.dma_start(
                        out=wdbf[ex, ic * 128:(ic + 1) * 128, :],
                        in_=dwb[ic][:]),
                        waits=[S['wldd']], inc=S['wstg'])
                else:
                    p.dma('sync', lambda e, ex=ex, ic=ic, uw=uw: e.dma_start(
                        out=wgb[uw % 2][:].rearrange(
                            "p (hc i) -> p hc i", i=128),
                        in_=wgbf[ex, :, ic * 128:(ic + 1) * 128].rearrange(
                            "(hc p) i -> p hc i", p=128)),
                        waits=wwait, inc=S['wldg'])
                    p.op('sync', lambda e: e.nop(), inc=S['wstg'])
                    p.dma('sync', lambda e, ex=ex, ic=ic, uw=uw: e.dma_start(
                        out=wub[uw % 2][:].rearrange(
                            "p (hc i) -> p hc i", i=128),
                        in_=wubf[ex, :, ic * 128:(ic + 1) * 128].rearrange(
                            "(hc p) i -> p hc i", p=128)),
                        waits=wwait, inc=S['wldg'])
                    p.op('sync', lambda e: e.nop(), inc=S['wstg'])
                    p.dma('sync', lambda e, ex=ex, ic=ic: e.dma_start(
                        out=dwb[ic][:],
                        in_=wdbf[ex, ic * 128:(ic + 1) * 128, :]),
                        waits=[S['dmm']], inc=S['wldd'])
                    p.op('sync', lambda e: e.nop(), inc=S['wstg'])
                HX['wstgh'].append(S['wstg'].n)
                for tq in range(2):
                    u2 = HX['u2']
                    pg = ppGU[0][u2 % 2]
                    pu = ppGU[1][u2 % 2]
                    guwait = ([(S['gev'], HX['gevh'][u2 - 2])]
                              if u2 >= 2 else [])
                    for hc in range(8):
                        p.op('tensor', lambda e, ic=ic, tq=tq, hc=hc, uw=uw,
                             pg=pg: e.matmul(
                                 out=pg[:],
                                 lhsT=wgb[uw % 2][:, hc * 128:(hc + 1) * 128],
                                 rhs=xnfQ[hc][:, tq * 512:(tq + 1) * 512],
                                 start=(hc == 0), stop=(hc == 7)),
                             waits=(([S['wldg'], S['cbe']] + guwait)
                                    if hc == 0 else ()),
                             inc=(S['gmm'] if hc == 7 else None))
                    HX['gmmgh'].append(S['gmm'].n)
                    for hc in range(8):
                        p.op('tensor', lambda e, ic=ic, tq=tq, hc=hc, uw=uw,
                             pu=pu: e.matmul(
                                 out=pu[:],
                                 lhsT=wub[uw % 2][:, hc * 128:(hc + 1) * 128],
                                 rhs=xnfQ[hc][:, tq * 512:(tq + 1) * 512],
                                 start=(hc == 0), stop=(hc == 7)),
                             waits=(),
                             inc=(S['gmm'] if hc == 7 else None))
                    p.op('scalar', lambda e, pg=pg, u2=u2: e.activation(
                        out=sgt[u2 % 2][:], in_=pg[:], func=AF.Silu),
                        waits=([(S['gmm'], HX['gmmgh'][-1])] + guwait),
                        inc=S['gev'])
                    p.op('vector', lambda e, pu=pu, u2=u2: e.tensor_tensor(
                        out=su[u2 % 2][:], in0=pu[:], in1=sgt[u2 % 2][:],
                        op=ALU.mult),
                        waits=([S['gev'], S['gmm']] +
                               ([(S['act'], HX['acth'][u2 - 2])]
                                if u2 >= 2 else [])),
                        inc=S['gev'])
                    HX['gevh'].append(S['gev'].n)
                    p.drain('vector')
                    p.op('vector', lambda e, ic=ic, tq=tq, u2=u2:
                         e.tensor_tensor(
                             out=actF[ic][:, tq * 512:(tq + 1) * 512],
                             in0=su[u2 % 2][:],
                             in1=cmbB[:, tq * 512:(tq + 1) * 512],
                             op=ALU.mult),
                         waits=[S['dmm']], inc=S['act'])
                    HX['acth'].append(S['act'].n)
                    p.drain('vector')
                    HX['u2'] += 1
                HX['gmmh'].append(S['gmm'].n)
                HX['uw'] += 1
            for tq in range(2):
                for ho in range(8):
                    v2 = HX['v2']
                    pd = ppD[v2 % 2]
                    dwait = ([(S['dev'], HX['devh'][v2 - 2])]
                             if v2 >= 2 else [])
                    for ic in range(IC):
                        p.op('tensor', lambda e, tq=tq, ho=ho, ic=ic,
                             pd=pd: e.matmul(
                                 out=pd[:],
                                 lhsT=dwb[ic][:, ho * 128:(ho + 1) * 128],
                                 rhs=actF[ic][:, tq * 512:(tq + 1) * 512],
                                 start=(ic == 0), stop=(ic == IC - 1)),
                             waits=(([S['act'], S['wldd']] + dwait)
                                    if ic == 0 else ()),
                             inc=(S['dmm'] if ic == IC - 1 else None))
                    if ex == 0:
                        p.op('vector', lambda e, tq=tq, ho=ho, pd=pd:
                             e.tensor_copy(
                                 out=macc[ho][:, tq * 512:(tq + 1) * 512],
                                 in_=pd[:]),
                             waits=[S['dmm'], S['yt'], S['yb']],
                             inc=S['dev'])
                    else:
                        p.op('vector', lambda e, tq=tq, ho=ho, pd=pd:
                             e.tensor_tensor(
                                 out=macc[ho][:, tq * 512:(tq + 1) * 512],
                                 in0=pd[:],
                                 in1=macc[ho][:, tq * 512:(tq + 1) * 512],
                                 op=ALU.add),
                             waits=[S['dmm']], inc=S['dev'])
                    HX['devh'].append(S['dev'].n)
                    p.drain('vector')
                    HX['v2'] += 1
        p.op('vector', lambda e: e.nop(), waits=[S['dev']], inc=S['ye'])

        for ml in range(2):
            m = q * 2 + ml
            yfv = [macc[o][:, ml * 512:(ml + 1) * 512] for o in range(8)]
            emit_postln_tail(
                p, S, yfv, row, rc, ppS1, ppS2, ppTr, yb, ysq, ones128b,
                ident32, xres, scr2, yt,
                x_src=lambda i, m=m: x[(m * SUB + i) * 128:
                                       (m * SUB + i + 1) * 128, :],
                out_dst=lambda i, m=m: x4[(m * SUB + i) * 128:
                                          (m * SUB + i + 1) * 128, :],
                epspp=epspp)

    p.op('sync', lambda e: e.nop(), waits=[S['xo']])




def build_fused():
    nc = bass.Bass("TRN2", target_bir_lowering=False)
    g = lambda nm, shape: nc.dram_tensor(nm, shape, F32,
                                         kind="ExternalInput").ap()
    x = g("x", [T, D])
    io1 = dict(x=x, wqkv=g("wqkv", [3 * D, D]), bqkv=g("bqkv", [3 * D]),
               wproj=g("wproj", [D, D]), bproj=g("bproj", [D]),
               pm=g("pm", [NF, HD]))
    io2 = dict(win=g("win", [D, 64]), binv=g("binv", [64]),
               w1=g("w1", [64, 16]), b1=g("b1", [64, 16]),
               w2=g("w2", [64, 16]), b2=g("b2", [64]),
               wout=g("wout", [64, D]), bout=g("bout", [D]))
    io3 = dict(wq=g("wq", [D, D]), wk=g("wk", [D, D]), wv=g("wv", [D, D]),
               wo=g("wo", [D, D]), bq=g("bq", [D]), bk=g("bk", [D]),
               bv=g("bv", [D]), bo=g("bo", [D]),
               rcos=g("rcos", [128, T]), rsin=g("rsin", [128, T]))
    io4 = dict(gw=g("gw", [8, D]), wg=g("wg", [8, D, 1408]),
               wu=g("wu", [8, D, 1408]), wd=g("wd", [8, 1408, D]),
               swg=g("swg", [D, 1408]), swu=g("swu", [D, 1408]),
               swd=g("swd", [1408, D]), sgw=g("sgw", [1, D]))
    x4out = nc.dram_tensor("out", [T, D], F32, kind="ExternalOutput").ap()
    X1 = nc.dram_tensor("X1", [T, D], F32, kind="Internal").ap()
    X2 = nc.dram_tensor("X2", [T, D], F32, kind="Internal").ap()
    X3 = nc.dram_tensor("X3", [T, D], F32, kind="Internal").ap()
    io1.update(x1=X1)
    io2.update(x=X1, x2=X2)
    io3.update(x=X2, x3=X3)
    io4.update(x=X3, x4=x4out)

    st = ExitStack()
    with st:
        PP = [st.enter_context(nc.psum_tensor(f"pp{i}", [128, 512], F32))
              for i in range(8)]
        S = mksems(st, nc, SEMS_ALL)
        p = Prog()
        mark = nc.sbuf_base
        emit_l0attn(nc, st, p, S, PP, io1, pref="a_")
        emit_barrier(p, S)
        nc.sbuf_base = mark
        emit_l0ffn(nc, st, p, S, PP, io2, pref="b_")
        emit_barrier(p, S)
        nc.sbuf_base = mark
        emit_l1attn(nc, st, p, S, PP, io3, pref="c_")
        emit_barrier(p, S)
        nc.sbuf_base = mark
        emit_moe(nc, st, p, S, PP, io4, pref="d_")
        with nc.Block() as block:
            p.replay(block)
    return nc


_FUSED_NC = None


def _get_fused():
    global _FUSED_NC
    if _FUSED_NC is None:
        _FUSED_NC = build_fused()
    return _FUSED_NC


def kernel(**inputs):
    W = {k: np.ascontiguousarray(np.asarray(v, dtype=np.float32))
         for k, v in inputs.items()}
    x0 = W.pop("L0_attn_x")
    rcos, rsin = np_rope_tables()
    nc = _get_fused()
    in_maps = []
    for c in range(NCORE):
        in_maps.append({
            "x": np.ascontiguousarray(x0[c]),
            "wqkv": W["L0_attn_W_qkv"], "bqkv": W["L0_attn_b_qkv"],
            "wproj": W["L0_attn_W_proj"], "bproj": W["L0_attn_b_proj"],
            "pm": W["L0_attn_proj_mat"],
            "win": W["L0_ffn_Win"], "binv": W["L0_ffn_bin_"],
            "w1": W["L0_ffn_W1"], "b1": W["L0_ffn_b1"],
            "w2": W["L0_ffn_W2"], "b2": W["L0_ffn_b2"],
            "wout": W["L0_ffn_Wout"], "bout": W["L0_ffn_bout"],
            "wq": W["L1_attn_Wq"], "bq": W["L1_attn_bq"],
            "wk": W["L1_attn_Wk"], "bk": W["L1_attn_bk"],
            "wv": W["L1_attn_Wv"], "bv": W["L1_attn_bv"],
            "wo": W["L1_attn_Wo"], "bo": W["L1_attn_bo"],
            "rcos": rcos, "rsin": rsin,
            "gw": W["L1_ffn_gate_w"], "wg": W["L1_ffn_w_gate"],
            "wu": W["L1_ffn_w_up"], "wd": W["L1_ffn_w_down"],
            "swg": W["L1_ffn_sw_gate"], "swu": W["L1_ffn_sw_up"],
            "swd": W["L1_ffn_sw_down"], "sgw": W["L1_ffn_shared_gate_w"],
        })
    res = run_bass_kernel_spmd(nc, in_maps, core_ids=list(range(NCORE)))
    return np.stack([res.results[c]["out"] for c in range(NCORE)],
                    axis=0).astype(np.float32)


# version 05 (final)
# speedup vs baseline: 1.3652x; 1.0009x over previous
import numpy as np
import ml_dtypes
from contextlib import ExitStack

import concourse.bass as bass
import concourse.mybir as mybir
from concourse.bass_utils import run_bass_kernel_spmd

F32 = mybir.dt.float32
BF16 = mybir.dt.bfloat16
AF = mybir.ActivationFunctionType
ALU = mybir.AluOpType
AX = mybir.AxisListType

T = 4096
D = 1024
NH = 16
HD = 64
NF = 64
MAC = 512
NM = T // MAC
SUB = 4
NCORE = 8
EPS_LN = 1e-5
EPS_PF = 1e-4
PF_NRM = float(NF) ** -0.25
PF_S1 = float(np.sqrt(PF_NRM / 4.6))
PF_PMS = PF_NRM / PF_S1
ATT_S = float(8.0 ** -0.5)
NEXP = 9
QTOK = 1024
NQT = T // QTOK
IC = 11


def _bf(x):
    return np.ascontiguousarray(np.asarray(x)).astype(ml_dtypes.bfloat16)


def np_rope_tables():
    theta = 10000.0 ** (-2.0 * np.arange(HD // 2, dtype=np.float64) / HD)
    theta2 = np.concatenate([theta, theta])
    pos = np.arange(1, T + 1, dtype=np.float64)
    C = theta2[:, None] * pos[None, :]
    cos = np.cos(C) * ATT_S
    sin = np.sin(C) * ATT_S
    sin2 = np.concatenate([-sin[:HD // 2], sin[HD // 2:]], axis=0)
    cos2 = np.concatenate([cos, cos], axis=0).astype(np.float32)
    sin22 = np.concatenate([sin2, sin2], axis=0).astype(np.float32)
    return np.ascontiguousarray(cos2), np.ascontiguousarray(sin22)


def np_trimask():
    k = np.arange(128)[:, None]
    q = np.arange(128)[None, :]
    return _bf((q >= k).astype(np.float32))



class Sem:
    def __init__(self, stack, nc, name):
        self.h = stack.enter_context(nc.semaphore(name))
        self.n = 0
        self.name = name


class Prog:
    def __init__(self):
        self.ops = []

    def op(self, eng, fn, waits=(), inc=None, inc_by=1):
        ws = []
        for s in waits:
            if isinstance(s, tuple):
                s, c = s
                if c > 0:
                    assert c <= s.n, f"future wait {s.name}: {c} > {s.n}"
                    ws.append((s, c))
            elif s.n > 0:
                ws.append((s, s.n))
        self.ops.append((eng, fn, tuple(ws), inc, inc_by))
        if inc is not None:
            inc.n += inc_by

    def dma(self, eng, fn, waits=(), inc=None):
        self.op(eng, fn, waits, inc, inc_by=16)

    def drain(self, eng):
        self.ops.append((eng, lambda e: e.drain(), (), None, 1))

    def replay(self, block):
        def make_body(name):
            def body(eng):
                last = {}
                for (e, fn, waits, inc, inc_by) in self.ops:
                    if e != name:
                        continue
                    for (s, c) in waits:
                        if c > last.get(s.name, -1):
                            eng.wait_ge(s.h, c)
                            last[s.name] = c
                    inst = fn(eng)
                    if inc is not None:
                        inst.then_inc(inc.h, inc_by)
            return body
        block.sync(make_body('sync'))
        block.scalar(make_body('scalar'))
        block.vector(make_body('vector'))
        block.tensor(make_body('tensor'))
        block.gpsimd(make_body('gpsimd'))


def mksems(stack, nc, names):
    return {n: Sem(stack, nc, n) for n in names}



def emit_preln_stats(p, xin, ln, S, xload_sem):
    for i in range(SUB):
        p.op('vector', lambda e, i=i: e.tensor_reduce(
            out=ln['scol'][:, i:i + 1], in_=xin[i][:], axis=AX.X, op=ALU.add),
            waits=[xload_sem])
        p.op('vector', lambda e, i=i: e.tensor_tensor(
            out=ln['scr'][:], in0=xin[i][:], in1=xin[i][:], op=ALU.mult))
        p.drain('vector')
        p.op('vector', lambda e, i=i: e.tensor_reduce(
            out=ln['s2col'][:, i:i + 1], in_=ln['scr'][:], axis=AX.X,
            op=ALU.add))
        p.drain('vector')
    p.op('vector', lambda e: e.tensor_scalar(
        out=ln['mcol'][:], in0=ln['scol'][:], scalar1=1.0 / D, scalar2=None,
        op0=ALU.mult))
    p.op('vector', lambda e: e.tensor_scalar(
        out=ln['e2col'][:], in0=ln['s2col'][:], scalar1=1.0 / D, scalar2=None,
        op0=ALU.mult))
    p.drain('vector')
    p.op('vector', lambda e: e.tensor_tensor(
        out=ln['mmcol'][:], in0=ln['mcol'][:], in1=ln['mcol'][:],
        op=ALU.mult))
    p.drain('vector')
    p.op('vector', lambda e: e.tensor_tensor(
        out=ln['varcol'][:], in0=ln['e2col'][:], in1=ln['mmcol'][:],
        op=ALU.subtract), inc=S['var'])
    p.op('scalar', lambda e: e.activation(
        out=ln['lncol'][:], in_=ln['varcol'][:], func=AF.Ln,
        bias=ln['epspp'][:, 0:1], scale=1.0), waits=[S['var']])
    p.drain('scalar')
    p.op('scalar', lambda e: e.activation(
        out=ln['rstdcol'][:], in_=ln['lncol'][:], func=AF.Exp, scale=-0.5),
        inc=S['rstd'])
    p.op('vector', lambda e: e.scalar_tensor_tensor(
        out=ln['nmrcol'][:], in0=ln['mcol'][:], scalar=-1.0,
        in1=ln['rstdcol'][:], op0=ALU.mult, op1=ALU.mult),
        waits=[S['rstd']], inc=S['nmr'])


def emit_xn_and_transpose(p, xin, xnb, xnf_m, ln, ppT, identb, S):
    ppb = ppT[:].bitcast(BF16)
    for i in range(SUB):
        p.op('scalar', lambda e, i=i: e.activation(
            out=xnb[i % 2][:], in_=xin[i][:], func=AF.Identity,
            scale=ln['rstdcol'][:, i:i + 1], bias=ln['nmrcol'][:, i:i + 1]),
            waits=[S['nmr'], S['tp']], inc=S['xn'])
        for d in range(8):
            p.op('tensor', lambda e, i=i, d=d: e.transpose(
                out=ppb[:, d * 128:(d + 1) * 128],
                in_=xnb[i % 2][:, d * 128:(d + 1) * 128], identity=identb[:]),
                waits=([S['xn'], S['tpe']] if d == 0 else ()),
                inc=(S['tp'] if d == 7 else None))
        for d in range(8):
            p.op('vector', lambda e, i=i, d=d: e.tensor_copy(
                out=xnf_m[d][:, i * 128:(i + 1) * 128],
                in_=ppb[:, d * 128:(d + 1) * 128]),
                waits=([S['tp']] if d == 0 else ()),
                inc=(S['tpe'] if d == 7 else None))


def emit_postln_tail(p, S, yf, row, rc, ppS1, ppS2, ppTr, yb, ysq,
                     ones128b, ident32, xres, scr2, yt,
                     x_src, out_dst, epspp):
    for o in range(8):
        p.op('vector', lambda e, o=o: e.tensor_copy(
            out=yb[o % 2][:], in_=yf[o][:]),
            waits=[S['ye'], S['smm']], inc=S['yb'])
        p.op('scalar', lambda e, o=o: e.activation(
            out=ysq[o % 2][:], in_=yb[o % 2][:], func=AF.Square),
            waits=[S['yb']], inc=S['ysq'])
        p.op('tensor', lambda e, o=o: e.matmul(
            out=ppS1[0:1, :], lhsT=ones128b[:], rhs=yb[o % 2][:],
            start=(o == 0), stop=(o == 7)),
            waits=([S['yb'], S['rowe']] if o == 0 else [S['yb']]))
        p.op('tensor', lambda e, o=o: e.matmul(
            out=ppS2[0:1, :], lhsT=ones128b[:], rhs=ysq[o % 2][:],
            start=(o == 0), stop=(o == 7)),
            waits=[S['ysq']],
            inc=(S['smv'] if o == 7 else S['smm']))
    p.op('vector', lambda e: e.tensor_scalar(
        out=row['m'][:], in0=ppS1[0:1, :], scalar1=1.0 / D, scalar2=None,
        op0=ALU.mult), waits=[S['smv']])
    p.op('vector', lambda e: e.tensor_scalar(
        out=row['e2'][:], in0=ppS2[0:1, :], scalar1=1.0 / D, scalar2=None,
        op0=ALU.mult), inc=S['rowe'])
    p.drain('vector')
    p.op('vector', lambda e: e.tensor_tensor(
        out=row['mm'][:], in0=row['m'][:], in1=row['m'][:], op=ALU.mult))
    p.drain('vector')
    p.op('vector', lambda e: e.tensor_tensor(
        out=row['var'][:], in0=row['e2'][:], in1=row['mm'][:],
        op=ALU.subtract), inc=S['rvar'])
    p.op('scalar', lambda e: e.activation(
        out=row['ln'][:], in_=row['var'][:], func=AF.Ln,
        bias=epspp[0:1, 0:1], scale=1.0), waits=[S['rvar']])
    p.drain('scalar')
    p.op('scalar', lambda e: e.activation(
        out=row['rstd'][:], in_=row['ln'][:], func=AF.Exp, scale=-0.5),
        inc=S['rrstd'])
    p.op('vector', lambda e: e.scalar_tensor_tensor(
        out=row['nmr'][:], in0=row['m'][:], scalar=-1.0, in1=row['rstd'][:],
        op0=ALU.mult, op1=ALU.mult), waits=[S['rrstd']], inc=S['rnmr'])
    for i in range(SUB):
        p.op('tensor', lambda e, i=i: e.transpose(
            out=ppTr[:, i:i + 1], in_=row['rstd'][0:1, i * 128:(i + 1) * 128],
            identity=ident32[0:1, 0:1]),
            waits=([S['rnmr'], S['yte']] if i == 0 else ()))
    for i in range(SUB):
        p.op('tensor', lambda e, i=i: e.transpose(
            out=ppTr[:, 4 + i:5 + i],
            in_=row['nmr'][0:1, i * 128:(i + 1) * 128],
            identity=ident32[0:1, 0:1]),
            inc=(S['rt'] if i == SUB - 1 else None))
    p.op('vector', lambda e: e.tensor_copy(out=rc[:], in_=ppTr[:, 0:8]),
         waits=[S['rt']], inc=S['rte'])
    for o in range(8):
        for i in range(SUB):
            p.op('tensor', lambda e, o=o, i=i: e.transpose(
                out=ppTr[:, i * 128:(i + 1) * 128],
                in_=yf[o][:, i * 128:(i + 1) * 128], identity=ident32[:]),
                waits=([S['rte'], S['yte']] if i == 0 else ()),
                inc=(S['yt'] if i == SUB - 1 else None))
        for i in range(SUB):
            p.op('vector', lambda e, o=o, i=i: e.tensor_copy(
                out=yt[i][:, o * 128:(o + 1) * 128],
                in_=ppTr[:, i * 128:(i + 1) * 128]),
                waits=([S['yt'], S['xo']] if i == 0 else ()),
                inc=(S['yte'] if i == SUB - 1 else None))
    for i in range(SUB):
        p.dma('sync', lambda e, i=i: e.dma_start(
            out=xres[i][:], in_=x_src(i)), waits=[S['ap'], S['xn']],
            inc=S['xr'])
        p.op('vector', lambda e, i=i: e.tensor_scalar(
            out=scr2[0][:], in0=yt[i][:], scalar1=rc[:, i:i + 1],
            scalar2=rc[:, 4 + i:5 + i], op0=ALU.mult, op1=ALU.add),
            waits=[S['yte'], S['rte'], S['xo']])
        p.drain('vector')
        p.op('vector', lambda e, i=i: e.tensor_add(
            out=yt[i][:], in0=scr2[0][:], in1=xres[i][:]),
            waits=[S['xr']], inc=S['ap'])
        p.drain('vector')
        p.dma('sync', lambda e, i=i: e.dma_start(
            out=out_dst(i), in_=yt[i][:]), waits=[S['ap']], inc=S['xo'])


def emit_weight_cast(p, src_ap, dst_ap, wbuf, S, rows, cols):
    n = rows // 128
    for c in range(n):
        p.dma('gpsimd', lambda e, c=c: e.dma_start(
            out=wbuf[c % 2][:, 0:cols], in_=src_ap[c * 128:(c + 1) * 128, :]),
            waits=[S['wst']], inc=S['wld'])
        p.dma('sync', lambda e, c=c: e.dma_start(
            out=dst_ap[c * 128:(c + 1) * 128, :], in_=wbuf[c % 2][:, 0:cols]),
            waits=[S['wld']], inc=S['wst'])


SEMS_ALL = ['wld', 'wst', 'wtp', 'cst', 'xi', 'var', 'rstd', 'nmr', 'xn', 'tp', 'tpe', 'qkv', 'qke', 'sq2', 'bnd', 'aug', 'qp', 'qpr', 'qps', 'kp', 'kmax', 'keb', 'kex', 've', 'kv', 'kve', 'ql', 'oa', 'rc', 'bc', 'rbe', 'at', 'yp', 'ye', 'yb', 'ysq', 'smm', 'smv', 'rowe', 'rvar', 'rrstd', 'rnmr', 'rt', 'rte', 'yt', 'yte', 'ap', 'xr', 'xo', 'sm', 'se', 'md1', 'md2', 'md3', 'gr', 'gb', 'gt', 'gte', 'w2ld', 'xnst', 'kmm', 'kev', 'ksh', 'krp', 'kn', 'knm', 'kdma', 'vev', 'vdma', 'tbl', 'qb', 'kL', 'vL', 'ss', 'px', 'mk', 'pv', 'od', 'gld', 'gtp', 'glg', 'gch', 'sgm', 'sgs', 'ctp', 'cte', 'cbc', 'cbe', 'wldg', 'wstg', 'wldd', 'gmm', 'gev', 'act', 'dmm', 'dev', 'ph_sync', 'ph_scalar', 'ph_vector', 'ph_tensor', 'ph_gpsimd']

ENGS = ('sync', 'scalar', 'vector', 'tensor', 'gpsimd')


def emit_barrier(p, S):
    for eng in ENGS:
        p.op(eng, lambda e: e.nop(), inc=S['ph_' + eng])
    allw = [S['ph_' + g] for g in ENGS]
    for eng in ENGS:
        p.op(eng, lambda e: e.nop(), waits=allw)


def emit_l0attn(nc, st, p, S, PP, io, pref='a_'):
    x = io['x']
    wqkv = io['wqkv']
    bqkv = io['bqkv']
    wproj = io['wproj']
    bproj = io['bproj']
    pm = io['pm']
    x1 = io['x1']

    wqkvbf = nc.dram_tensor(pref + "wqkvbf", [3 * D, D], BF16, kind="Internal").ap()
    wprojbf = nc.dram_tensor(pref + "wprojbf", [D, D], BF16, kind="Internal").ap()
    qps = nc.dram_tensor(pref + "qps", [D, T], BF16, kind="Internal").ap()

    identb_h = nc.inline_tensor(_bf(np.eye(128, dtype=np.float32)), pref + "identb")
    ident32_h = nc.inline_tensor(np.eye(128, dtype=np.float32), pref + "ident32")

    sb = lambda name, shape, dt: nc.alloc_sbuf_tensor(
        pref + name, shape, dt)

    ppT = PP[0]
    ppQ = [PP[1], PP[2]]
    ppA = PP[3]
    ppB = PP[4]
    ppKV = [PP[5], PP[6], PP[7]]
    ppO = [PP[0], PP[1]]
    ppBC = PP[2]
    ppY = [PP[3], PP[4]]
    ppS1, ppS2 = PP[5], PP[6]
    ppTr = PP[7]

    identb = sb("identb_s", [128, 128], BF16)
    ident32 = sb("ident32_s", [128, 128], F32)
    wqkvT = [sb(f"wqkvT{d}", [128, 3 * D], BF16) for d in range(8)]
    wprojT = [wqkvT[v][:, 0:D] for v in range(8)]
    xin = [sb(f"xin{i}", [128, D], F32) for i in range(SUB)]
    xnb = [sb(f"xnb{i}", [128, D], BF16) for i in range(2)]
    xnf = [[sb(f"xnf{m}_{d}", [128, MAC], BF16) for d in range(8)]
           for m in range(2)]
    vsb = [sb(f"vsb{i}", [128, NH * 65], BF16) for i in range(SUB)]
    ktmp = [sb(f"ktmp{i}", [128, D], BF16) for i in range(SUB)]
    kf2 = [sb(f"kf2{i}", [64, MAC], BF16) for i in range(4)]
    qaug = [sb(f"qaug{i}", [65, MAC], BF16) for i in range(4)]
    sqt = [sb(f"sqt{i}", [64, MAC], BF16) for i in range(4)]
    qpr = [sb(f"qpr{o}", [128, MAC], BF16) for o in range(8)]
    pmsT = sb("pmsT", [65, HD], BF16)
    pmTn = sb("pmTn", [NF, HD], BF16)
    pmstage = sb("pmstage", [NF, HD], BF16)
    kvsb = sb("kvsb", [128, NH * 65], BF16)
    bqk = sb("bqk", [128, 16], F32)
    bvb = sb("bvb", [128, D], F32)
    bpj = sb("bpj", [128, 8], F32)
    epspp = sb("epspp", [128, 1], F32)
    cq1 = sb("cq1", [1, 1], F32)
    mkneg = sb("mkneg", [128, 8], F32)
    ebias = sb("ebias", [128, 8], F32)
    ones64 = sb("ones64", [64, 1], BF16)
    ones128b = sb("ones128b", [128, 1], BF16)
    onesrF = sb("onesrF", [1, 64], F32)
    ln = dict(
        scol=sb("scol", [128, SUB], F32), s2col=sb("s2col", [128, SUB], F32),
        mcol=sb("mcol", [128, SUB], F32), e2col=sb("e2col", [128, SUB], F32),
        mmcol=sb("mmcol", [128, SUB], F32),
        varcol=sb("varcol", [128, SUB], F32),
        lncol=sb("lncol", [128, SUB], F32),
        rstdcol=sb("rstdcol", [128, SUB], F32),
        nmrcol=sb("nmrcol", [128, SUB], F32),
        scr=sb("lnscr", [128, D], F32), epspp=epspp)
    rr = [sb(f"rr{i}", [1, MAC], F32) for i in range(2)]
    rb = [sb(f"rb{i}", [64, MAC], F32) for i in range(2)]
    attf = [sb(f"attf{v}", [128, MAC], BF16) for v in range(8)]
    yf = [sb(f"yf{o}", [128, MAC], F32) for o in range(8)]
    yb = [sb(f"ybx{i}", [128, MAC], BF16) for i in range(2)]
    ysq = [sb(f"ysq{i}", [128, MAC], BF16) for i in range(2)]
    yt = [sb(f"yt{i}", [128, D], F32) for i in range(SUB)]
    rc = sb("rc", [128, 8], F32)
    row = dict(m=sb("rowm", [1, MAC], F32), e2=sb("rowe2", [1, MAC], F32),
               mm=sb("rowmm", [1, MAC], F32),
               var=sb("rowvar", [1, MAC], F32),
               rstd=sb("rowrstd", [1, MAC], F32),
               nmr=sb("rownmr", [1, MAC], F32))
    row['ln'] = row['mm']
    xres = xin
    scr2 = [ln["scr"]]
    wbuf = [ktmp[0], ktmp[1]]



    p.dma('sync', lambda e: e.dma_start(out=identb[:], in_=identb_h.ap()[:, :]),
          inc=S['cst'])
    p.dma('sync', lambda e: e.dma_start(out=ident32[:], in_=ident32_h.ap()[:, :]),
          inc=S['cst'])
    p.op('vector', lambda e: e.memset(epspp[:], EPS_LN))
    p.op('vector', lambda e: e.memset(cq1[:], EPS_PF - 0.5 * PF_PMS * PF_PMS))
    p.op('vector', lambda e: e.memset(ones64[:], 1.0))
    p.op('vector', lambda e: e.memset(ones128b[:], 1.0))
    p.op('vector', lambda e: e.memset(onesrF[:], 1.0))
    for i in range(SUB):
        p.op('vector', lambda e, i=i: e.memset(
            vsb[i][:].rearrange("p (h c) -> p h c", c=65)[:, :, 64:65], 1.0))
    p.op('vector', lambda e: e.nop(), inc=S['cst'])

    for c in range(8):
        p.dma('sync', lambda e, c=c: e.dma_start(
            out=bqk[:, c:c + 1], in_=bqkv[c * 128:(c + 1) * 128][:, None]),
            inc=S['cst'])
    for c in range(8):
        p.dma('sync', lambda e, c=c: e.dma_start(
            out=bqk[:, 8 + c:9 + c],
            in_=bqkv[D + c * 128:D + (c + 1) * 128][:, None]),
            inc=S['cst'])
    for c in range(8):
        p.dma('sync', lambda e, c=c: e.dma_start(
            out=bpj[:, c:c + 1], in_=bproj[c * 128:(c + 1) * 128][:, None]),
            inc=S['cst'])
    p.dma('gpsimd', lambda e: e.dma_start(
        out=bvb[:], in_=bqkv[2 * D:3 * D][None, :].broadcast_to([128, D])),
        inc=S['cst'])

    p.dma('gpsimd', lambda e: e.dma_start(out=pmstage[:], in_=pm[:, :]),
          inc=S['wld'])
    p.op('tensor', lambda e: e.transpose(
        out=ppT[:].bitcast(BF16)[0:64, 0:64], in_=pmstage[:],
        identity=identb[0:64, 0:64]),
        waits=[S['wld'], S['cst']], inc=S['wtp'])
    p.op('vector', lambda e: e.tensor_scalar(
        out=pmsT[0:64, :], in0=ppT[:].bitcast(BF16)[0:64, 0:64],
        scalar1=PF_PMS, scalar2=None, op0=ALU.mult), waits=[S['wtp']])
    p.op('vector', lambda e: e.tensor_scalar(
        out=pmTn[:], in0=ppT[:].bitcast(BF16)[0:64, 0:64],
        scalar1=PF_NRM, scalar2=None, op0=ALU.mult), inc=S['qke'])
    p.op('vector', lambda e: e.memset(pmsT[64:65, :], 1.0), inc=S['wtp'])

    emit_weight_cast(p, wqkv, wqkvbf, wbuf, S, 3 * D, D)
    emit_weight_cast(p, wproj, wprojbf, wbuf, S, D, D)
    for d in range(8):
        p.dma('sync', lambda e, d=d: e.dma_start_transpose(
            out=wqkvT[d][:], in_=wqkvbf[0:3 * D, d * 128:(d + 1) * 128]),
            waits=[S['wst']], inc=S['wld'])
    WREADY = S['wld']

    BQA = {}
    PENDA = [None]
    XNA = S['xn'].n
    for m in range(NM):
        mb = m % 2
        for i in range(SUB):
            p.dma('sync', lambda e, m=m, i=i: e.dma_start(
                out=xin[i][:],
                in_=x[(m * SUB + i) * 128:(m * SUB + i + 1) * 128, :]),
                waits=[(S['xn'], XNA + max(0, (m - 1) * 4 + i + 1))],
                inc=S['xi'])
        emit_preln_stats(p, xin, ln, S, S['xi'])
        emit_xn_and_transpose(p, xin, xnb, xnf[mb], ln, ppT, identb, S)

        for oc in range(16):
            qb = ppQ[oc % 2]
            for d in range(8):
                p.op('tensor', lambda e, oc=oc, d=d, qb=qb, mb=mb: e.matmul(
                    out=qb[:], lhsT=wqkvT[d][:, oc * 128:(oc + 1) * 128],
                    rhs=xnf[mb][d][:], start=(d == 0), stop=(d == 7)),
                    waits=(([S['tpe'], WREADY] +
                            ([BQA[oc % 2]] if BQA.get(oc % 2) else []))
                           if d == 0 else ()),
                    inc=(S['qkv'] if d == 7 else None))
            if PENDA[0] is not None:
                PENDA[0]()
                PENDA[0] = None
            if oc < 8:
                for h2 in range(2):
                    b0 = h2 * 64
                    bb = (oc % 2) * 2 + h2
                    p.op('vector', lambda e, oc=oc, h2=h2, qb=qb, b0=b0,
                         bb=bb: e.tensor_scalar(
                             out=qaug[bb][0:64, :],
                             in0=qb[b0:b0 + 64, :],
                             scalar1=bqk[b0:b0 + 64, oc:oc + 1],
                             scalar2=PF_S1, op0=ALU.add, op1=ALU.mult),
                         waits=[S['qkv'], S['qp']], inc=S['qke'])
                    p.drain('vector')
                    p.op('vector', lambda e, bb=bb: e.tensor_tensor(
                        out=sqt[bb][:], in0=qaug[bb][0:64, :],
                        in1=qaug[bb][0:64, :], op=ALU.mult),
                        waits=[S['bnd']], inc=S['sq2'])
                BQA[oc % 2] = (S['qke'], S['qke'].n)

                def qpath(p=p, S=S, oc=oc, m=m):
                    for h2 in range(2):
                        b0 = h2 * 64
                        bb = (oc % 2) * 2 + h2
                        p.op('tensor', lambda e, bb=bb: e.matmul(
                            out=ppA[0:1, :], lhsT=ones64[:],
                            rhs=sqt[bb][:],
                            start=True, stop=True),
                            waits=[S['sq2'], S['aug']], inc=S['bnd'])
                        p.op('scalar', lambda e, bb=bb: e.activation(
                            out=qaug[bb][64:65, :], in_=ppA[0:1, :],
                            func=AF.Identity, scale=-0.5,
                            bias=cq1[0:1, 0:1]),
                            waits=[S['bnd'], S['qp']], inc=S['aug'])
                        p.op('tensor', lambda e, bb=bb: e.matmul(
                            out=ppB[0:64, :], lhsT=pmsT[:], rhs=qaug[bb][:],
                            start=True, stop=True),
                            waits=[S['aug'], S['qpr']], inc=S['qp'])
                        p.op('scalar', lambda e, oc=oc, h2=h2, b0=b0:
                             e.activation(
                                 out=qpr[oc][b0:b0 + 64, :],
                                 in_=ppB[0:64, :],
                                 func=AF.Exp),
                             waits=[S['qp']], inc=S['qpr'])
                    p.dma('scalar', lambda e, m=m, oc=oc: e.dma_start(
                        out=qps[oc * 128:(oc + 1) * 128,
                                m * MAC:(m + 1) * MAC], in_=qpr[oc][:]),
                        waits=[S['qpr']], inc=S['qps'])
                PENDA[0] = qpath
            else:
                ock = oc - 8
                pack = ppA if ock % 2 == 0 else ppB
                for h2 in range(2):
                    b0 = h2 * 64
                    bb = (oc % 2) * 2 + h2
                    p.op('vector', lambda e, oc=oc, h2=h2, qb=qb, b0=b0,
                         bb=bb: e.tensor_scalar(
                             out=kf2[bb][:], in0=qb[b0:b0 + 64, :],
                             scalar1=bqk[b0:b0 + 64, 8 + oc - 8:9 + oc - 8],
                             scalar2=None, op0=ALU.add),
                         waits=[S['qkv'], S['kp']], inc=S['qke'])
                BQA[oc % 2] = (S['qke'], S['qke'].n)

                def kpath(p=p, S=S, oc=oc, ock=ock, pack=pack):
                    for h2 in range(2):
                        for i2 in range(SUB):
                            j = h2 * 4 + i2
                            bb = (oc % 2) * 2 + h2
                            p.op('tensor', lambda e, bb=bb, i2=i2, j=j,
                                 pack=pack: e.matmul(
                                     out=pack[:, j * 64:(j + 1) * 64],
                                     lhsT=kf2[bb][:, i2 * 128:(i2 + 1) * 128],
                                     rhs=pmTn[:], start=True, stop=True),
                                 waits=([S['qke'], S['kex'], S['aug'],
                                         S['qpr']] if j == 0 else ()),
                                 inc=(S['kp'] if j == 7 else None))
                    for j in range(8):
                        p.op('vector', lambda e, j=j, pack=pack:
                             e.tensor_reduce(
                                 out=mkneg[:, j:j + 1],
                                 in_=pack[:, j * 64:(j + 1) * 64],
                                 axis=AX.X, op=ALU.max, negate=True),
                             waits=([S['kp']] if j == 0 else ()))
                    p.drain('vector')
                    p.op('vector', lambda e: e.tensor_scalar(
                        out=ebias[:], in0=mkneg[:], scalar1=EPS_PF,
                        scalar2=None, op0=ALU.add),
                        waits=[S['kex']], inc=S['keb'])
                    for h2 in range(2):
                        for i2 in range(SUB):
                            j = h2 * 4 + i2
                            h = 2 * ock + h2
                            p.op('scalar', lambda e, j=j, h=h, i2=i2,
                                 pack=pack: e.activation(
                                     out=ktmp[i2][:, h * 64:(h + 1) * 64],
                                     in_=pack[:, j * 64:(j + 1) * 64],
                                     func=AF.Exp, bias=ebias[:, j:j + 1]),
                                 waits=[S['keb'], S['kv']],
                                 inc=(S['kex'] if j == 7 else None))
                PENDA[0] = kpath

        if PENDA[0] is not None:
            PENDA[0]()
            PENDA[0] = None
        for i2 in range(SUB):
            for vh in range(2):
                qb = ppQ[vh % 2]
                for d in range(8):
                    p.op('tensor', lambda e, i2=i2, vh=vh, d=d, qb=qb,
                         mb=mb: e.matmul(
                             out=qb[:],
                             lhsT=xnf[mb][d][:, i2 * 128:(i2 + 1) * 128],
                             rhs=wqkvT[d][:, 2 * D + vh * 512:
                                          2 * D + (vh + 1) * 512],
                             start=(d == 0), stop=(d == 7)),
                         waits=(([BQA[vh % 2]] if BQA.get(vh % 2) else [])
                                if d == 0 else ()),
                         inc=(S['qkv'] if d == 7 else None))
                p.op('vector', lambda e, i2=i2, vh=vh, qb=qb: e.tensor_tensor(
                    out=vsb[i2][:].rearrange("p (h c) -> p h c", c=65)
                        [:, vh * 8:(vh + 1) * 8, 0:64],
                    in0=qb[:].rearrange("p (h c) -> p h c", c=64),
                    in1=bvb[:, vh * 512:(vh + 1) * 512].rearrange(
                        "p (h c) -> p h c", c=64),
                    op=ALU.add),
                    waits=[S['qkv'], S['kv']], inc=S['ve'])
                BQA[vh % 2] = (S['ve'], S['ve'].n)

        for h in range(NH):
            bank, off = (0, h) if h < 7 else ((1, h - 7) if h < 14
                                              else (2, h - 14))
            for i2 in range(SUB):
                p.op('tensor', lambda e, h=h, i2=i2, bank=bank, off=off,
                     m=m: e.matmul(
                         out=ppKV[bank][0:64, off * 65:(off + 1) * 65],
                         lhsT=ktmp[i2][:, h * 64:(h + 1) * 64],
                         rhs=vsb[i2][:, h * 65:(h + 1) * 65],
                         start=(m == 0 and i2 == 0 and h in (0, 7, 14)),
                         stop=(m == NM - 1 and i2 == SUB - 1
                               and h in (6, 13, 15))),
                     waits=([S['kex'], S['ve']] if (h == 0 and i2 == 0)
                            else ()),
                     inc=(S['kv'] if (h == NH - 1 and i2 == SUB - 1)
                          else None))

    for v in range(8):
        p.dma('sync', lambda e, v=v: e.dma_start_transpose(
            out=wprojT[v], in_=wprojbf[0:D, v * 128:(v + 1) * 128]),
            waits=[S['wst'], S['qkv']], inc=S['wld'])
    for bank in range(3):
        lo = bank * 7 * 65
        w = 7 * 65 if bank < 2 else 2 * 65
        p.op('vector', lambda e, bank=bank, lo=lo, w=w: e.tensor_copy(
            out=kvsb[0:64, lo:lo + w], in_=ppKV[bank][0:64, 0:w]),
            waits=[S['kv']])
    p.drain('vector')
    p.op('vector', lambda e: e.tensor_copy(
        out=kvsb[64:128, :], in_=kvsb[0:64, :]))
    p.drain('vector')
    p.op('vector', lambda e: e.nop(), inc=S['kve'])

    ATH = []
    PENDB = [None]
    BQB = {}
    for m in range(NM):
        for oc in range(8):
            p.dma('sync', lambda e, m=m, oc=oc: e.dma_start(
                out=qpr[oc][:],
                in_=qps[oc * 128:(oc + 1) * 128, m * MAC:(m + 1) * MAC]),
                waits=[S['kve'], S['oa'], S['qps']], inc=S['ql'])
        for h in range(NH):
            b0 = (h % 2) * 64
            po = ppO[h % 2]
            hu = m * NH + h
            p.op('tensor', lambda e, h=h, b0=b0, po=po: e.matmul(
                out=po[0:65, :],
                lhsT=kvsb[b0:b0 + 64, h * 65:(h + 1) * 65],
                rhs=qpr[h // 2][b0:b0 + 64, :], start=True, stop=True),
                waits=([S['ql'], S['kve']] +
                       ([(S['at'], ATH[hu - 2])] if hu >= 2 else [])),
                inc=S['oa'])
            if PENDB[0] is not None:
                PENDB[0]()
                PENDB[0] = None

            def divpath(p=p, S=S, h=h, b0=b0, po=po):
                p.op('vector', lambda e, h=h, po=po: e.tensor_scalar(
                    out=rr[0][:], in0=po[64:65, :], scalar1=1e-6,
                    scalar2=None, op0=ALU.add), waits=[S['oa'], S['rc']])
                p.drain('vector')
                p.op('vector', lambda e: e.reciprocal(
                    out=rr[1][:], in_=rr[0][:]), waits=[S['bc']],
                    inc=S['rc'])
                p.op('tensor', lambda e: e.matmul(
                    out=ppBC[0:64, :], lhsT=onesrF[:], rhs=rr[1][:],
                    start=True, stop=True),
                    waits=[S['rc'], S['rbe']], inc=S['bc'])
                p.op('scalar', lambda e, h=h: e.activation(
                    out=rb[h % 2][:], in_=ppBC[0:64, :], func=AF.Identity),
                    waits=[S['bc'], S['at']], inc=S['rbe'])
                p.op('vector', lambda e, h=h, b0=b0, po=po: e.tensor_tensor(
                    out=attf[h // 2][b0:b0 + 64, :], in0=po[0:64, :],
                    in1=rb[h % 2][:], op=ALU.mult),
                    waits=[S['rbe']], inc=S['at'])
                ATH.append(S['at'].n)
            PENDB[0] = divpath
        if PENDB[0] is not None:
            PENDB[0]()
            PENDB[0] = None
        for o in range(8):
            py = ppY[o % 2]
            for v in range(8):
                p.op('tensor', lambda e, o=o, v=v, py=py: e.matmul(
                    out=py[:], lhsT=wprojT[v][:, o * 128:(o + 1) * 128],
                    rhs=attf[v][:], start=(v == 0), stop=(v == 7)),
                    waits=(([S['at'], S['wld']] +
                            ([BQB[o % 2]] if BQB.get(o % 2) else []))
                           if v == 0 else ()),
                    inc=(S['yp'] if v == 7 else None))
            p.op('vector', lambda e, o=o, py=py: e.tensor_scalar(
                out=yf[o][:], in0=py[:], scalar1=bpj[:, o:o + 1],
                scalar2=None, op0=ALU.add),
                waits=[S['yp']], inc=S['ye'])
            BQB[o % 2] = (S['ye'], S['ye'].n)
        emit_postln_tail(
            p, S, yf, row, rc, ppS1, ppS2, ppTr, yb, ysq, ones128b,
            ident32, xres, scr2, yt,
            x_src=lambda i, m=m: x[(m * SUB + i) * 128:
                                   (m * SUB + i + 1) * 128, :],
            out_dst=lambda i, m=m: x1[(m * SUB + i) * 128:
                                      (m * SUB + i + 1) * 128, :],
            epspp=epspp)

    p.op('sync', lambda e: e.nop(), waits=[S['xo']])



def emit_l0ffn(nc, st, p, S, PP, io, pref='b_'):
    x = io['x']
    win = io['win']
    binv = io['binv']
    w1 = io['w1']
    b1 = io['b1']
    w2 = io['w2']
    b2 = io['b2']
    wout = io['wout']
    bout = io['bout']
    x2 = io['x2']

    identb_h = nc.inline_tensor(_bf(np.eye(128, dtype=np.float32)), pref + "identb")
    ident32_h = nc.inline_tensor(np.eye(128, dtype=np.float32), pref + "ident32")

    sb = lambda name, shape, dt: nc.alloc_sbuf_tensor(
        pref + name, shape, dt)

    ppT = PP[0]
    ppS = PP[1]
    ppG = PP[2]
    ppY = [PP[3], PP[4]]
    ppS1, ppS2 = PP[5], PP[6]
    ppTr = PP[7]

    identb = sb("identb_s", [128, 128], BF16)
    ident32 = sb("ident32_s", [128, 128], F32)
    xin = [sb(f"xin{i}", [128, D], F32) for i in range(SUB)]
    xnb = [sb(f"xnb{i}", [128, D], BF16) for i in range(2)]
    xnf = [[sb(f"xnf{m}_{d}", [128, MAC], BF16) for d in range(8)]
           for m in range(2)]
    winb = sb("winb", [128, 512], BF16)
    woutb = sb("woutb", [64, D], BF16)
    binb4 = sb("binb4", [128, 256], F32)
    w1b = sb("w1b", [128, 1024], F32)
    b1b = sb("b1b", [128, 1024], F32)
    w2b = sb("w2b", [128, 1024], F32)
    b2b4 = sb("b2b4", [128, 256], F32)
    boc = sb("boc", [128, 8], F32)
    s4 = sb("s4", [128, 256], F32)
    mid1 = [sb(f"mid1{i}", [128, 1024], F32) for i in range(2)]
    mid2 = [sb(f"mid2{i}", [128, 1024], F32) for i in range(2)]
    gf32 = sb("gf32", [128, 256], F32)
    gbf = sb("gbf", [128, 256], BF16)
    gF = sb("gF", [64, MAC], BF16)
    epspp = sb("epspp", [128, 1], F32)
    ones128b = sb("ones128b", [128, 1], BF16)
    ln = dict(
        scol=sb("scol", [128, SUB], F32), s2col=sb("s2col", [128, SUB], F32),
        mcol=sb("mcol", [128, SUB], F32), e2col=sb("e2col", [128, SUB], F32),
        mmcol=sb("mmcol", [128, SUB], F32),
        varcol=sb("varcol", [128, SUB], F32),
        lncol=sb("lncol", [128, SUB], F32),
        rstdcol=sb("rstdcol", [128, SUB], F32),
        nmrcol=sb("nmrcol", [128, SUB], F32),
        scr=sb("lnscr", [128, D], F32), epspp=epspp)
    yf = [sb(f"yf{o}", [128, MAC], F32) for o in range(8)]
    yb = [sb(f"ybx{i}", [128, MAC], BF16) for i in range(2)]
    ysq = [sb(f"ysq{i}", [128, MAC], BF16) for i in range(2)]
    yt = [sb(f"yt{i}", [128, D], F32) for i in range(SUB)]
    rc = sb("rc", [128, 8], F32)
    row = dict(m=sb("rowm", [1, MAC], F32), e2=sb("rowe2", [1, MAC], F32),
               mm=sb("rowmm", [1, MAC], F32),
               var=sb("rowvar", [1, MAC], F32),
               rstd=sb("rowrstd", [1, MAC], F32),
               nmr=sb("rownmr", [1, MAC], F32))
    row['ln'] = row['mm']
    xres = xin
    scr2 = [ln["scr"]]



    p.dma('sync', lambda e: e.dma_start(out=identb[:], in_=identb_h.ap()[:, :]),
          inc=S['cst'])
    p.dma('sync', lambda e: e.dma_start(out=ident32[:], in_=ident32_h.ap()[:, :]),
          inc=S['cst'])
    p.op('vector', lambda e: e.memset(epspp[:], EPS_LN))
    p.op('vector', lambda e: e.memset(ones128b[:], 1.0))
    p.op('vector', lambda e: e.nop(), inc=S['cst'])
    for d in range(8):
        p.dma('gpsimd', lambda e, d=d: e.dma_start(
            out=winb[:, d * 64:(d + 1) * 64],
            in_=win[d * 128:(d + 1) * 128, :]), inc=S['wld'])
    p.dma('gpsimd', lambda e: e.dma_start(out=woutb[:], in_=wout[:, :]),
          inc=S['wld'])
    p.dma('gpsimd', lambda e: e.dma_start(
        out=binb4[:], in_=binv[None, None, :].broadcast_to([128, 4, 64])),
        inc=S['wld'])
    p.dma('gpsimd', lambda e: e.dma_start(
        out=w1b[:], in_=w1.flatten()[None, :].broadcast_to([128, 1024])),
        inc=S['wld'])
    p.dma('gpsimd', lambda e: e.dma_start(
        out=b1b[:], in_=b1.flatten()[None, :].broadcast_to([128, 1024])),
        inc=S['wld'])
    p.dma('gpsimd', lambda e: e.dma_start(
        out=w2b[:], in_=w2.flatten()[None, :].broadcast_to([128, 1024])),
        inc=S['wld'])
    p.dma('gpsimd', lambda e: e.dma_start(
        out=b2b4[:], in_=b2[None, None, :].broadcast_to([128, 4, 64])),
        inc=S['wld'])
    for c in range(8):
        p.dma('sync', lambda e, c=c: e.dma_start(
            out=boc[:, c:c + 1], in_=bout[c * 128:(c + 1) * 128][:, None]),
            inc=S['wld'])

    for m in range(NM):
        mb = m % 2
        for i in range(SUB):
            p.dma('sync', lambda e, m=m, i=i: e.dma_start(
                out=xin[i][:],
                in_=x[(m * SUB + i) * 128:(m * SUB + i + 1) * 128, :]),
                waits=[S['xn'], S['ap']], inc=S['xi'])
        emit_preln_stats(p, xin, ln, S, S['xi'])
        emit_xn_and_transpose(p, xin, xnb, xnf[mb], ln, ppT, identb, S)

        for i2 in range(SUB):
            for d in range(8):
                p.op('tensor', lambda e, i2=i2, d=d, mb=mb: e.matmul(
                    out=ppS[:, i2 * 64:(i2 + 1) * 64],
                    lhsT=xnf[mb][d][:, i2 * 128:(i2 + 1) * 128],
                    rhs=winb[:, d * 64:(d + 1) * 64],
                    start=(i2 == 0 and d == 0), stop=(i2 == 3 and d == 7)),
                    waits=([S['tpe'], S['wld'], S['se']]
                           if (i2 == 0 and d == 0) else ()),
                    inc=(S['sm'] if (i2 == 3 and d == 7) else None))
        p.op('vector', lambda e: e.tensor_add(
            out=s4[:], in0=ppS[:, 0:256], in1=binb4[:]),
            waits=[S['sm'], S['md1']], inc=S['se'])
        for i2 in range(SUB):
            sb_view = s4[:, i2 * 64:(i2 + 1) * 64].rearrange(
                "p (m o) -> p m o", o=1).broadcast_to([128, 64, 16])
            w1v = w1b[:].rearrange("p (m i) -> p m i", i=16)
            p.op('vector', lambda e, i2=i2, sb_view=sb_view, w1v=w1v:
                 e.tensor_tensor(
                     out=mid1[i2 % 2][:].rearrange("p (m i) -> p m i", i=16),
                     in0=sb_view, in1=w1v, op=ALU.mult),
                 waits=[S['se'], S['md2']], inc=S['md1'])
            p.drain('vector')
            p.op('vector', lambda e, i2=i2: e.tensor_add(
                out=mid2[i2 % 2][:], in0=mid1[i2 % 2][:], in1=b1b[:]),
                waits=[S['md3']], inc=S['md1'])
            p.op('scalar', lambda e, i2=i2: e.activation(
                out=mid1[i2 % 2][:], in_=mid2[i2 % 2][:], func=AF.Relu),
                waits=[S['md1']], inc=S['md2'])
            p.op('vector', lambda e, i2=i2: e.tensor_mul(
                out=mid2[i2 % 2][:], in0=mid1[i2 % 2][:], in1=w2b[:]),
                waits=[S['md2']], inc=S['md3'])
            p.drain('vector')
            p.op('vector', lambda e, i2=i2: e.tensor_reduce(
                out=gf32[:, i2 * 64:(i2 + 1) * 64],
                in_=mid2[i2 % 2][:].rearrange("p (m i) -> p m i", i=16),
                axis=AX.X, op=ALU.add), inc=S['gr'])
            p.drain('vector')
        p.op('vector', lambda e: e.tensor_add(
            out=gbf[:], in0=gf32[:], in1=b2b4[:]),
            waits=[S['gr'], S['gt']], inc=S['gb'])
        for i2 in range(SUB):
            p.op('tensor', lambda e, i2=i2: e.transpose(
                out=ppG[:].bitcast(BF16)[0:64, i2 * 128:(i2 + 1) * 128],
                in_=gbf[:, i2 * 64:(i2 + 1) * 64], identity=identb[:]),
                waits=([S['gb'], S['gte']] if i2 == 0 else ()),
                inc=(S['gt'] if i2 == 3 else None))
        p.op('vector', lambda e: e.tensor_copy(
            out=gF[:], in_=ppG[:].bitcast(BF16)[0:64, 0:512]),
            waits=[S['gt']], inc=S['gte'])
        for o in range(8):
            py = ppY[o % 2]
            p.op('tensor', lambda e, o=o, py=py: e.matmul(
                out=py[:], lhsT=woutb[:, o * 128:(o + 1) * 128],
                rhs=gF[:], start=True, stop=True),
                waits=[S['gte'], S['ye']], inc=S['yp'])
            p.op('vector', lambda e, o=o, py=py: e.tensor_scalar(
                out=yf[o][:], in0=py[:], scalar1=boc[:, o:o + 1],
                scalar2=None, op0=ALU.add),
                waits=[S['yp']], inc=S['ye'])
        emit_postln_tail(
            p, S, yf, row, rc, ppS1, ppS2, ppTr, yb, ysq, ones128b,
            ident32, xres, scr2, yt,
            x_src=lambda i, m=m: x[(m * SUB + i) * 128:
                                   (m * SUB + i + 1) * 128, :],
            out_dst=lambda i, m=m: x2[(m * SUB + i) * 128:
                                      (m * SUB + i + 1) * 128, :],
            epspp=epspp)

    p.op('sync', lambda e: e.nop(), waits=[S['xo']])



def emit_l1attn(nc, st, p, S, PP, io, pref='c_'):
    x = io['x']
    wq = io['wq']
    wk = io['wk']
    wv = io['wv']
    wo = io['wo']
    bq = io['bq']
    bk = io['bk']
    bv = io['bv']
    bo = io['bo']
    rcos = io['rcos']
    rsin = io['rsin']
    x3 = io['x3']

    wbf = [nc.dram_tensor(pref + f"w{n}bf", [D, D], BF16, kind="Internal").ap()
        for n in ("q", "k", "v", "o")]
    xn2 = nc.dram_tensor(pref + "xn2", [D, T], BF16, kind="Internal").ap()
    ks_h = nc.dram_tensor(pref + "ksh", [NH, 65, T], BF16, kind="Internal").ap()
    vs_h = nc.dram_tensor(pref + "vsh", [T, NH * 65], BF16, kind="Internal").ap()

    identb_h = nc.inline_tensor(_bf(np.eye(128, dtype=np.float32)), pref + "identb")
    ident32_h = nc.inline_tensor(np.eye(128, dtype=np.float32), pref + "ident32")
    trim_h = nc.inline_tensor(np_trimask(), pref + "trimask")

    sb = lambda name, shape, dt: nc.alloc_sbuf_tensor(
        pref + name, shape, dt)

    ppT = PP[7]
    ppQ = [PP[0], PP[1]]
    ppA = PP[2]
    ppS_ = [PP[3], PP[4]]
    ppO = [PP[5], PP[6]]
    ppS1, ppS2 = PP[3], PP[4]
    ppTr = PP[7]

    identb = sb("identb_s", [128, 128], BF16)
    ident32 = sb("ident32_s", [128, 128], F32)
    trim = sb("trim_s", [128, 128], BF16)
    wT = [[sb(f"wT{g}_{d}", [128, D], BF16) for d in range(8)]
          for g in range(2)]
    xin = [sb(f"xin{i}", [128, D], F32) for i in range(SUB)]
    xnb = [sb(f"xnb{i}", [128, D], BF16) for i in range(2)]
    xnf = [[sb(f"xnf{m}_{d}", [128, MAC], BF16) for d in range(8)]
           for m in range(2)]
    kst = [sb(f"kst{h}", [65, MAC], BF16) for h in range(NH)]
    vst = [sb(f"vst{i}", [128, NH * 65], BF16) for i in range(SUB)]
    kro = sb("kro", [128, MAC], F32)
    ksh2 = sb("ksh2", [128, MAC], F32)
    scrA = sb("scrA", [128, MAC], F32)
    scrB = sb("scrB", [128, MAC], F32)
    scrC = sb("scrC", [128, MAC], F32)
    cosS = sb("cosS", [128, MAC], F32)
    sinS = sb("sinS", [128, MAC], F32)
    kc2 = sb("kc2", [1, 2], F32)
    nkc2 = sb("nkc2", [1, 1], F32)
    bcol = sb("bcol", [128, 32], F32)
    bvb = sb("bvb", [128, D], F32)
    kl = [sb(f"kl{i}", [65, MAC], BF16) for i in range(4)]
    vl = [sb(f"vl{i}", [128, 4, 130], BF16) for i in range(2)]
    pt = [sb(f"pt{i}", [128, MAC], BF16) for i in range(2)]
    ptm = sb("ptm", [128, 128], BF16)
    attfL = [sb(f"attf{v}", [128, MAC], BF16) for v in range(8)]
    rr = [sb(f"rr{i}", [1, MAC], F32) for i in range(2)]
    rb = [sb(f"rb{i}", [64, MAC], F32) for i in range(2)]
    epspp = sb("epspp", [128, 1], F32)
    ones64 = sb("ones64", [64, 1], BF16)
    ones64F = sb("ones64F", [128, 1], F32)
    ones128b = sb("ones128b", [128, 1], BF16)
    onesrF = sb("onesrF", [1, 64], F32)
    ln = dict(
        scol=sb("scol", [128, SUB], F32), s2col=sb("s2col", [128, SUB], F32),
        mcol=sb("mcol", [128, SUB], F32), e2col=sb("e2col", [128, SUB], F32),
        mmcol=sb("mmcol", [128, SUB], F32),
        varcol=sb("varcol", [128, SUB], F32),
        lncol=sb("lncol", [128, SUB], F32),
        rstdcol=sb("rstdcol", [128, SUB], F32),
        nmrcol=sb("nmrcol", [128, SUB], F32),
        scr=sb("lnscr", [128, D], F32), epspp=epspp)
    yf = [sb(f"yf{o}", [128, MAC], F32) for o in range(8)]
    yb = [sb(f"ybx{i}", [128, MAC], BF16) for i in range(2)]
    ysq = [sb(f"ysq{i}", [128, MAC], BF16) for i in range(2)]
    yt = [sb(f"yt{i}", [128, D], F32) for i in range(SUB)]
    rc = sb("rc", [128, 8], F32)
    row = dict(m=sb("rowm", [1, MAC], F32), e2=sb("rowe2", [1, MAC], F32),
               mm=sb("rowmm", [1, MAC], F32),
               var=sb("rowvar", [1, MAC], F32),
               rstd=sb("rowrstd", [1, MAC], F32),
               nmr=sb("rownmr", [1, MAC], F32))
    row['ln'] = row['mm']
    xres = xin
    scr2 = [ln["scr"]]



    p.dma('sync', lambda e: e.dma_start(out=identb[:], in_=identb_h.ap()[:, :]),
          inc=S['cst'])
    p.dma('sync', lambda e: e.dma_start(out=ident32[:], in_=ident32_h.ap()[:, :]),
          inc=S['cst'])
    p.dma('sync', lambda e: e.dma_start(out=trim[:], in_=trim_h.ap()[:, :]),
          inc=S['cst'])
    p.op('vector', lambda e: e.memset(epspp[:], EPS_LN))
    p.op('vector', lambda e: e.memset(ones64[:], 1.0))
    p.op('vector', lambda e: e.memset(ones64F[:], 1.0))
    p.op('vector', lambda e: e.memset(ones128b[:], 1.0))
    p.op('vector', lambda e: e.memset(onesrF[:], 1.0))
    p.op('vector', lambda e: e.memset(kc2[:], 0.0))
    for h in range(NH):
        p.op('vector', lambda e, h=h: e.memset(kst[h][64:65, :], 1.0))
    for i in range(SUB):
        p.op('vector', lambda e, i=i: e.memset(
            vst[i][:].rearrange("p (h c) -> p h c", c=65)[:, :, 64:65], 1.0))
    p.op('vector', lambda e: e.nop(), inc=S['cst'])
    for gi, bsrc in enumerate((bq, bk, bv, bo)):
        for c in range(8):
            p.dma('sync', lambda e, gi=gi, c=c, bsrc=bsrc: e.dma_start(
                out=bcol[:, gi * 8 + c:gi * 8 + c + 1],
                in_=bsrc[c * 128:(c + 1) * 128][:, None]), inc=S['cst'])
    p.dma('gpsimd', lambda e: e.dma_start(
        out=bvb[:], in_=bv[None, :].broadcast_to([128, D])), inc=S['cst'])
    wstage = [yb[0], yb[1]]
    wstage = [xnb[0], xnb[1]]
    for gi, wsrc in enumerate((wq, wk, wv, wo)):
        for c in range(8):
            p.dma('gpsimd', lambda e, c=c, wsrc=wsrc: e.dma_start(
                out=wstage[c % 2][:], in_=wsrc[c * 128:(c + 1) * 128, :]),
                waits=[S['wst']], inc=S['wld'])
            p.dma('sync', lambda e, gi=gi, c=c: e.dma_start(
                out=wbf[gi][c * 128:(c + 1) * 128, :],
                in_=wstage[c % 2][:]),
                waits=[S['wld']], inc=S['wst'])
    for d in range(8):
        p.dma('sync', lambda e, d=d: e.dma_start_transpose(
            out=wT[0][d][:], in_=wbf[1][0:D, d * 128:(d + 1) * 128]),
            waits=[S['wst']], inc=S['w2ld'])
        p.dma('sync', lambda e, d=d: e.dma_start_transpose(
            out=wT[1][d][:], in_=wbf[2][0:D, d * 128:(d + 1) * 128]),
            waits=[S['wst']], inc=S['w2ld'])

    BQ = {}
    PEND = [None]
    KNH = [0]
    XN0 = S['xn'].n
    for m in range(NM):
        mb = m % 2
        for i in range(SUB):
            p.dma('sync', lambda e, m=m, i=i: e.dma_start(
                out=xin[i][:],
                in_=x[(m * SUB + i) * 128:(m * SUB + i + 1) * 128, :]),
                waits=[(S['xn'], XN0 + max(0, (m - 1) * 4 + i + 1)),
                       S['wld']], inc=S['xi'])
        emit_preln_stats(p, xin, ln, S, S['xi'])
        emit_xn_and_transpose(p, xin, xnb, xnf[mb], ln, ppT, identb, S)
        for d in range(8):
            p.dma('scalar', lambda e, m=m, d=d, mb=mb: e.dma_start(
                out=xn2[d * 128:(d + 1) * 128, m * MAC:(m + 1) * MAC],
                in_=xnf[mb][d][:]),
                waits=[S['tpe']], inc=S['xnst'])
        p.dma('sync', lambda e, m=m: e.dma_start(
            out=cosS[:], in_=rcos[:, m * MAC:(m + 1) * MAC]),
            waits=[S['krp']], inc=S['tbl'])
        p.dma('sync', lambda e, m=m: e.dma_start(
            out=sinS[:], in_=rsin[:, m * MAC:(m + 1) * MAC]),
            waits=[S['krp']], inc=S['tbl'])
        for oc in range(8):
            qb_ = ppQ[oc % 2]
            for d in range(8):
                p.op('tensor', lambda e, oc=oc, d=d, qb_=qb_, mb=mb:
                     e.matmul(
                         out=qb_[:],
                         lhsT=wT[0][d][:, oc * 128:(oc + 1) * 128],
                         rhs=xnf[mb][d][:], start=(d == 0), stop=(d == 7)),
                     waits=(([S['tpe'], S['w2ld']] +
                             ([BQ[oc % 2]] if BQ.get(oc % 2) else []))
                            if d == 0 else ()),
                     inc=(S['kmm'] if d == 7 else None))
            if PEND[0] is not None:
                PEND[0]()
                PEND[0] = None
            p.op('vector', lambda e, oc=oc, qb_=qb_: e.tensor_scalar(
                out=kro[:], in0=qb_[:], scalar1=bcol[:, 8 + oc:9 + oc],
                scalar2=None, op0=ALU.add),
                waits=[S['kmm'], S['krp']], inc=S['kev'])
            BQ[oc % 2] = (S['kev'], S['kev'].n)
            p.drain('vector')
            p.op('vector', lambda e: e.tensor_mul(
                out=scrC[:], in0=kro[:], in1=kro[:]),
                waits=[(S['kn'], KNH[0])], inc=S['ksh'])

            def knorm_path(p=p, S=S, scrC=scrC, ppA=ppA, kc2=kc2,
                           ones64F=ones64F, KNH=KNH):
                for h2 in range(2):
                    p.op('tensor', lambda e, h2=h2: e.matmul(
                        out=ppA[0:1, :],
                        lhsT=ones64F[h2 * 64:(h2 + 1) * 64, :],
                        rhs=scrC[h2 * 64:(h2 + 1) * 64, :],
                        start=True, stop=True),
                        waits=[S['ksh'], S['knm']], inc=S['kn'])
                    p.op('vector', lambda e, h2=h2: e.tensor_reduce(
                        out=kc2[0:1, 1:2], in_=ppA[0:1, :], axis=AX.X,
                        op=ALU.max), waits=[S['kn']])
                    p.drain('vector')
                    p.op('vector', lambda e: e.tensor_max(
                        out=kc2[0:1, 0:1], in0=kc2[0:1, 0:1],
                        in1=kc2[0:1, 1:2]), inc=S['knm'])
                    p.drain('vector')
                KNH[0] = S['kn'].n
            PEND[0] = knorm_path
            p.op('vector', lambda e: e.tensor_copy(
                out=ksh2[0:32, :], in_=kro[32:64, :]))
            p.op('vector', lambda e: e.tensor_copy(
                out=ksh2[32:64, :], in_=kro[0:32, :]))
            p.op('vector', lambda e: e.tensor_copy(
                out=ksh2[64:96, :], in_=kro[96:128, :]))
            p.op('vector', lambda e: e.tensor_copy(
                out=ksh2[96:128, :], in_=kro[64:96, :]))
            p.drain('vector')
            p.op('vector', lambda e: e.tensor_mul(
                out=scrA[:], in0=kro[:], in1=cosS[:]), waits=[S['tbl']])
            p.op('vector', lambda e: e.tensor_mul(
                out=scrB[:], in0=ksh2[:], in1=sinS[:]))
            p.drain('vector')
            for h2 in range(2):
                p.op('vector', lambda e, oc=oc, h2=h2: e.tensor_add(
                    out=kst[2 * oc + h2][0:64, :],
                    in0=scrA[h2 * 64:(h2 + 1) * 64, :],
                    in1=scrB[h2 * 64:(h2 + 1) * 64, :]),
                    waits=[S['kdma']],
                    inc=(S['krp'] if h2 == 1 else None))
            p.drain('vector')
            for h2 in range(2):
                p.dma('scalar', lambda e, m=m, oc=oc, h2=h2: e.dma_start(
                    out=ks_h[2 * oc + h2, :, m * MAC:(m + 1) * MAC],
                    in_=kst[2 * oc + h2][:]),
                    waits=[S['krp']], inc=S['kdma'])
        for i2 in range(SUB):
            for vh in range(2):
                qb_ = ppQ[vh % 2]
                for d in range(8):
                    p.op('tensor', lambda e, i2=i2, vh=vh, d=d, qb_=qb_,
                         mb=mb: e.matmul(
                             out=qb_[:],
                             lhsT=xnf[mb][d][:, i2 * 128:(i2 + 1) * 128],
                             rhs=wT[1][d][:, vh * 512:(vh + 1) * 512],
                             start=(d == 0), stop=(d == 7)),
                         waits=(([BQ[vh % 2]] if BQ.get(vh % 2) else [])
                                if d == 0 else ()),
                         inc=(S['kmm'] if d == 7 else None))
                p.op('vector', lambda e, i2=i2, vh=vh, qb_=qb_:
                     e.tensor_tensor(
                         out=vst[i2][:].rearrange("p (h c) -> p h c", c=65)
                             [:, vh * 8:(vh + 1) * 8, 0:64],
                         in0=qb_[:].rearrange("p (h c) -> p h c", c=64),
                         in1=bvb[:, vh * 512:(vh + 1) * 512].rearrange(
                             "p (h c) -> p h c", c=64),
                         op=ALU.add),
                     waits=[S['kmm'], S['vdma']], inc=S['vev'])
                BQ[vh % 2] = (S['vev'], S['vev'].n)
            p.dma('scalar', lambda e, m=m, i2=i2: e.dma_start(
                out=vs_h[(m * SUB + i2) * 128:(m * SUB + i2 + 1) * 128, :],
                in_=vst[i2][:]),
                waits=[S['vev']], inc=S['vdma'])

    if PEND[0] is not None:
        PEND[0]()
        PEND[0] = None
    p.op('vector', lambda e: e.tensor_scalar(
        out=nkc2[:], in0=kc2[0:1, 0:1],
        scalar1=-0.5 * ATT_S * ATT_S, scalar2=None, op0=ALU.mult),
        waits=[S['knm']], inc=S['knm'])

    for d in range(8):
        p.dma('sync', lambda e, d=d: e.dma_start_transpose(
            out=wT[0][d][:], in_=wbf[0][0:D, d * 128:(d + 1) * 128]),
            waits=[S['kmm']], inc=S['w2ld'])
        p.dma('sync', lambda e, d=d: e.dma_start_transpose(
            out=wT[1][d][:], in_=wbf[3][0:D, d * 128:(d + 1) * 128]),
            waits=[S['kmm']], inc=S['w2ld'])

    ptrot = [0]
    FL = dict(kl={}, vl={}, sb={}, pt={}, ptm=0, knm=0)
    for qi in range(NM):
        for d in range(8):
            p.dma('sync', lambda e, qi=qi, d=d: e.dma_start(
                out=xnf[0][d][:],
                in_=xn2[d * 128:(d + 1) * 128, qi * MAC:(qi + 1) * MAC]),
                waits=[S['kmm'], S['xnst']], inc=S['xi'])
        p.dma('sync', lambda e, qi=qi: e.dma_start(
            out=cosS[:], in_=rcos[:, qi * MAC:(qi + 1) * MAC]),
            waits=[S['krp']], inc=S['tbl'])
        p.dma('sync', lambda e, qi=qi: e.dma_start(
            out=sinS[:], in_=rsin[:, qi * MAC:(qi + 1) * MAC]),
            waits=[S['krp']], inc=S['tbl'])
        for oc in range(8):
            qb_ = ppQ[oc % 2]
            for d in range(8):
                p.op('tensor', lambda e, oc=oc, d=d, qb_=qb_: e.matmul(
                    out=qb_[:], lhsT=wT[0][d][:, oc * 128:(oc + 1) * 128],
                    rhs=xnf[0][d][:], start=(d == 0), stop=(d == 7)),
                    waits=(([S['xi'], S['w2ld']] +
                            ([BQ[oc % 2]] if BQ.get(oc % 2) else []))
                           if d == 0 else ()),
                    inc=(S['kmm'] if d == 7 else None))
            if PEND[0] is not None:
                PEND[0]()
                PEND[0] = None
            p.op('vector', lambda e, oc=oc, qb_=qb_: e.tensor_scalar(
                out=kro[:], in0=qb_[:], scalar1=bcol[:, oc:oc + 1],
                scalar2=None, op0=ALU.add),
                waits=[S['kmm'], S['krp']], inc=S['kev'])
            BQ[oc % 2] = (S['kev'], S['kev'].n)
            p.drain('vector')
            p.op('vector', lambda e: e.tensor_mul(
                out=scrC[:], in0=kro[:], in1=kro[:]),
                waits=[(S['kn'], KNH[0])], inc=S['ksh'])

            def qbound_path(p=p, S=S, scrC=scrC, ppA=ppA, kst=kst,
                            nkc2=nkc2, ones64F=ones64F, KNH=KNH, oc=oc):
                for h2 in range(2):
                    p.op('tensor', lambda e, h2=h2: e.matmul(
                        out=ppA[0:1, :],
                        lhsT=ones64F[h2 * 64:(h2 + 1) * 64, :],
                        rhs=scrC[h2 * 64:(h2 + 1) * 64, :],
                        start=True, stop=True),
                        waits=[S['ksh'], S['knm']], inc=S['kn'])
                    p.op('scalar', lambda e, oc=oc, h2=h2: e.activation(
                        out=kst[2 * oc + h2][64:65, :], in_=ppA[0:1, :],
                        func=AF.Identity, scale=-0.5 * ATT_S * ATT_S,
                        bias=nkc2[0:1, 0:1]),
                        waits=[S['kn'], S['ss'], S['kdma']], inc=S['knm'])
                KNH[0] = S['kn'].n
            PEND[0] = qbound_path
            p.op('vector', lambda e: e.tensor_copy(
                out=ksh2[0:32, :], in_=kro[32:64, :]))
            p.op('vector', lambda e: e.tensor_copy(
                out=ksh2[32:64, :], in_=kro[0:32, :]))
            p.op('vector', lambda e: e.tensor_copy(
                out=ksh2[64:96, :], in_=kro[96:128, :]))
            p.op('vector', lambda e: e.tensor_copy(
                out=ksh2[96:128, :], in_=kro[64:96, :]))
            p.drain('vector')
            p.op('vector', lambda e: e.tensor_mul(
                out=scrA[:], in0=kro[:], in1=cosS[:]), waits=[S['tbl']])
            p.op('vector', lambda e: e.tensor_mul(
                out=scrB[:], in0=ksh2[:], in1=sinS[:]))
            p.drain('vector')
            for h2 in range(2):
                p.op('vector', lambda e, oc=oc, h2=h2: e.tensor_add(
                    out=kst[2 * oc + h2][0:64, :],
                    in0=scrA[h2 * 64:(h2 + 1) * 64, :],
                    in1=scrB[h2 * 64:(h2 + 1) * 64, :]),
                    waits=[S['ss'], S['kdma']],
                    inc=(S['krp'] if h2 == 1 else None))
            p.drain('vector')
        if PEND[0] is not None:
            PEND[0]()
            PEND[0] = None
        p.op('vector', lambda e: e.nop(), inc=S['qb'])
        FL['knm'] = S['knm'].n

        for pr in range(8):
            first_pv = [True, True]
            for ki in range(qi + 1):
                kb = ki % 2
                diag = (ki == qi)
                for hh in range(2):
                    h = 2 * pr + hh
                    klw = [(S['ss'], FL['kl'].get(hh * 2 + kb, 0)),
                           S['qb']]
                    p.dma('sync', lambda e, h=h, ki=ki, kb=kb, hh=hh:
                          e.dma_start(
                              out=kl[hh * 2 + kb][:],
                              in_=ks_h[h, :, ki * MAC:(ki + 1) * MAC]),
                          waits=klw, inc=S['kL'])
                p.dma('sync', lambda e, pr=pr, ki=ki, kb=kb: e.dma_start(
                    out=vl[kb][:],
                    in_=vs_h[ki * MAC:(ki + 1) * MAC,
                             2 * pr * 65:2 * pr * 65 + 130].rearrange(
                                 "(s p) c -> p s c", p=128)),
                    waits=[(S['pv'], FL['vl'].get(kb, 0))], inc=S['vL'])
                for hh in range(2):
                    h = 2 * pr + hh
                    nks = 4
                    for ks in range(nks):
                        if diag:
                            nq = MAC - ks * 128
                            qoff = ks * 128
                        else:
                            nq = MAC
                            qoff = 0
                        rot = ptrot[0] % 2
                        ptrot[0] += 1
                        sbank = ppS_[ks % 2]
                        p.op('tensor', lambda e, h=h, ks=ks, kb=kb,
                             hh=hh, nq=nq, qoff=qoff, sbank=sbank:
                             e.matmul(
                                 out=sbank[0:128, 0:nq],
                                 lhsT=kl[hh * 2 + kb][:, ks * 128:
                                                      (ks + 1) * 128],
                                 rhs=kst[h][:, qoff:MAC],
                                 start=True, stop=True),
                             waits=[S['kL'], S['qb'],
                                    (S['px'], FL['sb'].get(ks % 2, 0)),
                                    (S['knm'], FL['knm'])],
                             inc=S['ss'])
                        FL['kl'][hh * 2 + kb] = S['ss'].n
                        p.op('scalar', lambda e, rot=rot, nq=nq,
                             sbank=sbank: e.activation(
                                 out=pt[rot][0:128, 0:nq],
                                 in_=sbank[0:128, 0:nq], func=AF.Exp),
                             waits=[S['ss'],
                                    (S['pv'], FL['pt'].get(rot, 0))],
                             inc=S['px'])
                        FL['sb'][ks % 2] = S['px'].n
                        if diag:
                            p.op('vector', lambda e, rot=rot:
                                 e.tensor_mul(
                                     out=ptm[:], in0=pt[rot][:, 0:128],
                                     in1=trim[:]),
                                 waits=[S['px'],
                                        (S['pv'], FL['ptm'])],
                                 inc=S['mk'])
                        groups = []
                        if diag:
                            groups.append((ks * 128, 128, True))
                            if ks < 3:
                                groups.append(((ks + 1) * 128,
                                               (3 - ks) * 128, False))
                        else:
                            groups.append((0, 512, False))
                        for (qc0, qw, dblk) in groups:
                            rhs_ap = (ptm[:] if dblk else
                                      pt[rot][:, (qc0 - qoff):
                                              (qc0 - qoff) + qw])
                            w = [S['mk'] if dblk else S['px'], S['vL']]
                            if first_pv[hh]:
                                w.append(S['at'])
                            p.op('tensor', lambda e, hh=hh, ks=ks, kb=kb,
                                 qc0=qc0, qw=qw, rhs_ap=rhs_ap, h=h,
                                 fp=first_pv[hh]: e.matmul(
                                     out=ppO[hh][0:65, qc0:qc0 + qw],
                                     lhsT=vl[kb][:, ks,
                                                 hh * 65:(hh + 1) * 65],
                                     rhs=rhs_ap,
                                     start=fp,
                                     stop=(ki == qi and ks == 3 and
                                           dblk)),
                                 waits=w, inc=S['pv'])
                            first_pv[hh] = False
                            if dblk:
                                FL['ptm'] = S['pv'].n
                        FL['pt'][rot] = S['pv'].n
                FL['vl'][kb] = S['pv'].n
            p.op('tensor', lambda e: e.nop(), waits=[S['pv']],
                 inc=S['od'])
            for hh in range(2):
                p.op('vector', lambda e, hh=hh: e.reciprocal(
                    out=rr[hh][:], in_=ppO[hh][64:65, :]),
                    waits=[S['od'], S['bc']], inc=S['rbe'])
                p.drain('vector')
            for hh in range(2):
                p.op('tensor', lambda e, hh=hh: e.matmul(
                    out=(ppA[0:64, :] if hh == 0 else ppTr[0:64, :]),
                    lhsT=onesrF[:], rhs=rr[hh][:],
                    start=True, stop=True),
                    waits=[S['rbe'], S['knm'], S['at'], S['rte'],
                           S['yte']],
                    inc=S['bc'])
            for hh in range(2):
                p.op('scalar', lambda e, hh=hh: e.activation(
                    out=rb[hh][:],
                    in_=(ppA[0:64, :] if hh == 0 else ppTr[0:64, :]),
                    func=AF.Identity),
                    waits=[S['bc']], inc=S['rbe'])
            for hh in range(2):
                p.op('vector', lambda e, pr=pr, hh=hh: e.tensor_tensor(
                    out=attfL[pr][hh * 64:(hh + 1) * 64, :],
                    in0=ppO[hh][0:64, :], in1=rb[hh][:], op=ALU.mult),
                    waits=[S['rbe'], S['yp']], inc=S['at'])
        for o in range(8):
            py = ppQ[o % 2]
            for v in range(8):
                p.op('tensor', lambda e, o=o, v=v, py=py: e.matmul(
                    out=py[:], lhsT=wT[1][v][:, o * 128:(o + 1) * 128],
                    rhs=attfL[v][:], start=(v == 0), stop=(v == 7)),
                    waits=(([S['at']] +
                            ([BQ[o % 2]] if BQ.get(o % 2) else []))
                           if v == 0 else ()),
                    inc=(S['yp'] if v == 7 else None))
            p.op('vector', lambda e, o=o, py=py: e.tensor_scalar(
                out=yf[o][:], in0=py[:], scalar1=bcol[:, 24 + o:25 + o],
                scalar2=None, op0=ALU.add),
                waits=[S['yp']], inc=S['ye'])
            BQ[o % 2] = (S['ye'], S['ye'].n)
        emit_postln_tail(
            p, S, yf, row, rc, ppS1, ppS2, ppTr, yb, ysq, ones128b,
            ident32, xres, scr2, yt,
            x_src=lambda i, qi=qi: x[(qi * SUB + i) * 128:
                                     (qi * SUB + i + 1) * 128, :],
            out_dst=lambda i, qi=qi: x3[(qi * SUB + i) * 128:
                                        (qi * SUB + i + 1) * 128, :],
            epspp=epspp)

    p.op('sync', lambda e: e.nop(), waits=[S['xo']])



def emit_moe(nc, st, p, S, PP, io, pref='d_'):
    x = io['x']
    gw = io['gw']
    wg = io['wg']
    wu = io['wu']
    wd = io['wd']
    swg = io['swg']
    swu = io['swu']
    swd = io['swd']
    sgw = io['sgw']
    x4 = io['x4']

    wgbf = nc.dram_tensor(pref + "wgbf", [NEXP, D, 1408], BF16, kind="Internal").ap()
    wubf = nc.dram_tensor(pref + "wubf", [NEXP, D, 1408], BF16, kind="Internal").ap()
    wdbf = nc.dram_tensor(pref + "wdbf", [NEXP, 1408, D], BF16, kind="Internal").ap()

    identb_h = nc.inline_tensor(_bf(np.eye(128, dtype=np.float32)), pref + "identb")
    ident32_h = nc.inline_tensor(np.eye(128, dtype=np.float32), pref + "ident32")

    sb = lambda name, shape, dt: nc.alloc_sbuf_tensor(
        pref + name, shape, dt)

    ppT = PP[7]
    ppTr = PP[7]
    ppGU = [[PP[0], PP[1]], [PP[2], PP[3]]]
    ppD = [PP[4], PP[5]]
    ppX = PP[6]
    ppS1, ppS2 = PP[0], PP[1]

    identb = sb("identb_s", [128, 128], BF16)
    ident32 = sb("ident32_s", [128, 128], F32)
    xin = [sb(f"xin{i}", [128, D], F32) for i in range(SUB)]
    xnb = [sb(f"xnb{i}", [128, D], BF16) for i in range(2)]
    xnfQ = [sb(f"xnfq{d}", [128, QTOK], BF16) for d in range(8)]
    macc = [sb(f"macc{o}", [128, QTOK], F32) for o in range(8)]
    actF = [sb(f"actf{i}", [128, QTOK], BF16) for i in range(IC)]
    dwb = [sb(f"dwb{i}", [128, D], BF16) for i in range(IC)]
    wgb = [sb(f"wgb{i}", [128, D], BF16) for i in range(2)]
    wub = [sb(f"wub{i}", [128, D], BF16) for i in range(2)]
    cmbB = sb("cmbB", [128, QTOK], F32)
    cmball = sb("cmball", [128, 64], F32)
    cmbrow = sb("cmbrow", [1, QTOK], F32)
    sgrow = sb("sgrow", [1, QTOK], F32)
    gwT = sb("gwT", [128, 64], BF16)
    sgwT = sb("sgwT", [128, 8], BF16)
    gstage = sb("gstage", [8, D], BF16)
    sgt = [sb(f"sgt{i}", [128, 512], F32) for i in range(2)]
    su = [sb(f"su{i}", [128, 512], F32) for i in range(2)]
    lg0 = sb("lg0", [128, 8], F32)
    el = sb("el", [128, 8], F32)
    mskA = sb("mskA", [128, 8], F32)
    lg1 = sb("lg1", [128, 8], F32)
    nm1 = sb("nm1", [128, 1], F32)
    m2s = sb("m2s", [128, 1], F32)
    dsum = sb("dsum", [128, 1], F32)
    recs = sb("recs", [128, 1], F32)
    epspp = sb("epspp", [128, 1], F32)
    ones128b = sb("ones128b", [128, 1], BF16)
    onesr128F = sb("onesr128F", [1, 128], F32)
    ln = dict(
        scol=sb("scol", [128, SUB], F32), s2col=sb("s2col", [128, SUB], F32),
        mcol=sb("mcol", [128, SUB], F32), e2col=sb("e2col", [128, SUB], F32),
        mmcol=sb("mmcol", [128, SUB], F32),
        varcol=sb("varcol", [128, SUB], F32),
        lncol=sb("lncol", [128, SUB], F32),
        rstdcol=sb("rstdcol", [128, SUB], F32),
        nmrcol=sb("nmrcol", [128, SUB], F32),
        scr=sb("lnscr", [128, D], F32), epspp=epspp)
    yb = [sb(f"ybx{i}", [128, MAC], BF16) for i in range(2)]
    ysq = [sb(f"ysq{i}", [128, MAC], BF16) for i in range(2)]
    yt = [sb(f"yt{i}", [128, D], F32) for i in range(SUB)]
    rc = sb("rc", [128, 8], F32)
    row = dict(m=sb("rowm", [1, MAC], F32), e2=sb("rowe2", [1, MAC], F32),
               mm=sb("rowmm", [1, MAC], F32),
               var=sb("rowvar", [1, MAC], F32),
               rstd=sb("rowrstd", [1, MAC], F32),
               nmr=sb("rownmr", [1, MAC], F32))
    row['ln'] = row['mm']
    xres = xin
    scr2 = [ln["scr"]]



    p.dma('sync', lambda e: e.dma_start(out=identb[:], in_=identb_h.ap()[:, :]),
          inc=S['cst'])
    p.dma('sync', lambda e: e.dma_start(out=ident32[:], in_=ident32_h.ap()[:, :]),
          inc=S['cst'])
    p.op('vector', lambda e: e.memset(epspp[:], EPS_LN))
    p.op('vector', lambda e: e.memset(ones128b[:], 1.0))
    p.op('vector', lambda e: e.memset(onesr128F[:], 1.0))
    p.op('vector', lambda e: e.nop(), inc=S['cst'])
    p.dma('gpsimd', lambda e: e.dma_start(out=gstage[0:8, :], in_=gw[:, :]),
          inc=S['gld'])
    for d in range(8):
        p.op('tensor', lambda e, d=d: e.transpose(
            out=ppX[:].bitcast(BF16)[0:128, d * 8:(d + 1) * 8],
            in_=gstage[0:8, d * 128:(d + 1) * 128],
            identity=identb[0:8, 0:8]),
            waits=([S['gld'], S['cst']] if d == 0 else ()),
            inc=(S['gtp'] if d == 7 else None))
    p.op('vector', lambda e: e.tensor_copy(
        out=gwT[:], in_=ppX[:].bitcast(BF16)[0:128, 0:64]),
        waits=[S['gtp']], inc=S['gte'])
    p.dma('gpsimd', lambda e: e.dma_start(out=gstage[0:1, :], in_=sgw[:, :]),
          waits=[S['gte']], inc=S['gld'])
    for d in range(8):
        p.op('tensor', lambda e, d=d: e.transpose(
            out=ppX[:].bitcast(BF16)[0:128, 2 * d:2 * d + 1],
            in_=gstage[0:1, d * 128:(d + 1) * 128],
            identity=identb[0:1, 0:1]),
            waits=([S['gld'], S['gte']] if d == 0 else ()),
            inc=(S['gtp'] if d == 7 else None))
    p.op('vector', lambda e: e.tensor_copy(
        out=sgwT[:],
        in_=ppX[:].bitcast(BF16)[0:128, 0:16].rearrange(
            "p (a b) -> p a b", b=2)[:, :, 0:1].rearrange(
                "p a b -> p (a b)")),
        waits=[S['gtp']], inc=S['gte'])

    HX = dict(uw=0, u2=0, v2=0, gmmh=[], gmmgh=[], wstgh=[], gevh=[],
              acth=[], devh=[])

    for q in range(NQT):
        for ml in range(2):
            m = q * 2 + ml
            for i in range(SUB):
                p.dma('sync', lambda e, m=m, i=i: e.dma_start(
                    out=xin[i][:],
                    in_=x[(m * SUB + i) * 128:(m * SUB + i + 1) * 128, :]),
                    waits=[S['xn'], S['ap'], S['gmm']], inc=S['xi'])
            emit_preln_stats(p, xin, ln, S, S['xi'])
            ppb = ppT[:].bitcast(BF16)
            for i in range(SUB):
                p.op('scalar', lambda e, i=i: e.activation(
                    out=xnb[i % 2][:], in_=xin[i][:], func=AF.Identity,
                    scale=ln['rstdcol'][:, i:i + 1],
                    bias=ln['nmrcol'][:, i:i + 1]),
                    waits=[S['nmr'], S['tp']], inc=S['xn'])
                for d in range(8):
                    p.op('tensor', lambda e, i=i, d=d, ppb=ppb: e.transpose(
                        out=ppb[:, d * 128:(d + 1) * 128],
                        in_=xnb[i % 2][:, d * 128:(d + 1) * 128],
                        identity=identb[:]),
                        waits=([S['xn'], S['tpe']] if d == 0 else ()),
                        inc=(S['tp'] if d == 7 else None))
                for d in range(8):
                    p.op('vector', lambda e, i=i, d=d, ml=ml, ppb=ppb:
                         e.tensor_copy(
                             out=xnfQ[d][:, ml * 512 + i * 128:
                                         ml * 512 + (i + 1) * 128],
                             in_=ppb[:, d * 128:(d + 1) * 128]),
                         waits=([S['tp'], S['gmm']] if d == 0 else ()),
                         inc=(S['tpe'] if d == 7 else None))
        for s in range(8):
            for d in range(8):
                p.op('tensor', lambda e, s=s, d=d: e.matmul(
                    out=ppX[0:128, s * 8:(s + 1) * 8],
                    lhsT=xnfQ[d][:, s * 128:(s + 1) * 128],
                    rhs=gwT[:, d * 8:(d + 1) * 8],
                    start=(d == 0), stop=(d == 7)),
                    waits=([S['tpe'], S['gte'], S['cte'], S['gch']]
                           if d == 0 else ()),
                    inc=(S['glg'] if d == 7 else None))
            p.op('vector', lambda e, s=s: e.tensor_reduce(
                out=nm1[:], in_=ppX[0:128, s * 8:(s + 1) * 8],
                axis=AX.X, op=ALU.max, negate=True), waits=[S['glg']])
            p.drain('vector')
            p.op('vector', lambda e, s=s: e.tensor_scalar(
                out=lg0[:], in0=ppX[0:128, s * 8:(s + 1) * 8],
                scalar1=nm1[:, 0:1], scalar2=None, op0=ALU.add),
                inc=S['gch'])
            p.drain('vector')
            p.op('scalar', lambda e: e.activation(
                out=el[:], in_=lg0[:], func=AF.Exp),
                waits=[S['gch']], inc=S['sgm'])
            p.op('vector', lambda e: e.tensor_scalar(
                out=mskA[:], in0=lg0[:], scalar1=-1e-30, scalar2=None,
                op0=ALU.is_ge))
            p.drain('vector')
            p.op('vector', lambda e: e.scalar_tensor_tensor(
                out=lg1[:], in0=mskA[:], scalar=-1e30, in1=lg0[:],
                op0=ALU.mult, op1=ALU.add))
            p.drain('vector')
            p.op('vector', lambda e: e.tensor_reduce(
                out=m2s[:], in_=lg1[:], axis=AX.X, op=ALU.max))
            p.drain('vector')
            p.op('vector', lambda e: e.tensor_scalar(
                out=lg1[:], in0=lg0[:], scalar1=m2s[:, 0:1], scalar2=None,
                op0=ALU.is_ge))
            p.drain('vector')
            p.op('vector', lambda e: e.tensor_tensor(
                out=el[:], in0=el[:], in1=lg1[:], op=ALU.mult),
                waits=[S['sgm']])
            p.drain('vector')
            p.op('vector', lambda e: e.tensor_reduce(
                out=dsum[:], in_=el[:], axis=AX.X, op=ALU.add))
            p.drain('vector')
            p.op('vector', lambda e: e.reciprocal(
                out=recs[:], in_=dsum[:]))
            p.drain('vector')
            p.op('vector', lambda e, s=s: e.tensor_scalar(
                out=cmball[:, s * 8:(s + 1) * 8], in0=el[:],
                scalar1=recs[:, 0:1], scalar2=None, op0=ALU.mult),
                inc=S['gch'])
            p.drain('vector')
        for tq in range(2):
            for d in range(8):
                p.op('tensor', lambda e, tq=tq, d=d: e.matmul(
                    out=ppX[0:1, 0:512],
                    lhsT=sgwT[:, d:d + 1],
                    rhs=xnfQ[d][:, tq * 512:(tq + 1) * 512],
                    start=(d == 0), stop=(d == 7)),
                    waits=([S['gch'], S['sgs']] if d == 0 else ()),
                    inc=(S['glg'] if d == 7 else None))
            p.op('scalar', lambda e, tq=tq: e.activation(
                out=sgrow[0:1, tq * 512:(tq + 1) * 512],
                in_=ppX[0:1, 0:512], func=AF.Sigmoid),
                waits=[S['glg']], inc=S['sgs'])

        for ex in range(NEXP):
            if ex < 8:
                for s in range(8):
                    p.op('tensor', lambda e, ex=ex, s=s: e.transpose(
                        out=ppX[:].bitcast(F32)[0:1,
                                                (s % 4) * 128:
                                                (s % 4) * 128 + 128],
                        in_=cmball[:, s * 8 + ex:s * 8 + ex + 1],
                        identity=ident32[:]),
                        waits=([S['gch'], S['sgs'], S['cte']]
                               if s % 4 == 0 else ()),
                        inc=(S['ctp'] if s % 4 == 3 else None))
                    if s % 4 == 3:
                        half = s // 4
                        p.op('vector', lambda e, half=half: e.tensor_copy(
                            out=cmbrow[0:1, half * 512:(half + 1) * 512],
                            in_=ppX[:].bitcast(F32)[0:1, 0:512]),
                            waits=[S['ctp']], inc=S['cte'])
            else:
                p.op('vector', lambda e: e.tensor_copy(
                    out=cmbrow[:], in_=sgrow[:]),
                    waits=[S['sgs'], S['cbc']], inc=S['cte'])
            for half in range(2):
                p.op('tensor', lambda e, half=half: e.matmul(
                    out=ppX[0:128, 0:512], lhsT=onesr128F[:],
                    rhs=cmbrow[0:1, half * 512:(half + 1) * 512],
                    start=True, stop=True),
                    waits=[S['cte'], S['cbe'], S['act']], inc=S['cbc'])
                p.op('vector', lambda e, half=half: e.tensor_copy(
                    out=cmbB[:, half * 512:(half + 1) * 512],
                    in_=ppX[0:128, 0:512]),
                    waits=[S['cbc']], inc=S['cbe'])
            for ic in range(IC):
                uw = HX['uw']
                wwait = ([(S['gmm'], HX['gmmh'][uw - 2]),
                          (S['wstg'], HX['wstgh'][uw - 2])] if uw >= 2 else [])
                if q == 0:
                    p.dma('gpsimd', lambda e, ex=ex, ic=ic, uw=uw: e.dma_start(
                        out=wgb[uw % 2][:].rearrange(
                            "p (hc i) -> p hc i", i=128),
                        in_=(wg[ex, :, ic * 128:(ic + 1) * 128]
                             if ex < 8 else
                             swg[:, ic * 128:(ic + 1) * 128]).rearrange(
                                 "(hc p) i -> p hc i", p=128)),
                        waits=wwait, inc=S['wldg'])
                    p.dma('sync', lambda e, ex=ex, ic=ic, uw=uw: e.dma_start(
                        out=wgbf[ex, :, ic * 128:(ic + 1) * 128].rearrange(
                            "(hc p) i -> p hc i", p=128),
                        in_=wgb[uw % 2][:].rearrange(
                            "p (hc i) -> p hc i", i=128)),
                        waits=[S['wldg']], inc=S['wstg'])
                    p.dma('gpsimd', lambda e, ex=ex, ic=ic, uw=uw: e.dma_start(
                        out=wub[uw % 2][:].rearrange(
                            "p (hc i) -> p hc i", i=128),
                        in_=(wu[ex, :, ic * 128:(ic + 1) * 128]
                             if ex < 8 else
                             swu[:, ic * 128:(ic + 1) * 128]).rearrange(
                                 "(hc p) i -> p hc i", p=128)),
                        waits=wwait, inc=S['wldg'])
                    p.dma('sync', lambda e, ex=ex, ic=ic, uw=uw: e.dma_start(
                        out=wubf[ex, :, ic * 128:(ic + 1) * 128].rearrange(
                            "(hc p) i -> p hc i", p=128),
                        in_=wub[uw % 2][:].rearrange(
                            "p (hc i) -> p hc i", i=128)),
                        waits=[S['wldg']], inc=S['wstg'])
                    p.dma('gpsimd', lambda e, ex=ex, ic=ic: e.dma_start(
                        out=dwb[ic][:],
                        in_=(wd[ex, ic * 128:(ic + 1) * 128, :]
                             if ex < 8 else
                             swd[ic * 128:(ic + 1) * 128, :])),
                        waits=[S['dmm']], inc=S['wldd'])
                    p.dma('sync', lambda e, ex=ex, ic=ic: e.dma_start(
                        out=wdbf[ex, ic * 128:(ic + 1) * 128, :],
                        in_=dwb[ic][:]),
                        waits=[S['wldd']], inc=S['wstg'])
                else:
                    p.dma('sync', lambda e, ex=ex, ic=ic, uw=uw: e.dma_start(
                        out=wgb[uw % 2][:].rearrange(
                            "p (hc i) -> p hc i", i=128),
                        in_=wgbf[ex, :, ic * 128:(ic + 1) * 128].rearrange(
                            "(hc p) i -> p hc i", p=128)),
                        waits=wwait, inc=S['wldg'])
                    p.op('sync', lambda e: e.nop(), inc=S['wstg'])
                    p.dma('sync', lambda e, ex=ex, ic=ic, uw=uw: e.dma_start(
                        out=wub[uw % 2][:].rearrange(
                            "p (hc i) -> p hc i", i=128),
                        in_=wubf[ex, :, ic * 128:(ic + 1) * 128].rearrange(
                            "(hc p) i -> p hc i", p=128)),
                        waits=wwait, inc=S['wldg'])
                    p.op('sync', lambda e: e.nop(), inc=S['wstg'])
                    p.dma('sync', lambda e, ex=ex, ic=ic: e.dma_start(
                        out=dwb[ic][:],
                        in_=wdbf[ex, ic * 128:(ic + 1) * 128, :]),
                        waits=[S['dmm']], inc=S['wldd'])
                    p.op('sync', lambda e: e.nop(), inc=S['wstg'])
                HX['wstgh'].append(S['wstg'].n)
                for tq in range(2):
                    u2 = HX['u2']
                    pg = ppGU[0][u2 % 2]
                    pu = ppGU[1][u2 % 2]
                    guwait = ([(S['gev'], HX['gevh'][u2 - 2])]
                              if u2 >= 2 else [])
                    for hc in range(8):
                        p.op('tensor', lambda e, ic=ic, tq=tq, hc=hc, uw=uw,
                             pg=pg: e.matmul(
                                 out=pg[:],
                                 lhsT=wgb[uw % 2][:, hc * 128:(hc + 1) * 128],
                                 rhs=xnfQ[hc][:, tq * 512:(tq + 1) * 512],
                                 start=(hc == 0), stop=(hc == 7)),
                             waits=(([S['wldg'], S['cbe']] + guwait)
                                    if hc == 0 else ()),
                             inc=(S['gmm'] if hc == 7 else None))
                    HX['gmmgh'].append(S['gmm'].n)
                    for hc in range(8):
                        p.op('tensor', lambda e, ic=ic, tq=tq, hc=hc, uw=uw,
                             pu=pu: e.matmul(
                                 out=pu[:],
                                 lhsT=wub[uw % 2][:, hc * 128:(hc + 1) * 128],
                                 rhs=xnfQ[hc][:, tq * 512:(tq + 1) * 512],
                                 start=(hc == 0), stop=(hc == 7)),
                             waits=(),
                             inc=(S['gmm'] if hc == 7 else None))
                    p.op('scalar', lambda e, pg=pg, u2=u2: e.activation(
                        out=sgt[u2 % 2][:], in_=pg[:], func=AF.Silu),
                        waits=([(S['gmm'], HX['gmmgh'][-1])] + guwait),
                        inc=S['gev'])
                    p.op('vector', lambda e, pu=pu, u2=u2: e.tensor_tensor(
                        out=su[u2 % 2][:], in0=pu[:], in1=sgt[u2 % 2][:],
                        op=ALU.mult),
                        waits=([S['gev'], S['gmm']] +
                               ([(S['act'], HX['acth'][u2 - 2])]
                                if u2 >= 2 else [])),
                        inc=S['gev'])
                    HX['gevh'].append(S['gev'].n)
                    p.drain('vector')
                    p.op('vector', lambda e, ic=ic, tq=tq, u2=u2:
                         e.tensor_tensor(
                             out=actF[ic][:, tq * 512:(tq + 1) * 512],
                             in0=su[u2 % 2][:],
                             in1=cmbB[:, tq * 512:(tq + 1) * 512],
                             op=ALU.mult),
                         waits=[S['dmm']], inc=S['act'])
                    HX['acth'].append(S['act'].n)
                    p.drain('vector')
                    HX['u2'] += 1
                HX['gmmh'].append(S['gmm'].n)
                HX['uw'] += 1
            for tq in range(2):
                for ho in range(8):
                    v2 = HX['v2']
                    pd = ppD[v2 % 2]
                    dwait = ([(S['dev'], HX['devh'][v2 - 2])]
                             if v2 >= 2 else [])
                    for ic in range(IC):
                        p.op('tensor', lambda e, tq=tq, ho=ho, ic=ic,
                             pd=pd: e.matmul(
                                 out=pd[:],
                                 lhsT=dwb[ic][:, ho * 128:(ho + 1) * 128],
                                 rhs=actF[ic][:, tq * 512:(tq + 1) * 512],
                                 start=(ic == 0), stop=(ic == IC - 1)),
                             waits=(([S['act'], S['wldd']] + dwait)
                                    if ic == 0 else ()),
                             inc=(S['dmm'] if ic == IC - 1 else None))
                    if ex == 0:
                        p.op('vector', lambda e, tq=tq, ho=ho, pd=pd:
                             e.tensor_copy(
                                 out=macc[ho][:, tq * 512:(tq + 1) * 512],
                                 in_=pd[:]),
                             waits=[S['dmm'], S['yt'], S['yb']],
                             inc=S['dev'])
                    else:
                        p.op('vector', lambda e, tq=tq, ho=ho, pd=pd:
                             e.tensor_tensor(
                                 out=macc[ho][:, tq * 512:(tq + 1) * 512],
                                 in0=pd[:],
                                 in1=macc[ho][:, tq * 512:(tq + 1) * 512],
                                 op=ALU.add),
                             waits=[S['dmm']], inc=S['dev'])
                    HX['devh'].append(S['dev'].n)
                    p.drain('vector')
                    HX['v2'] += 1
        p.op('vector', lambda e: e.nop(), waits=[S['dev']], inc=S['ye'])

        for ml in range(2):
            m = q * 2 + ml
            yfv = [macc[o][:, ml * 512:(ml + 1) * 512] for o in range(8)]
            emit_postln_tail(
                p, S, yfv, row, rc, ppS1, ppS2, ppTr, yb, ysq, ones128b,
                ident32, xres, scr2, yt,
                x_src=lambda i, m=m: x[(m * SUB + i) * 128:
                                       (m * SUB + i + 1) * 128, :],
                out_dst=lambda i, m=m: x4[(m * SUB + i) * 128:
                                          (m * SUB + i + 1) * 128, :],
                epspp=epspp)

    p.op('sync', lambda e: e.nop(), waits=[S['xo']])




def build_fused():
    nc = bass.Bass("TRN2", target_bir_lowering=False)
    g = lambda nm, shape: nc.dram_tensor(nm, shape, F32,
                                         kind="ExternalInput").ap()
    x = g("x", [T, D])
    io1 = dict(x=x, wqkv=g("wqkv", [3 * D, D]), bqkv=g("bqkv", [3 * D]),
               wproj=g("wproj", [D, D]), bproj=g("bproj", [D]),
               pm=g("pm", [NF, HD]))
    io2 = dict(win=g("win", [D, 64]), binv=g("binv", [64]),
               w1=g("w1", [64, 16]), b1=g("b1", [64, 16]),
               w2=g("w2", [64, 16]), b2=g("b2", [64]),
               wout=g("wout", [64, D]), bout=g("bout", [D]))
    io3 = dict(wq=g("wq", [D, D]), wk=g("wk", [D, D]), wv=g("wv", [D, D]),
               wo=g("wo", [D, D]), bq=g("bq", [D]), bk=g("bk", [D]),
               bv=g("bv", [D]), bo=g("bo", [D]),
               rcos=g("rcos", [128, T]), rsin=g("rsin", [128, T]))
    io4 = dict(gw=g("gw", [8, D]), wg=g("wg", [8, D, 1408]),
               wu=g("wu", [8, D, 1408]), wd=g("wd", [8, 1408, D]),
               swg=g("swg", [D, 1408]), swu=g("swu", [D, 1408]),
               swd=g("swd", [1408, D]), sgw=g("sgw", [1, D]))
    x4out = nc.dram_tensor("out", [T, D], F32, kind="ExternalOutput").ap()
    X1 = nc.dram_tensor("X1", [T, D], F32, kind="Internal").ap()
    X2 = nc.dram_tensor("X2", [T, D], F32, kind="Internal").ap()
    X3 = nc.dram_tensor("X3", [T, D], F32, kind="Internal").ap()
    io1.update(x1=X1)
    io2.update(x=X1, x2=X2)
    io3.update(x=X2, x3=X3)
    io4.update(x=X3, x4=x4out)

    st = ExitStack()
    with st:
        PP = [st.enter_context(nc.psum_tensor(f"pp{i}", [128, 512], F32))
              for i in range(8)]
        S = mksems(st, nc, SEMS_ALL)
        p = Prog()
        mark = nc.sbuf_base
        emit_l0attn(nc, st, p, S, PP, io1, pref="a_")
        emit_barrier(p, S)
        nc.sbuf_base = mark
        emit_l0ffn(nc, st, p, S, PP, io2, pref="b_")
        emit_barrier(p, S)
        nc.sbuf_base = mark
        emit_l1attn(nc, st, p, S, PP, io3, pref="c_")
        emit_barrier(p, S)
        nc.sbuf_base = mark
        emit_moe(nc, st, p, S, PP, io4, pref="d_")
        with nc.Block() as block:
            p.replay(block)
    return nc


_FUSED_NC = None


def _get_fused():
    global _FUSED_NC
    if _FUSED_NC is None:
        _FUSED_NC = build_fused()
    return _FUSED_NC


def kernel(**inputs):
    W = {k: np.ascontiguousarray(np.asarray(v, dtype=np.float32))
         for k, v in inputs.items()}
    x0 = W.pop("L0_attn_x")
    rcos, rsin = np_rope_tables()
    nc = _get_fused()
    in_maps = []
    for c in range(NCORE):
        in_maps.append({
            "x": np.ascontiguousarray(x0[c]),
            "wqkv": W["L0_attn_W_qkv"], "bqkv": W["L0_attn_b_qkv"],
            "wproj": W["L0_attn_W_proj"], "bproj": W["L0_attn_b_proj"],
            "pm": W["L0_attn_proj_mat"],
            "win": W["L0_ffn_Win"], "binv": W["L0_ffn_bin_"],
            "w1": W["L0_ffn_W1"], "b1": W["L0_ffn_b1"],
            "w2": W["L0_ffn_W2"], "b2": W["L0_ffn_b2"],
            "wout": W["L0_ffn_Wout"], "bout": W["L0_ffn_bout"],
            "wq": W["L1_attn_Wq"], "bq": W["L1_attn_bq"],
            "wk": W["L1_attn_Wk"], "bk": W["L1_attn_bk"],
            "wv": W["L1_attn_Wv"], "bv": W["L1_attn_bv"],
            "wo": W["L1_attn_Wo"], "bo": W["L1_attn_bo"],
            "rcos": rcos, "rsin": rsin,
            "gw": W["L1_ffn_gate_w"], "wg": W["L1_ffn_w_gate"],
            "wu": W["L1_ffn_w_up"], "wd": W["L1_ffn_w_down"],
            "swg": W["L1_ffn_sw_gate"], "swu": W["L1_ffn_sw_up"],
            "swd": W["L1_ffn_sw_down"], "sgw": W["L1_ffn_shared_gate_w"],
        })
    res = run_bass_kernel_spmd(nc, in_maps, core_ids=list(range(NCORE)))
    return np.stack([res.results[c]["out"] for c in range(NCORE)],
                    axis=0).astype(np.float32)
